# version 12
# baseline (speedup 1.0000x reference)
import math
import numpy as np
import ml_dtypes
from contextlib import ExitStack

import concourse.bass as bass
import concourse.mybir as mybir
from concourse.bass_utils import run_bass_kernel_spmd
from concourse.expressions import RuntimeValue

F32 = mybir.dt.float32
BF16 = mybir.dt.bfloat16
AF = mybir.ActivationFunctionType
ALU = mybir.AluOpType
AXL = mybir.AxisListType
ds = bass.ds

EPS = 1e-5


def had(n):
    a = np.arange(n)
    bits = a[:, None] & a[None, :]
    pc = np.zeros((n, n), np.int64)
    x = bits
    while x.max() > 0:
        pc += x & 1
        x >>= 1
    return np.where(pc % 2 == 0, 1.0, -1.0).astype(np.float32)


def cgroups(total, w):
    off = 0
    while off < total:
        yield off, min(w, total - off)
        off += w


class Sem:
    def __init__(self, es, nc, name):
        self.h = es.enter_context(nc.semaphore(name))
        self.n = 0

    def inc(self, instr, k=1):
        instr.then_inc(self.h, k)
        self.n += k
        return self.n


class BufferedBlock:

    def __init__(self, real):
        self._real = real
        self._q = []

    def _add(self, eng, fn):
        if self._q and self._q[-1][0] == eng:
            self._q[-1][1].append(fn)
        else:
            self._q.append((eng, [fn]))
        return fn

    def tensor(self, fn):
        return self._add("tensor", fn)

    def scalar(self, fn):
        return self._add("scalar", fn)

    def vector(self, fn):
        return self._add("vector", fn)

    def gpsimd(self, fn):
        return self._add("gpsimd", fn)

    def sync(self, fn):
        return self._add("sync", fn)

    def flush(self):
        merged = {}
        order = []
        for eng, fns in self._q:
            if eng not in merged:
                merged[eng] = []
                order.append(eng)
            merged[eng].extend(fns)
        self._q = []
        for eng in order:
            fns = merged[eng]

            def body(e, fns=fns):
                for f in fns:
                    f(e)
            getattr(self._real, eng)(body)


def build_nc(S, DM, HEADS, INTER, n_cores=8):
    HD = 128
    assert DM == HEADS * HD and DM % 512 == 0 and INTER % 512 == 0 and S % 1024 == 0
    NH = HEADS // 2
    MC = DM // 128
    KC = INTER // 128
    FSH = INTER // n_cores
    JL = FSH // 128
    NSH = DM // n_cores
    NST = NSH // 128
    assert NST >= 1 and NSH % 128 == 0 and 512 % NSH == 0 or NSH % 512 == 0
    QT = 512
    NQT = S // QT
    CPT = QT // 128
    SH = S // 2
    TB = 512
    NB = SH // TB
    TSUB = TB // 128
    NGO = DM // 512
    VC = NH * HD
    TG = 512
    NTG = S // TG
    NSG = S // 512
    NT16 = SH // 128
    Q_SCALE = 1.0 / math.sqrt(float(HD))
    ALPHA_UG = 1.0 / math.sqrt(float(DM))
    ALPHA_DP = 1.0 / math.sqrt(float(INTER))
    MOFF = (CPT - 1) * 128
    MW = MOFF + QT

    nc = bass.Bass("TRN2", target_bir_lowering=False, debug=False)

    x_in = nc.dram_tensor("x_in", [S, DM], F32, kind="ExternalInput").ap()
    xh_in = nc.dram_tensor("xh_in", [SH, DM], F32, kind="ExternalInput").ap()
    wqkv_in = nc.dram_tensor("wqkv_in", [DM, 3 * VC], F32, kind="ExternalInput").ap()
    wout_in = nc.dram_tensor("wout_in", [DM, DM], F32, kind="ExternalInput").ap()
    wup_in = nc.dram_tensor("wup_in", [FSH, DM], F32, kind="ExternalInput").ap()
    wgate_in = nc.dram_tensor("wgate_in", [FSH, DM], F32, kind="ExternalInput").ap()
    wdown_in = nc.dram_tensor("wdown_in", [NSH, INTER], F32, kind="ExternalInput").ap()
    suug_in = nc.dram_tensor("suug_in", [MC, 128], F32, kind="ExternalInput").ap()
    sudp_in = nc.dram_tensor("sudp_in", [KC, 128], F32, kind="ExternalInput").ap()
    wsu_in = nc.dram_tensor("wsu_in", [JL, 128], F32, kind="ExternalInput").ap()
    wsg_in = nc.dram_tensor("wsg_in", [JL, 128], F32, kind="ExternalInput").ap()
    wsdp_in = nc.dram_tensor("wsdp_in", [NST, 128], F32, kind="ExternalInput").ap()
    out_ext = nc.dram_tensor("out_ext", [SH, DM], F32, kind="ExternalOutput").ap()

    h128_d = nc.inline_tensor(had(128).astype(ml_dtypes.bfloat16), name="h128c")
    identb_d = nc.inline_tensor(np.eye(128, dtype=ml_dtypes.bfloat16), name="identbc")
    identf_d = nc.inline_tensor(np.eye(128, dtype=np.float32), name="identfc")
    onesc_d = nc.inline_tensor(np.ones((128, 1), dtype=ml_dtypes.bfloat16), name="onescc")
    onesr_d = nc.inline_tensor(np.ones((1, 128), dtype=np.float32), name="onesrc")
    ones128_d = nc.inline_tensor(np.ones((128, 128), dtype=ml_dtypes.bfloat16),
                                 name="ones128c")
    mk = np.arange(128)[:, None] <= (np.arange(MW)[None, :] - MOFF)
    mask_d = nc.inline_tensor(mk.astype(ml_dtypes.bfloat16), name="maskc")

    xbf = nc.dram_tensor("xbf", [S, DM], BF16).ap()
    wqkvbf = nc.dram_tensor("wqkvbf", [DM, 3 * VC], BF16).ap()
    qscr = nc.dram_tensor("qscr", [NH * 128, S], BF16).ap()
    kscr = nc.dram_tensor("kscr", [NH * 128, S], BF16).ap()
    vscr = nc.dram_tensor("vscr", [NH * 128 * (S // 128), 128], BF16).ap()
    uscr = nc.dram_tensor("uscr", [SH, DM], BF16).ap()
    ag1u_i = nc.dram_tensor("ag1u_i", [2 * JL * MC * 128, 128], BF16).ap()
    ag1u_o = nc.dram_tensor("ag1u_o", [n_cores * 2 * JL * MC * 128, 128], BF16,
                            addr_space="Shared").ap()
    ag1d_i = nc.dram_tensor("ag1d_i", [KC * 128, NSH], BF16).ap()
    ag1d_o = nc.dram_tensor("ag1d_o", [n_cores * KC * 128, NSH], BF16,
                            addr_space="Shared").ap()
    ag2_i = nc.dram_tensor("ag2_i", [NH * 128, S], BF16).ap()
    ag2_o = nc.dram_tensor("ag2_o", [n_cores * NH * 128, S], BF16,
                            addr_space="Shared").ap()

    ES = ExitStack()
    with ES:
        h128 = ES.enter_context(nc.sbuf_tensor("h128", [128, 128], BF16))
        identb = ES.enter_context(nc.sbuf_tensor("identb", [128, 128], BF16))
        identf = ES.enter_context(nc.sbuf_tensor("identf", [128, 128], F32))
        onesc = ES.enter_context(nc.sbuf_tensor("onesc", [128, 1], BF16))
        onesr = ES.enter_context(nc.sbuf_tensor("onesr", [1, 128], F32))
        ones128 = ES.enter_context(nc.sbuf_tensor("ones128", [128, 128], BF16))
        maskt = ES.enter_context(nc.sbuf_tensor("maskt", [128, MW], BF16))
        su_ug = ES.enter_context(nc.sbuf_tensor("su_ug", [128, MC], F32))
        su_dp = ES.enter_context(nc.sbuf_tensor("su_dp", [128, KC], F32))
        ws_up = ES.enter_context(nc.sbuf_tensor("ws_up", [128, JL], F32))
        ws_gt = ES.enter_context(nc.sbuf_tensor("ws_gt", [128, JL], F32))
        ws_dp = ES.enter_context(nc.sbuf_tensor("ws_dp", [128, NST], F32))
        ubias = ES.enter_context(nc.sbuf_tensor("ubias", [128, NH], F32))

        s_pe = Sem(ES, nc, "s_pe")
        s_act = Sem(ES, nc, "s_act")
        s_dve = Sem(ES, nc, "s_dve")
        s_pool = Sem(ES, nc, "s_pool")
        s_sp = Sem(ES, nc, "s_sp")
        s_cc = Sem(ES, nc, "s_cc")
        d_gl = Sem(ES, nc, "d_gl")
        d_wc = Sem(ES, nc, "d_wc")
        d_wl = Sem(ES, nc, "d_wl")
        d_xb = Sem(ES, nc, "d_xb")
        d_xb2 = Sem(ES, nc, "d_xb2")
        d_c = Sem(ES, nc, "d_c")
        d_e = [Sem(ES, nc, f"d_e{i}") for i in range(4)]
        d_x = [Sem(ES, nc, f"d_x{i}") for i in range(2)]
        d_k = [Sem(ES, nc, f"d_k{i}") for i in range(2)]
        d_o = [Sem(ES, nc, f"d_o{i}") for i in range(2)]
        d_ag = Sem(ES, nc, "d_ag")
        d_ob = Sem(ES, nc, "d_ob")
        d_xh = [Sem(ES, nc, f"d_xh{i}") for i in range(2)]
        d_u = [Sem(ES, nc, f"d_u{i}") for i in range(2)]
        d_uf = Sem(ES, nc, "d_uf")
        d_un = [Sem(ES, nc, f"d_un{i}") for i in range(2)]
        d_f = [Sem(ES, nc, f"d_f{i}") for i in range(2)]
        d_d = [Sem(ES, nc, f"d_d{i}") for i in range(2)]
        d_out = [Sem(ES, nc, f"d_out{i}") for i in range(2)]

        block = BufferedBlock(ES.enter_context(nc.Block()))
        sb_pe = Sem(ES, nc, "sb_pe")
        sb_act = Sem(ES, nc, "sb_act")
        sb_dve = Sem(ES, nc, "sb_dve")
        sb_pool = Sem(ES, nc, "sb_pool")
        sb_sp = Sem(ES, nc, "sb_sp")

        def phase_barrier():
            block.flush()
            for s in (sb_pe, sb_act, sb_dve, sb_pool, sb_sp):
                s.n += 1

            @block.tensor
            def _(t):
                t.drain().then_inc(sb_pe.h, 1)

            @block.scalar
            def _(s):
                s.drain().then_inc(sb_act.h, 1)

            @block.vector
            def _(v):
                v.drain().then_inc(sb_dve.h, 1)

            @block.gpsimd
            def _(g):
                g.drain().then_inc(sb_pool.h, 1)

            @block.sync
            def _(sy):
                sy.drain().then_inc(sb_sp.h, 1)
            vals = [(s, s.n) for s in (sb_pe, sb_act, sb_dve, sb_pool, sb_sp)]
            for eng_name, own in (("tensor", sb_pe), ("scalar", sb_act), ("vector", sb_dve),
                                  ("gpsimd", sb_pool), ("sync", sb_sp)):
                def mk(own=own):
                    def f(eng):
                        for s, v in vals:
                            if s is not own:
                                eng.wait_ge(s.h, v)
                    return f
                getattr(block, eng_name)(mk())
            block.flush()

        @block.gpsimd
        def _(g):
            g.dma_start(out=xbf[0:S // 2, :], in_=x_in[0:S // 2, :]).then_inc(d_xb.h, 16)
            g.dma_start(out=wqkvbf, in_=wqkv_in).then_inc(d_wc.h, 16)
            g.dma_start(out=xbf[S // 2:S, :], in_=x_in[S // 2:S, :]
                        ).then_inc(d_xb2.h, 16)
        d_xb.n += 16
        d_wc.n += 16
        d_xb2.n += 16
        XBF_LD1 = d_xb.n

        @block.sync
        def _(sy):
            for dst, src in ((h128, h128_d), (identb, identb_d), (identf, identf_d),
                             (onesc, onesc_d), (onesr, onesr_d),
                             (ones128, ones128_d), (maskt, mask_d)):
                sy.dma_start(out=dst[:], in_=src.ap()).then_inc(d_c.h, 16)
        d_c.n += 7 * 16

        with ExitStack() as es0:
            tvin = es0.enter_context(nc.sbuf_tensor("tvin", [128, 128], F32))
            ptv = es0.enter_context(nc.psum_tensor("ptv", [128, 128], F32))
            vecs = [(suug_in, MC, su_ug, ALPHA_UG), (sudp_in, KC, su_dp, ALPHA_DP),
                    (wsu_in, JL, ws_up, 1.0), (wsg_in, JL, ws_gt, 1.0),
                    (wsdp_in, NST, ws_dp, 1.0)]
            for vi, (src, rows, dst, scale) in enumerate(vecs):
                @block.sync
                def _(sy, src=src, rows=rows, vi=vi, dvn=s_dve.n):
                    if vi > 0:
                        sy.wait_ge(s_dve.h, dvn)
                    sy.dma_start(out=tvin[0:rows, :], in_=src).then_inc(d_c.h, 16)
                d_c.n += 16

                @block.tensor
                def _(t, rows=rows, vi=vi, dcn=d_c.n, dvn=s_dve.n):
                    t.wait_ge(d_c.h, dcn)
                    if vi > 0:
                        t.wait_ge(s_dve.h, dvn)
                    t.transpose(ptv[:, 0:rows], tvin[0:rows, :],
                                identf[0:rows, 0:rows]).then_inc(s_pe.h, 1)
                s_pe.n += 1

                @block.vector
                def _(v, rows=rows, dst=dst, scale=scale, pen=s_pe.n):
                    v.wait_ge(s_pe.h, pen)
                    v.tensor_scalar(out=dst[:, 0:rows], in0=ptv[:, 0:rows],
                                    scalar1=float(scale), scalar2=None,
                                    op0=ALU.mult).then_inc(s_dve.h, 1)
                s_dve.n += 1
            block.flush()

        phase_barrier()

        with ExitStack() as es1:
            BW = max(MC * FSH, KC * NSH)
            b1 = es1.enter_context(nc.sbuf_tensor("b1", [128, BW], F32))
            b2 = es1.enter_context(nc.sbuf_tensor("b2", [128, BW], F32))
            wtb = es1.enter_context(nc.sbuf_tensor("wtb", [128, BW], BF16))
            wrow = es1.enter_context(nc.sbuf_tensor("wrow", [128, max(DM, INTER)], BF16))
            fout = es1.enter_context(nc.sbuf_tensor("fout", [128, max(FSH, NSH)], BF16))
            ptr4 = [es1.enter_context(nc.psum_tensor(f"ptr4{i}", [128, 512], BF16))
                    for i in range(2)]
            ph1 = [es1.enter_context(nc.psum_tensor(f"ph1{i}", [128, 1024], F32))
                   for i in range(2)]

            def transform(w_src, n_rows_tiles, n_chunks, chunk_w, row_w, sign_tile,
                          scale_rows_ap, ag_write):
                tr_evac = {}
                tgi = [0]
                for r in range(n_rows_tiles):
                    @block.gpsimd
                    def _(g, r=r, pen=s_pe.n):
                        g.wait_ge(s_pe.h, pen)
                        g.dma_start(out=wrow[:, 0:row_w],
                                    in_=w_src[r * 128:(r + 1) * 128, :]).then_inc(d_gl.h, 16)
                    d_gl.n += 16

                    if scale_rows_ap is not None:
                        @block.vector
                        def _(v, r=r, dgn=d_gl.n):
                            v.wait_ge(d_gl.h, dgn)
                            v.tensor_scalar(out=wrow[:, 0:row_w],
                                            in0=wrow[:, 0:row_w],
                                            scalar1=scale_rows_ap(r),
                                            scalar2=None, op0=ALU.mult
                                            ).then_inc(s_dve.h, 1)
                        s_dve.n += 1

                    nch = row_w // 128
                    for cc4 in range(0, nch, 4):
                        ng4 = min(4, nch - cc4)
                        gslot = tgi[0] % 2

                        @block.tensor
                        def _(t, r=r, cc4=cc4, ng4=ng4, gslot=gslot, dgn=d_gl.n,
                              dvn=s_dve.n, dcn=d_c.n,
                              pfree=tr_evac.get(tgi[0] - 2, 0)):
                            t.wait_ge(d_c.h, dcn)
                            if scale_rows_ap is not None:
                                t.wait_ge(s_dve.h, dvn)
                            else:
                                t.wait_ge(d_gl.h, dgn)
                            t.wait_ge(s_act.h, pfree)
                            for i in range(ng4):
                                cc = cc4 + i
                                last = t.transpose(ptr4[gslot][:, i * 128:(i + 1) * 128],
                                                   wrow[:, cc * 128:(cc + 1) * 128],
                                                   identb[:])
                            last.then_inc(s_pe.h, 1)
                        s_pe.n += 1

                        @block.scalar
                        def _(s, r=r, cc4=cc4, ng4=ng4, gslot=gslot, pen=s_pe.n):
                            s.wait_ge(s_pe.h, pen)
                            for i in range(ng4):
                                cc = cc4 + i
                                last = s.activation(
                                    wtb[:, cc * chunk_w + r * 128:
                                        cc * chunk_w + (r + 1) * 128],
                                    ptr4[gslot][:, i * 128:(i + 1) * 128], AF.Copy)
                            last.then_inc(s_act.h, 1)
                        s_act.n += 1
                        tr_evac[tgi[0]] = s_act.n
                        tgi[0] += 1
                h_evac = {}
                hgi = [0]
                for cc in range(n_chunks):
                    for off, w in cgroups(chunk_w, 1024):
                        gslot = hgi[0] % 2

                        @block.tensor
                        def _(t, cc=cc, off=off, w=w, gslot=gslot, acn=s_act.n,
                              dvn=h_evac.get(hgi[0] - 2, 0)):
                            t.wait_ge(s_act.h, acn)
                            t.wait_ge(s_dve.h, dvn)
                            for o2, w2 in cgroups(w, 512):
                                last = t.matmul(
                                    ph1[gslot][:, o2:o2 + w2], h128[:],
                                    wtb[:, cc * chunk_w + off + o2:
                                        cc * chunk_w + off + o2 + w2],
                                    start=True, stop=True)
                            last.then_inc(s_pe.h, 1)
                        s_pe.n += 1

                        @block.vector
                        def _(v, cc=cc, off=off, w=w, gslot=gslot, pen=s_pe.n):
                            v.wait_ge(s_pe.h, pen)
                            v.tensor_copy(
                                out=b1[:, cc * chunk_w + off: cc * chunk_w + off + w],
                                in_=ph1[gslot][:, 0:w]).then_inc(s_dve.h, 1)
                        s_dve.n += 1
                        h_evac[hgi[0]] = s_dve.n
                        hgi[0] += 1
                nst = int(round(math.log2(n_chunks)))
                hw = chunk_w // 2

                def bfly(eng, sem, c0, c1):
                    cur, nxt = b1, b2
                    for st in range(nst):
                        stride = 1 << st
                        cv = cur[:, 0:n_chunks * chunk_w].rearrange(
                            "p (g two s w) -> p g two s w",
                            two=2, s=stride, w=chunk_w)
                        nv = nxt[:, 0:n_chunks * chunk_w].rearrange(
                            "p (g two s w) -> p g two s w",
                            two=2, s=stride, w=chunk_w)
                        eng.tensor_tensor(out=nv[:, :, 0, :, c0:c1],
                                          in0=cv[:, :, 0, :, c0:c1],
                                          in1=cv[:, :, 1, :, c0:c1],
                                          op=ALU.add)
                        eng.tensor_tensor(out=nv[:, :, 1, :, c0:c1],
                                          in0=cv[:, :, 0, :, c0:c1],
                                          in1=cv[:, :, 1, :, c0:c1],
                                          op=ALU.subtract)
                        eng.drain()
                        cur, nxt = nxt, cur
                    eng.drain().then_inc(sem, 1)

                @block.vector
                def _(v, dvn=s_dve.n, pon=s_pool.n):
                    v.wait_ge(s_dve.h, dvn)
                    v.wait_ge(s_pool.h, pon)
                    bfly(v, s_dve.h, 0, hw)
                s_dve.n += 1

                @block.gpsimd
                def _(g, dvn=s_dve.n - 1, pon=s_pool.n):
                    g.wait_ge(s_dve.h, dvn)
                    g.wait_ge(s_pool.h, pon)
                    bfly(g, s_pool.h, hw, chunk_w)
                s_pool.n += 1
                cur = b1 if nst % 2 == 0 else b2
                nxt = b2 if nst % 2 == 0 else b1
                for cc in range(n_chunks):
                    @block.vector
                    def _(v, cc=cc, cur=cur, dvn=s_dve.n, pon=s_pool.n, dan=d_ag.n):
                        v.wait_ge(s_dve.h, dvn)
                        v.wait_ge(s_pool.h, pon)
                        v.wait_ge(d_ag.h, dan)
                        v.tensor_scalar(
                            out=fout[:, 0:chunk_w], in0=cur[:, cc * chunk_w:(cc + 1) * chunk_w],
                            scalar1=sign_tile[:, cc:cc + 1], scalar2=None,
                            op0=ALU.mult).then_inc(s_dve.h, 1)
                    s_dve.n += 1

                    @block.gpsimd
                    def _(g, cc=cc, dvn=s_dve.n):
                        g.wait_ge(s_dve.h, dvn)
                        ag_write(g, cc)
                    d_ag.n += 16

            for w_i, (w_src, wsrows, base) in enumerate(
                    ((wup_in, ws_up, 0), (wgate_in, ws_gt, JL * MC * 128))):
                def agw_ug(g, mc, base=base):
                    dst = ag1u_i[base:base + JL * MC * 128, :]
                    dst = dst.rearrange("(jl m p) c -> p jl m c", jl=JL, m=MC)
                    g.dma_start(out=dst[:, :, mc, :],
                                in_=fout[:, 0:FSH].rearrange("p (jl c) -> p jl c", jl=JL)
                                ).then_inc(d_ag.h, 16)
                transform(w_src, JL, MC, FSH, DM, su_ug,
                          (lambda r, wsrows=wsrows: wsrows[:, r:r + 1]), agw_ug)
            def agw_dn(g, cc):
                g.dma_start(out=ag1d_i[cc * 128:(cc + 1) * 128, :],
                            in_=fout[:, 0:NSH]).then_inc(d_ag.h, 16)
            transform(wdown_in, NST, KC, NSH, INTER, su_dp,
                      (lambda r: ws_dp[:, r:r + 1]), agw_dn)

            block.flush()

        @block.gpsimd
        def _(g, dan=d_ag.n):
            g.wait_ge(d_ag.h, dan)
            g.collective_compute("AllGather", ALU.bypass,
                                 replica_groups=[list(range(n_cores))],
                                 ins=[ag1u_i.opt()],
                                 outs=[ag1u_o.opt()]).then_inc(s_cc.h, 1)
            g.collective_compute("AllGather", ALU.bypass,
                                 replica_groups=[list(range(n_cores))],
                                 ins=[ag1d_i.opt()],
                                 outs=[ag1d_o.opt()]).then_inc(s_cc.h, 1)
        s_cc.n += 2
        CC1 = s_cc.n
        XBF_LD2 = d_xb2.n

        phase_barrier()

        with ExitStack() as es2:
            wqkv_sb = es2.enter_context(nc.sbuf_tensor("wqkv_sb", [128, MC * 3 * VC], BF16))
            xfm = [es2.enter_context(nc.sbuf_tensor(f"xfm{i}", [128, MC * TG], BF16))
                   for i in range(2)]
            ev = es2.enter_context(nc.sbuf_tensor("ev", [128, 4 * TG], BF16))
            p_qk = [es2.enter_context(nc.psum_tensor(f"p_qk{i}", [128, TG], F32))
                    for i in range(2)]
            p_v = [es2.enter_context(nc.psum_tensor(f"p_v{i}", [128, 512], F32))
                   for i in range(2)]

            nstrip = es2.enter_context(nc.sbuf_tensor("nstrip", [1, 2 * NH * NTG], F32))
            nsq2 = [es2.enter_context(nc.sbuf_tensor(f"nsq2{i}", [128, TG], BF16))
                    for i in range(2)]
            p_nn2 = [es2.enter_context(nc.psum_tensor(f"p_nn2{i}", [128, 512], F32))
                     for i in range(2)]
            nrm_ctr = [0]
            nsq_dve = {}
            nred_dve = {}

            @block.sync
            def _(sy, wcn=d_wc.n):
                sy.wait_ge(d_wc.h, wcn)
                src = wqkvbf.rearrange("(m p) f -> p m f", p=128)
                sy.dma_start(out=wqkv_sb[:].rearrange("p (m f) -> p m f", m=MC),
                             in_=src).then_inc(d_wl.h, 16)
            d_wl.n += 16
            WQKV_LD = d_wl.n

            ev_cnt = [0]
            ev_slot_uses = [0, 0, 0, 0]
            ev_slot_sq = {}
            last_act_for_pqk = {}
            last_act_for_pv = {}

            def load_xfm(tg):
                buf = xfm[tg % 2]
                sem = d_x[tg % 2]

                @block.sync
                def _(sy, tg=tg, buf=buf, sem=sem, pen=s_pe.n):
                    sy.wait_ge(s_pe.h, pen)
                    if (tg + 1) * TG <= S // 2:
                        sy.wait_ge(d_xb.h, XBF_LD1)
                    else:
                        sy.wait_ge(d_xb2.h, XBF_LD2)
                    for mc in range(MC):
                        sy.dma_start_transpose(
                            out=buf[:, mc * TG:(mc + 1) * TG],
                            in_=xbf[tg * TG:(tg + 1) * TG, mc * 128:(mc + 1) * 128]
                        ).then_inc(sem.h, 16)
                sem.n += 16 * MC

            def qkv_tile(kind, idx, tg, lhs_fn, rhs_fn, psl, last_act_map, scale, dst_fn):
                p = psl[idx % 2]

                @block.tensor
                def _(t, p=p, kind=kind, idx=idx, tg=tg,
                      xn=d_x[tg % 2].n, wq=WQKV_LD,
                      pfree=last_act_map.get(idx % 2, 0)):
                    t.wait_ge(d_x[tg % 2].h, xn)
                    t.wait_ge(d_wl.h, wq)
                    t.wait_ge(s_act.h, pfree)
                    for mc in range(MC):
                        last = t.matmul(p[:, 0:rhs_w], lhs_fn(t, mc), rhs_fn(t, mc),
                                        start=(mc == 0), stop=(mc == MC - 1))
                    last.then_inc(s_pe.h, 1)
                s_pe.n += 1

                slot = ev_cnt[0] % 4

                @block.scalar
                def _(s, p=p, slot=slot, pen=s_pe.n, su=ev_slot_uses[slot],
                      sqfree=ev_slot_sq.get(slot, None)):
                    s.wait_ge(s_pe.h, pen)
                    s.wait_ge(d_e[slot].h, su * 16)
                    if sqfree is not None:
                        s.wait_ge(s_dve.h, sqfree)
                    s.activation(ev[:, slot * TG: slot * TG + rhs_w],
                                 p[:, 0:rhs_w], AF.Copy, scale=scale
                                 ).then_inc(s_act.h, 1)
                s_act.n += 1
                last_act_map[idx % 2] = s_act.n

                @block.sync
                def _(sy, slot=slot, acn=s_act.n):
                    sy.wait_ge(s_act.h, acn)
                    dst_fn(sy, ev[:, slot * TG: slot * TG + rhs_w], d_e[slot])
                d_e[slot].n += 16
                if kind == "qk":
                    u = nrm_ctr[0]

                    @block.vector
                    def _(v, slot=slot, u=u, acn=s_act.n,
                          sqf=nsq_dve.get(u - 2, 0), pef=s_pe.n):
                        v.wait_ge(s_act.h, acn)
                        v.wait_ge(s_pe.h, pef)
                        v.tensor_tensor(out=nsq2[u % 2][:],
                                        in0=ev[:, slot * TG: (slot + 1) * TG],
                                        in1=ev[:, slot * TG: (slot + 1) * TG],
                                        op=ALU.mult).then_inc(s_dve.h, 1)
                    s_dve.n += 1
                    nsq_dve[u] = s_dve.n
                    ev_slot_sq[slot] = s_dve.n

                    @block.tensor
                    def _(t, u=u, dvn=s_dve.n, rfree=nred_dve.get(u - 2, 0)):
                        t.wait_ge(s_dve.h, dvn)
                        t.wait_ge(s_dve.h, rfree)
                        t.matmul(p_nn2[u % 2][:], ones128[:], nsq2[u % 2][:],
                                 start=True, stop=True).then_inc(s_pe.h, 1)
                    s_pe.n += 1

                    @block.vector
                    def _(v, u=u, pen=s_pe.n):
                        v.wait_ge(s_pe.h, pen)
                        v.tensor_reduce(out=nstrip[0:1, u:u + 1],
                                        in_=p_nn2[u % 2][0:1, :], axis=AXL.X,
                                        op=ALU.max).then_inc(s_dve.h, 1)
                    s_dve.n += 1
                    nred_dve[u] = s_dve.n
                    nrm_ctr[0] += 1
                ev_slot_uses[slot] += 1
                ev_cnt[0] += 1

            load_xfm(0)
            for tg in range(NTG):
                if tg + 1 < NTG:
                    load_xfm(tg + 1)
                buf = xfm[tg % 2]
                rhs_w = TG
                for j in range(2 * NH):
                    def dst_qk(g, src_ap, dsem, j=j, tg=tg):
                        scr = qscr if j < NH else kscr
                        hh = j % NH
                        g.dma_start(out=scr[hh * 128:(hh + 1) * 128, tg * TG:(tg + 1) * TG],
                                    in_=src_ap).then_inc(dsem.h, 16)
                    qkv_tile("qk", j, tg,
                             (lambda t, mc, j=j: wqkv_sb[:, mc * 3 * VC + j * 128:
                                                         mc * 3 * VC + (j + 1) * 128]),
                             (lambda t, mc, buf=buf: buf[:, mc * TG:(mc + 1) * TG]),
                             p_qk, last_act_for_pqk,
                             Q_SCALE if j < NH else 1.0, dst_qk)
                rhs_w = 512
                vi = 0
                for tt in range(TG // 128):
                    for fh in range(VC // 512):
                        def dst_v(g, src_ap, dsem, tt=tt, fh=fh, tg=tg):
                            cc0 = (tg * TG + tt * 128) // 128
                            dst = vscr.rearrange("(h p cc) d -> p h cc d",
                                                 h=NH, p=128)
                            g.dma_start(out=dst[:, fh * 4:(fh + 1) * 4, cc0, :],
                                        in_=src_ap.rearrange("p (h d) -> p h d", d=128)
                                        ).then_inc(dsem.h, 16)
                        qkv_tile("v", vi, tg,
                                 (lambda t, mc, buf=buf, tt=tt:
                                  buf[:, mc * TG + tt * 128: mc * TG + (tt + 1) * 128]),
                                 (lambda t, mc, fh=fh:
                                  wqkv_sb[:, mc * 3 * VC + 2 * VC + fh * 512:
                                          mc * 3 * VC + 2 * VC + (fh + 1) * 512]),
                                 p_v, last_act_for_pv, 1.0, dst_v)
                        vi += 1

            with ExitStack() as esn:
                nrm = esn.enter_context(nc.sbuf_tensor("nrm", [1, 6 * NH], F32))
                p_bb = esn.enter_context(nc.psum_tensor("p_bb", [128, NH], F32))

                @block.vector
                def _(v, dvn=s_dve.n):
                    v.wait_ge(s_dve.h, dvn)
                    v.drain()
                    for h in range(NH):
                        v.tensor_reduce(out=nrm[0:1, h:h + 1],
                                        in_=nstrip[0:1, h:2 * NH * NTG:2 * NH],
                                        axis=AXL.X, op=ALU.max)
                        v.tensor_reduce(out=nrm[0:1, NH + h:NH + h + 1],
                                        in_=nstrip[0:1, NH + h:2 * NH * NTG:2 * NH],
                                        axis=AXL.X, op=ALU.max)
                    v.drain()
                    v.tensor_tensor(out=nrm[0:1, 2 * NH:3 * NH],
                                    in0=nrm[0:1, 0:NH],
                                    in1=nrm[0:1, NH:2 * NH],
                                    op=ALU.mult).then_inc(s_dve.h, 1)
                s_dve.n += 1

                @block.scalar
                def _(s, dvn=s_dve.n):
                    s.wait_ge(s_dve.h, dvn)
                    s.activation(nrm[0:1, 3 * NH:4 * NH], nrm[0:1, 2 * NH:3 * NH],
                                 AF.Ln)
                    s.drain()
                    s.activation(nrm[0:1, 4 * NH:5 * NH], nrm[0:1, 3 * NH:4 * NH],
                                 AF.Exp, scale=0.5).then_inc(s_act.h, 1)
                s_act.n += 1

                @block.vector
                def _(v, acn=s_act.n):
                    v.wait_ge(s_act.h, acn)
                    v.tensor_scalar(out=nrm[0:1, 5 * NH:6 * NH],
                                    in0=nrm[0:1, 4 * NH:5 * NH],
                                    scalar1=-1.0, scalar2=None,
                                    op0=ALU.mult).then_inc(s_dve.h, 1)
                s_dve.n += 1

                @block.tensor
                def _(t, dvn=s_dve.n):
                    t.wait_ge(s_dve.h, dvn)
                    t.matmul(p_bb[:], onesr[:], nrm[0:1, 5 * NH:6 * NH],
                             start=True, stop=True).then_inc(s_pe.h, 1)
                s_pe.n += 1

                @block.vector
                def _(v, pen=s_pe.n):
                    v.wait_ge(s_pe.h, pen)
                    v.tensor_copy(out=ubias[:, 0:NH],
                                  in_=p_bb[:]).then_inc(s_dve.h, 1)
                s_dve.n += 1
                block.flush()

        phase_barrier()

        with ExitStack() as es3:
            K_h = [es3.enter_context(nc.sbuf_tensor(f"K_h{i}", [128, S], BF16)) for i in range(2)]
            Q_h = [es3.enter_context(nc.sbuf_tensor(f"Q_h{i}", [128, S], BF16)) for i in range(2)]
            V_h = [es3.enter_context(nc.sbuf_tensor(f"V_h{i}", [128, S], BF16)) for i in range(2)]
            A_t = es3.enter_context(nc.sbuf_tensor("A_t", [128, 2 * 1024], BF16))
            r_ln = es3.enter_context(nc.sbuf_tensor("r_ln", [128, 512], F32))
            r_rec = es3.enter_context(nc.sbuf_tensor("r_rec", [128, 2 * 512], F32))
            O_t = es3.enter_context(nc.sbuf_tensor("O_t", [128, 2 * 512], BF16))
            p_S = [es3.enter_context(nc.psum_tensor(f"p_S{i}", [128, 1024], F32))
                   for i in range(3)]
            p_O = es3.enter_context(nc.psum_tensor("p_O", [128, 512], F32))
            p_rb = es3.enter_context(nc.psum_tensor("p_rb", [128, 512], F32))

            head_done_pe = {}
            exp_act = {}
            cons_pe = {}
            ep = {}
            gp_ctr = [0]
            tl_ctr = [0]

            def load_head(h):
                par = h % 2
                sem = d_k[par]

                @block.gpsimd
                def _(g, h=h, par=par, sem=sem,
                      free=head_done_pe.get(h - 2, 0),
                      devals=[(dd.h, dd.n) for dd in d_e] if h < 2 else []):
                    g.wait_ge(s_pe.h, free)
                    for hh2, vv2 in devals:
                        g.wait_ge(hh2, vv2)
                    g.dma_start(out=K_h[par][:], in_=kscr[h * 128:(h + 1) * 128, :]
                                ).then_inc(sem.h, 16)
                    g.dma_start(out=Q_h[par][:], in_=qscr[h * 128:(h + 1) * 128, :]
                                ).then_inc(sem.h, 16)
                    vsrc = vscr.rearrange("(h p cc) d -> h p cc d", h=NH, p=128)
                    vdst = V_h[par][:].rearrange("p (cc d) -> p cc d", d=128)
                    g.dma_start(out=vdst, in_=vsrc[h]).then_inc(sem.h, 16)
                sem.n += 48

            def emit_epilogue(tile):
                if tile not in ep:
                    return
                e = ep[tile]

                @block.vector
                def _(v, e=e, don=d_o[e["tp"]].n):
                    v.wait_ge(s_act.h, e["recip"])
                    v.wait_ge(d_o[e["tp"]].h, don)
                    v.tensor_tensor(out=O_t[:, e["tp"] * 512:(e["tp"] + 1) * 512],
                                    in0=p_O[:],
                                    in1=r_rec[:, e["tp"] * 512:(e["tp"] + 1) * 512],
                                    op=ALU.mult).then_inc(s_dve.h, 1)
                s_dve.n += 1
                e["mult"] = s_dve.n

                @block.gpsimd
                def _(g, e=e):
                    g.wait_ge(s_dve.h, e["mult"])
                    g.dma_start(out=ag2_i[e["h"] * 128:(e["h"] + 1) * 128,
                                          e["q0"]:e["q0"] + 512],
                                in_=O_t[:, e["tp"] * 512:(e["tp"] + 1) * 512]
                                ).then_inc(d_o[e["tp"]].h, 16)
                d_o[e["tp"]].n += 16

            load_head(0)
            if NH > 1:
                load_head(1)
            for h in range(NH):
                par = h % 2
                ksem_val = d_k[par].n
                for qt in range(NQT):
                    tile = tl_ctr[0]
                    tp = tile % 2
                    q0 = qt * QT
                    cmax = (qt + 1) * CPT
                    G = cmax // 2
                    gp0 = gp_ctr[0]

                    emit_epilogue(tile - 1)

                    def s_pair(g, h=h, par=par, q0=q0, gp0=gp0, G=G, ksem_val=ksem_val):
                        if g >= G:
                            return
                        gp = gp0 + g

                        @block.tensor
                        def _(t, g=g, gp=gp,
                              psfree=exp_act.get(gp - 3, 0)):
                            t.wait_ge(s_act.h, psfree)
                            if g == 0:
                                t.wait_ge(d_k[par].h, ksem_val)
                            for ci in range(2):
                                c = 2 * g + ci
                                last = t.matmul(p_S[gp % 3][:, ci * 512:(ci + 1) * 512],
                                                K_h[par][:, c * 128:(c + 1) * 128],
                                                Q_h[par][:, q0:q0 + 512],
                                                start=True, stop=True)
                            last.then_inc(s_pe.h, 1)
                        s_pe.n += 1
                        return s_pe.n

                    smm = {}
                    smm[0] = s_pair(0)
                    if G > 1:
                        smm[1] = s_pair(1)
                    if G > 2:
                        smm[2] = s_pair(2)
                    for g in range(G):
                        gp = gp0 + g
                        @block.scalar
                        def _(s, g=g, gp=gp, h=h, sv=smm[g],
                              afree=cons_pe.get(gp - 2, 0)):
                            s.wait_ge(s_pe.h, sv)
                            s.wait_ge(s_pe.h, afree)
                            s.activation(
                                A_t[:, (gp % 2) * 1024:(gp % 2) * 1024 + 1024],
                                p_S[gp % 3][:], AF.Exp,
                                bias=ubias[:, h:h + 1]).then_inc(s_act.h, 1)
                        s_act.n += 1
                        exp_act[gp] = s_act.n
                        gate_sem, gate_val = s_act, s_act.n
                        if 2 * g >= cmax - CPT:
                            @block.vector
                            def _(v, g=g, gp=gp, qt=qt, cmax=cmax, acn=s_act.n):
                                v.wait_ge(s_act.h, acn)
                                for ci in range(2):
                                    c = 2 * g + ci
                                    jj = c - (cmax - CPT)
                                    sl = A_t[:, (gp % 2) * 1024 + ci * 512:
                                             (gp % 2) * 1024 + (ci + 1) * 512]
                                    last = v.tensor_tensor(out=sl, in0=sl,
                                                           in1=maskt[:, MOFF - 128 * jj:
                                                                     MOFF - 128 * jj + QT],
                                                           op=ALU.mult)
                                last.then_inc(s_dve.h, 1)
                            s_dve.n += 1
                            gate_sem, gate_val = s_dve, s_dve.n
                        @block.tensor
                        def _(t, g=g, gp=gp, tile=tile, tp=tp, par=par, cmax=cmax,
                              gs=gate_sem, gv=gate_val,
                              pofree=(ep.get(tile - 1, {}).get("mult", 0)),
                              prfree=(ep.get(tile - 1, {}).get("recip", 0))):
                            t.wait_ge(gs.h, gv)
                            if g == 0:
                                t.wait_ge(s_dve.h, pofree)
                                t.wait_ge(s_act.h, prfree)
                            for ci in range(2):
                                c = 2 * g + ci
                                sl = A_t[:, (gp % 2) * 1024 + ci * 512:
                                         (gp % 2) * 1024 + (ci + 1) * 512]
                                t.matmul(p_rb[:], ones128[:], sl,
                                         start=(c == 0), stop=(c == cmax - 1))
                                last = t.matmul(p_O[:],
                                                V_h[par][:, c * 128:(c + 1) * 128],
                                                sl, start=(c == 0), stop=(c == cmax - 1))
                            last.then_inc(s_pe.h, 1)
                        s_pe.n += 1
                        cons_pe[gp] = s_pe.n
                        if g + 3 < G:
                            smm[g + 3] = s_pair(g + 3)
                    @block.scalar
                    def _(s, tp=tp, pen=s_pe.n,
                          rfree=(ep.get(tile - 2, {}).get("mult", 0))):
                        s.wait_ge(s_pe.h, pen)
                        s.wait_ge(s_dve.h, rfree)
                        s.activation(r_ln[:], p_rb[:], AF.Ln)
                        s.drain()
                        s.activation(r_rec[:, tp * 512:(tp + 1) * 512], r_ln[:],
                                     AF.Exp, scale=-1.0).then_inc(s_act.h, 1)
                    s_act.n += 1
                    ep[tile] = {"recip": s_act.n, "tp": tp, "h": h, "q0": q0}
                    gp_ctr[0] += G
                    tl_ctr[0] += 1
                head_done_pe[h] = s_pe.n
                if h + 2 < NH:
                    load_head(h + 2)
            emit_epilogue(tl_ctr[0] - 1)
            block.flush()

        @block.gpsimd
        def _(g, don0=d_o[0].n, don1=d_o[1].n):
            g.wait_ge(d_o[0].h, don0)
            g.wait_ge(d_o[1].h, don1)
            g.collective_compute("AllGather", ALU.bypass,
                                 replica_groups=[list(range(n_cores))],
                                 ins=[ag2_i.opt()],
                                 outs=[ag2_o.opt()]).then_inc(s_cc.h, 1)
        s_cc.n += 1
        CC2 = s_cc.n

        phase_barrier()

        with ExitStack() as es4:
            o_sb = es4.enter_context(nc.sbuf_tensor("o_sb", [128, 2 * NH * SH], BF16))
            wout_sb = es4.enter_context(nc.sbuf_tensor("wout_sb", [128, MC * DM], BF16))
            xh_sb = [es4.enter_context(nc.sbuf_tensor(f"xh_sb{i}", [128, DM], F32))
                     for i in range(2)]
            upre = [es4.enter_context(nc.sbuf_tensor(f"upre{i}", [128, DM], F32))
                    for i in range(2)]
            sqscr = es4.enter_context(nc.sbuf_tensor("sqscr", [128, DM], F32))
            statp = es4.enter_context(nc.sbuf_tensor("statp", [128, 8], F32))
            u_t = [es4.enter_context(nc.sbuf_tensor(f"u_t{i}", [128, DM], BF16))
                   for i in range(2)]
            p_pr = [es4.enter_context(nc.psum_tensor(f"p_pr{i}", [128, DM], F32))
                    for i in range(2)]

            @block.gpsimd
            def _(g, ccv=CC2):
                g.dma_start(out=wout_sb[:].rearrange("p (m f) -> p m f", m=MC),
                            in_=wout_in.rearrange("(m p) f -> p m f", p=128)
                            ).then_inc(d_ob.h, 16)
                g.wait_ge(s_cc.h, ccv)
                pid = g.compute_val(nc.partition_id())
                with g.register("sibr") as rs, g.register("ownb") as ro, \
                        g.register("sibb") as rb, g.register("colr") as rc:
                    g.reg_alu(rs, pid, 1, ALU.bitwise_xor)
                    g.reg_alu(ro, pid, NH * 128, ALU.mult)
                    g.reg_alu(rb, rs, NH * 128, ALU.mult)
                    g.reg_alu(rc, pid, 1, ALU.bitwise_and)
                    g.reg_alu(rc, rc, SH, ALU.mult)
                    col_rv = RuntimeValue(rc)
                    for si, rreg in ((0, ro), (1, rb)):
                        src = ag2_o[ds(RuntimeValue(rreg), NH * 128), ds(col_rv, SH)]
                        src = src.rearrange("(hh p) q -> p hh q", p=128)
                        dst = o_sb[:, si * NH * SH:(si + 1) * NH * SH]
                        dst = dst.rearrange("p (hh q) -> p hh q", hh=NH)
                        g.dma_start(out=dst, in_=src).then_inc(d_ob.h, 16)
            d_ob.n += 16 + 32
            OSB_LD = d_ob.n

            ln_evac = {}
            norm_act = {}
            for ti in range(NT16):
                tpp = ti % 2

                @block.sync
                def _(sy, ti=ti, tpp=tpp, free=ln_evac.get(ti - 2, 0)):
                    sy.wait_ge(s_dve.h, free)
                    sy.dma_start(out=xh_sb[tpp][:], in_=xh_in[ti * 128:(ti + 1) * 128, :]
                                 ).then_inc(d_xh[tpp].h, 16)
                d_xh[tpp].n += 16

                @block.tensor
                def _(t, ti=ti, tpp=tpp, obn=OSB_LD, free=ln_evac.get(ti - 2, 0)):
                    t.wait_ge(d_ob.h, obn)
                    t.wait_ge(s_dve.h, free)
                    for ng in range(NGO):
                        for i in range(2 * NH):
                            last = t.matmul(
                                p_pr[tpp][:, ng * 512:(ng + 1) * 512],
                                o_sb[:, i * SH + ti * 128: i * SH + (ti + 1) * 128],
                                wout_sb[:, i * DM + ng * 512: i * DM + (ng + 1) * 512],
                                start=(i == 0), stop=(i == 2 * NH - 1))
                    last.then_inc(s_pe.h, 1)
                s_pe.n += 1

                @block.vector
                def _(v, ti=ti, tpp=tpp, pen=s_pe.n, xn=d_xh[tpp].n,
                      nfree=norm_act.get(ti - 2, 0)):
                    v.wait_ge(s_pe.h, pen)
                    v.wait_ge(d_xh[tpp].h, xn)
                    v.wait_ge(s_dve.h, nfree)
                    v.drain()
                    v.tensor_tensor(out=upre[tpp][:], in0=p_pr[tpp][:], in1=xh_sb[tpp][:],
                                    op=ALU.add)
                    v.drain()
                    v.tensor_reduce(out=statp[:, 0:1], in_=upre[tpp][:],
                                    axis=AXL.X, op=ALU.add,
                                    negate=True).then_inc(s_dve.h, 1)
                s_dve.n += 1
                ln_evac[ti] = s_dve.n

                @block.vector
                def _(v, tpp=tpp, dvn=s_dve.n):
                    v.wait_ge(s_dve.h, dvn)
                    v.drain()
                    v.tensor_tensor(out=sqscr[:], in0=upre[tpp][:], in1=upre[tpp][:],
                                    op=ALU.mult)
                    v.drain()
                    v.tensor_reduce(out=statp[:, 1:2], in_=sqscr[:], axis=AXL.X,
                                    op=ALU.add)
                    v.drain()
                    v.tensor_scalar(out=statp[:, 2:3], in0=statp[:, 0:1],
                                    scalar1=1.0 / DM, scalar2=None, op0=ALU.mult)
                    v.tensor_scalar(out=statp[:, 3:4], in0=statp[:, 1:2],
                                    scalar1=1.0 / DM, scalar2=None, op0=ALU.mult)
                    v.drain()
                    v.tensor_tensor(out=statp[:, 4:5], in0=statp[:, 2:3],
                                    in1=statp[:, 2:3], op=ALU.mult)
                    v.drain()
                    v.tensor_tensor(out=statp[:, 5:6], in0=statp[:, 3:4],
                                    in1=statp[:, 4:5], op=ALU.subtract)
                    v.drain()
                    v.tensor_scalar(out=statp[:, 5:6], in0=statp[:, 5:6],
                                    scalar1=EPS, scalar2=None,
                                    op0=ALU.add).then_inc(s_dve.h, 1)
                s_dve.n += 1

                @block.scalar
                def _(s, dvn=s_dve.n):
                    s.wait_ge(s_dve.h, dvn)
                    s.drain()
                    s.activation(statp[:, 6:7], statp[:, 5:6], AF.Ln)
                    s.drain()
                    s.activation(statp[:, 7:8], statp[:, 6:7], AF.Exp,
                                 scale=-0.5).then_inc(s_act.h, 1)
                s_act.n += 1

                @block.vector
                def _(v, acn=s_act.n):
                    v.wait_ge(s_act.h, acn)
                    v.drain()
                    v.tensor_tensor(out=statp[:, 2:3], in0=statp[:, 2:3],
                                    in1=statp[:, 7:8],
                                    op=ALU.mult).then_inc(s_dve.h, 1)
                s_dve.n += 1

                @block.vector
                def _(v, ti=ti, tpp=tpp, dvn=s_dve.n, dun=d_u[tpp].n):
                    v.wait_ge(s_dve.h, dvn)
                    v.wait_ge(d_u[tpp].h, dun)
                    v.drain()
                    v.tensor_scalar(out=u_t[tpp][:], in0=upre[tpp][:],
                                    scalar1=statp[:, 7:8], scalar2=statp[:, 2:3],
                                    op0=ALU.mult, op1=ALU.add).then_inc(s_dve.h, 1)
                s_dve.n += 1

                norm_act[ti] = s_dve.n

                @block.gpsimd
                def _(g, ti=ti, tpp=tpp, acn=s_dve.n):
                    g.wait_ge(s_dve.h, acn)
                    g.dma_start(out=uscr[ti * 128:(ti + 1) * 128, :], in_=u_t[tpp][:]
                                ).then_inc(d_u[tpp].h, 16)
                d_u[tpp].n += 16
            block.flush()

        phase_barrier()

        with ExitStack() as es5:
            hbuf = es5.enter_context(nc.sbuf_tensor("hbuf", [128, KC * TB], BF16))
            u_fm = es5.enter_context(nc.sbuf_tensor("u_fm", [128, MC * TB], BF16))
            wu_sb = [es5.enter_context(nc.sbuf_tensor(f"wu_sb{i}", [128, MC * 128], BF16))
                     for i in range(2)]
            wg_sb = [es5.enter_context(nc.sbuf_tensor(f"wg_sb{i}", [128, MC * 128], BF16))
                     for i in range(2)]
            sgb = [es5.enter_context(nc.sbuf_tensor(f"sgb{i}", [128, TB], F32))
                   for i in range(2)]
            wd_sb = [es5.enter_context(nc.sbuf_tensor(f"wd_sb{i}", [128, 8 * 512], BF16))
                     for i in range(2)]
            u_ng = [es5.enter_context(nc.sbuf_tensor(f"u_ng{i}", [128, TSUB * 512], BF16))
                    for i in range(2)]
            res = es5.enter_context(nc.sbuf_tensor("res", [128, TSUB * DM], BF16))
            sums = es5.enter_context(nc.sbuf_tensor("sums", [128, TSUB * NGO], F32))
            ssq = es5.enter_context(nc.sbuf_tensor("ssq", [128, TSUB * NGO], F32))
            st2 = es5.enter_context(nc.sbuf_tensor("st2", [128, 4 * TSUB], F32))
            sq2 = es5.enter_context(nc.sbuf_tensor("sq2", [128, 512], F32))
            outt = [es5.enter_context(nc.sbuf_tensor(f"outt{i}", [128, DM], F32))
                    for i in range(2)]
            p_up = [es5.enter_context(nc.psum_tensor(f"p_up{i}", [128, TB], F32))
                    for i in range(2)]
            p_gt = [es5.enter_context(nc.psum_tensor(f"p_gt{i}", [128, TB], F32))
                    for i in range(2)]
            p_dn = [es5.enter_context(nc.psum_tensor(f"p_dn{i}", [128, 512], F32))
                    for i in range(TSUB)]

            UG_SLOT = 2 * JL * MC * 128
            ug_last_pe = {0: 0, 1: 0}
            wd_last_pe = {0: 0, 1: 0}
            ug_end_pe = {}
            for nb in range(NB):
                t0 = nb * TB
                @block.sync
                def _(sy, nb=nb, t0=t0,
                      pen=(ug_end_pe.get(nb - 1, 0) if nb > 0 else s_pe.n),
                      du0=d_u[0].n, du1=d_u[1].n):
                    sy.wait_ge(s_pe.h, pen)
                    sy.wait_ge(d_u[0].h, du0)
                    sy.wait_ge(d_u[1].h, du1)
                    for mc in range(MC):
                        sy.dma_start_transpose(
                            out=u_fm[:, mc * TB:(mc + 1) * TB],
                            in_=uscr[t0:t0 + TB, mc * 128:(mc + 1) * 128]
                        ).then_inc(d_uf.h, 16)
                d_uf.n += 16 * MC
                UFM_LD = d_uf.n

                hdve = {}
                ugact = {}
                wld = {}
                ACT0B = s_act.n
                DVE0B = s_dve.n

                def load_ug(j):
                    sl = j % 2

                    @block.sync
                    def _(sy, j=j, sl=sl, ccv=CC1, pefree=ug_last_pe[j % 2]):
                        sy.wait_ge(s_cc.h, ccv)
                        sy.wait_ge(s_pe.h, pefree)
                        slot = j // JL
                        jl = j % JL
                        for w_i, wbuf in ((0, wu_sb[sl]), (1, wg_sb[sl])):
                            rbase = slot * UG_SLOT + (w_i * JL + jl) * MC * 128
                            src = ag1u_o[rbase:rbase + MC * 128, :]
                            sy.dma_start(out=wbuf[:].rearrange("p (m c) -> p m c", m=MC),
                                         in_=src.rearrange("(m p) c -> p m c", p=128)
                                         ).then_inc(d_f[sl].h, 16)
                    d_f[j % 2].n += 32
                    wld[("ld", j)] = d_f[j % 2].n

                load_ug(0)
                if KC > 1:
                    load_ug(1)
                for j in range(KC):
                    sl = j % 2

                    @block.tensor
                    def _(t, j=j, sl=sl, ldn=wld[("ld", j)], un=UFM_LD,
                          pfree=ugact.get(j - 2, ACT0B), pfree2=hdve.get(j - 2, DVE0B)):
                        t.wait_ge(d_f[sl].h, ldn)
                        t.wait_ge(d_uf.h, un)
                        t.wait_ge(s_act.h, pfree)
                        t.wait_ge(s_dve.h, pfree2)
                        for mc in range(MC):
                            t.matmul(p_up[sl][:], wu_sb[sl][:, mc * 128:(mc + 1) * 128],
                                     u_fm[:, mc * TB:(mc + 1) * TB],
                                     start=(mc == 0), stop=(mc == MC - 1))
                        for mc in range(MC):
                            last = t.matmul(p_gt[sl][:], wg_sb[sl][:, mc * 128:(mc + 1) * 128],
                                            u_fm[:, mc * TB:(mc + 1) * TB],
                                            start=(mc == 0), stop=(mc == MC - 1))
                        last.then_inc(s_pe.h, 1)
                    s_pe.n += 1
                    wld[("pe", j)] = s_pe.n
                    ug_last_pe[j % 2] = s_pe.n

                    @block.scalar
                    def _(s, j=j, sl=sl, pen=s_pe.n, hfree=hdve.get(j - 2, DVE0B)):
                        s.wait_ge(s_pe.h, pen)
                        s.wait_ge(s_dve.h, hfree)
                        s.activation(sgb[sl][:], p_gt[sl][:],
                                     AF.Silu).then_inc(s_act.h, 1)
                    s_act.n += 1
                    ugact[j] = s_act.n

                    @block.vector
                    def _(v, j=j, sl=sl, acn=s_act.n):
                        v.wait_ge(s_act.h, acn)
                        v.scalar_tensor_tensor(
                            out=hbuf[:, j * TB:(j + 1) * TB], in0=p_up[sl][:],
                            scalar=1.0, in1=sgb[sl][:], op0=ALU.mult,
                            op1=ALU.mult).then_inc(s_dve.h, 1)
                    s_dve.n += 1
                    hdve[j] = s_dve.n
                    if j + 2 < KC:
                        load_ug(j + 2)
                H_DONE = s_dve.n
                ug_end_pe[nb] = s_pe.n

                KG = KC // 8
                dn_evac = {}
                dwld = {}
                ACT0D = s_act.n
                DVE0D = s_dve.n

                def load_wd(ng, kg):
                    sl = kg % 2

                    @block.sync
                    def _(sy, ng=ng, kg=kg, sl=sl, ccv=CC1, free=wd_last_pe[kg % 2]):
                        sy.wait_ge(s_cc.h, ccv)
                        sy.wait_ge(s_pe.h, free)
                        nsub = 512 // NSH if NSH < 512 else 1
                        csub = min(NSH, 512)
                        for si in range(nsub):
                            shard = (ng * 512) // NSH + si
                            rbase = shard * KC * 128 + kg * 8 * 128
                            src = ag1d_o[rbase:rbase + 8 * 128, :]
                            dstv = wd_sb[sl][:].rearrange("p (k n) -> p k n", k=8)
                            sy.dma_start(out=dstv[:, :, si * csub:(si + 1) * csub],
                                         in_=src.rearrange("(k p) n -> p k n", p=128)
                                         ).then_inc(d_d[sl].h, 16)
                    d_d[kg % 2].n += 16 * (512 // NSH if NSH < 512 else 1)
                    dwld[("ld", ng, kg)] = d_d[kg % 2].n

                for ng in range(NGO):
                    load_wd(ng, 0)
                    if KG > 1:
                        load_wd(ng, 1)
                    usl = ng % 2

                    @block.sync
                    def _(sy, ng=ng, t0=t0, usl=usl, free=dn_evac.get(ng - 2, DVE0D),
                          du0=d_u[0].n, du1=d_u[1].n):
                        sy.wait_ge(s_dve.h, free)
                        sy.wait_ge(d_u[0].h, du0)
                        sy.wait_ge(d_u[1].h, du1)
                        sy.dma_start(out=u_ng[usl][:].rearrange("p (t n) -> p t n", t=TSUB),
                                     in_=uscr[t0:t0 + TB, ng * 512:(ng + 1) * 512]
                                     .rearrange("(t p) n -> p t n", p=128)
                                     ).then_inc(d_un[usl].h, 16)
                    d_un[usl].n += 16

                    for kg in range(KG):
                        sl = kg % 2

                        @block.tensor
                        def _(t, ng=ng, kg=kg, sl=sl, ldn=dwld[("ld", ng, kg)],
                              hn=H_DONE, efree=dn_evac.get(ng - 1, DVE0D),
                              afree=dn_evac.get(("a", ng - 1), ACT0D)):
                            t.wait_ge(d_d[sl].h, ldn)
                            t.wait_ge(s_dve.h, hn)
                            if kg == 0:
                                t.wait_ge(s_dve.h, efree)
                                t.wait_ge(s_act.h, afree)
                            for k8 in range(8):
                                k = kg * 8 + k8
                                for ts in range(TSUB):
                                    last = t.matmul(
                                        p_dn[ts][:],
                                        hbuf[:, k * TB + ts * 128: k * TB + (ts + 1) * 128],
                                        wd_sb[sl][:, k8 * 512:(k8 + 1) * 512],
                                        start=(k == 0), stop=(k == KC - 1))
                            last.then_inc(s_pe.h, 1)
                        s_pe.n += 1
                        dwld[("pe", ng, kg)] = s_pe.n
                        wd_last_pe[kg % 2] = s_pe.n
                        if kg + 2 < KG:
                            load_wd(ng, kg + 2)
                    for ts in range(TSUB):
                        @block.vector
                        def _(v, ng=ng, ts=ts, usl=usl, pen=s_pe.n, dn=d_un[usl].n):
                            v.wait_ge(s_pe.h, pen)
                            v.wait_ge(d_un[usl].h, dn)
                            v.drain()
                            v.tensor_tensor(
                                out=res[:, ts * DM + ng * 512: ts * DM + (ng + 1) * 512],
                                in0=p_dn[ts][:], in1=u_ng[usl][:, ts * 512:(ts + 1) * 512],
                                op=ALU.add)
                            v.drain()
                            v.tensor_reduce(
                                out=sums[:, ts * NGO + ng: ts * NGO + ng + 1],
                                in_=res[:, ts * DM + ng * 512: ts * DM + (ng + 1) * 512],
                                axis=AXL.X, op=ALU.add).then_inc(s_dve.h, 1)
                        s_dve.n += 1

                        @block.vector
                        def _(v, ng=ng, ts=ts, dvn=s_dve.n):
                            v.drain()
                            rslice = res[:, ts * DM + ng * 512: ts * DM + (ng + 1) * 512]
                            v.tensor_tensor(out=sq2[:], in0=rslice, in1=rslice,
                                            op=ALU.mult)
                            v.drain()
                            v.tensor_reduce(
                                out=ssq[:, ts * NGO + ng: ts * NGO + ng + 1],
                                in_=sq2[:], axis=AXL.X,
                                op=ALU.add).then_inc(s_dve.h, 1)
                        s_dve.n += 1
                    dn_evac[ng] = s_dve.n
                    dn_evac[("a", ng)] = s_act.n

                @block.vector
                def _(v, dvn=s_dve.n):
                    v.wait_ge(s_dve.h, dvn)
                    v.drain()
                    for ts in range(TSUB):
                        v.tensor_reduce(out=st2[:, ts:ts + 1],
                                        in_=sums[:, ts * NGO:(ts + 1) * NGO],
                                        axis=AXL.X, op=ALU.add, negate=True)
                    for ts in range(TSUB):
                        v.tensor_reduce(out=st2[:, TSUB + ts:TSUB + ts + 1],
                                        in_=ssq[:, ts * NGO:(ts + 1) * NGO],
                                        axis=AXL.X, op=ALU.add)
                    v.drain()
                    v.tensor_scalar(out=st2[:, 0:TSUB], in0=st2[:, 0:TSUB],
                                    scalar1=1.0 / DM, scalar2=None, op0=ALU.mult)
                    v.tensor_scalar(out=st2[:, TSUB:2 * TSUB], in0=st2[:, TSUB:2 * TSUB],
                                    scalar1=1.0 / DM, scalar2=None, op0=ALU.mult)
                    v.drain()
                    v.tensor_tensor(out=st2[:, 2 * TSUB:3 * TSUB], in0=st2[:, 0:TSUB],
                                    in1=st2[:, 0:TSUB], op=ALU.mult)
                    v.drain()
                    v.tensor_tensor(out=st2[:, 2 * TSUB:3 * TSUB],
                                    in0=st2[:, TSUB:2 * TSUB],
                                    in1=st2[:, 2 * TSUB:3 * TSUB], op=ALU.subtract)
                    v.drain()
                    v.tensor_scalar(out=st2[:, 2 * TSUB:3 * TSUB],
                                    in0=st2[:, 2 * TSUB:3 * TSUB],
                                    scalar1=EPS, scalar2=None,
                                    op0=ALU.add).then_inc(s_dve.h, 1)
                s_dve.n += 1

                @block.scalar
                def _(s, dvn=s_dve.n):
                    s.wait_ge(s_dve.h, dvn)
                    s.drain()
                    s.activation(st2[:, 3 * TSUB:4 * TSUB], st2[:, 2 * TSUB:3 * TSUB],
                                 AF.Ln)
                    s.drain()
                    s.activation(st2[:, 2 * TSUB:3 * TSUB], st2[:, 3 * TSUB:4 * TSUB],
                                 AF.Exp, scale=-0.5).then_inc(s_act.h, 1)
                s_act.n += 1

                @block.vector
                def _(v, acn=s_act.n):
                    v.wait_ge(s_act.h, acn)
                    v.drain()
                    v.tensor_tensor(out=st2[:, 0:TSUB], in0=st2[:, 0:TSUB],
                                    in1=st2[:, 2 * TSUB:3 * TSUB],
                                    op=ALU.mult).then_inc(s_dve.h, 1)
                s_dve.n += 1

                for ts in range(TSUB):
                    @block.vector
                    def _(v, ts=ts, dvn=s_dve.n, don=d_out[ts % 2].n):
                        v.wait_ge(s_dve.h, dvn)
                        v.wait_ge(d_out[ts % 2].h, don)
                        v.drain()
                        v.tensor_scalar(out=outt[ts % 2][:],
                                        in0=res[:, ts * DM:(ts + 1) * DM],
                                        scalar1=st2[:, 2 * TSUB + ts:2 * TSUB + ts + 1],
                                        scalar2=st2[:, ts:ts + 1],
                                        op0=ALU.mult, op1=ALU.add).then_inc(s_dve.h, 1)
                    s_dve.n += 1

                    @block.gpsimd
                    def _(g, nb=nb, ts=ts, t0=t0, acn=s_dve.n):
                        g.wait_ge(s_dve.h, acn)
                        g.dma_start(out=out_ext[t0 + ts * 128: t0 + (ts + 1) * 128, :],
                                    in_=outt[ts % 2][:]).then_inc(d_out[ts % 2].h, 16)
                    d_out[ts % 2].n += 16
            block.flush()

        @block.gpsimd
        def _(g, don0=d_out[0].n, don1=d_out[1].n):
            g.wait_ge(d_out[0].h, don0)
            g.wait_ge(d_out[1].h, don1)
        block.flush()

    return nc


_NC_CACHE = {}


def shard_inputs(x, w_qkv, w_out, su_ug, su_dp, ws_ug, ws_dp, w_up, w_gate, w_down,
                 n_cores=8):
    B, S, DM = x.shape
    HEADS = DM // 128
    INTER = w_up.shape[0]
    NH = HEADS // 2
    VC = NH * 128
    SH = S // 2
    FSH = INTER // n_cores
    NSH = DM // n_cores
    MC, KC, NST = DM // 128, INTER // 128, NSH // 128
    wq = w_qkv[:, 0:DM]
    wk = w_qkv[:, DM:2 * DM]
    wv = w_qkv[:, 2 * DM:3 * DM]
    c = np.ascontiguousarray
    in_maps = []
    for core in range(n_cores):
        b, hs = core // 2, core % 2
        hsl = slice(hs * VC, (hs + 1) * VC)
        in_maps.append({
            "x_in": c(x[b]),
            "xh_in": c(x[b, hs * SH:(hs + 1) * SH]),
            "wqkv_in": c(np.concatenate([wq[:, hsl], wk[:, hsl], wv[:, hsl]], axis=1)),
            "wout_in": c(np.concatenate([w_out[hs * VC:(hs + 1) * VC],
                                         w_out[(1 - hs) * VC:(2 - hs) * VC]], axis=0)),
            "wup_in": c(w_up[core * FSH:(core + 1) * FSH]),
            "wgate_in": c(w_gate[core * FSH:(core + 1) * FSH]),
            "wdown_in": c(w_down[core * NSH:(core + 1) * NSH]),
            "suug_in": c(su_ug.reshape(MC, 128)),
            "sudp_in": c(su_dp.reshape(KC, 128)),
            "wsu_in": c(ws_ug[core * FSH:(core + 1) * FSH].reshape(FSH // 128, 128)),
            "wsg_in": c(ws_ug[INTER + core * FSH:INTER + (core + 1) * FSH]
                        .reshape(FSH // 128, 128)),
            "wsdp_in": c(ws_dp[core * NSH:(core + 1) * NSH].reshape(NST, 128)),
        })
    return in_maps


def kernel(L0_attn_x, L0_attn_w_qkv, L0_attn_w_out, L0_ffn_SU_ug, L0_ffn_SU_dp,
           L0_ffn_Wscale_ug, L0_ffn_Wscale_dp, L0_ffn_W_up, L0_ffn_W_gate, L0_ffn_W_down):
    x = np.asarray(L0_attn_x)
    B, S, DM = x.shape
    HEADS = DM // 128
    INTER = np.asarray(L0_ffn_W_up).shape[0]
    n_cores = 8
    NH = HEADS // 2
    VC = NH * 128
    SH = S // 2
    FSH = INTER // n_cores
    NSH = DM // n_cores
    MC, KC, NST = DM // 128, INTER // 128, NSH // 128

    key = (S, DM, HEADS, INTER)
    if key not in _NC_CACHE:
        _NC_CACHE[key] = build_nc(S, DM, HEADS, INTER, n_cores)
    nc = _NC_CACHE[key]

    in_maps = shard_inputs(
        x, np.asarray(L0_attn_w_qkv), np.asarray(L0_attn_w_out),
        np.asarray(L0_ffn_SU_ug), np.asarray(L0_ffn_SU_dp),
        np.asarray(L0_ffn_Wscale_ug), np.asarray(L0_ffn_Wscale_dp),
        np.asarray(L0_ffn_W_up), np.asarray(L0_ffn_W_gate), np.asarray(L0_ffn_W_down),
        n_cores)

    res = run_bass_kernel_spmd(nc, in_maps, core_ids=list(range(n_cores)))
    out = np.empty((B, S, DM), np.float32)
    for core in range(n_cores):
        b, hs = core // 2, core % 2
        out[b, hs * SH:(hs + 1) * SH] = res.results[core]["out_ext"]
    return out


# version 13
# speedup vs baseline: 1.0182x; 1.0182x over previous
import math
import numpy as np
import ml_dtypes
from contextlib import ExitStack

import concourse.bass as bass
import concourse.mybir as mybir
from concourse.bass_utils import run_bass_kernel_spmd
from concourse.expressions import RuntimeValue

F32 = mybir.dt.float32
BF16 = mybir.dt.bfloat16
AF = mybir.ActivationFunctionType
ALU = mybir.AluOpType
AXL = mybir.AxisListType
ds = bass.ds

EPS = 1e-5


def had(n):
    a = np.arange(n)
    bits = a[:, None] & a[None, :]
    pc = np.zeros((n, n), np.int64)
    x = bits
    while x.max() > 0:
        pc += x & 1
        x >>= 1
    return np.where(pc % 2 == 0, 1.0, -1.0).astype(np.float32)


def cgroups(total, w):
    off = 0
    while off < total:
        yield off, min(w, total - off)
        off += w


class Sem:
    def __init__(self, es, nc, name):
        self.h = es.enter_context(nc.semaphore(name))
        self.n = 0

    def inc(self, instr, k=1):
        instr.then_inc(self.h, k)
        self.n += k
        return self.n


class BufferedBlock:

    def __init__(self, real):
        self._real = real
        self._q = []

    def _add(self, eng, fn):
        if self._q and self._q[-1][0] == eng:
            self._q[-1][1].append(fn)
        else:
            self._q.append((eng, [fn]))
        return fn

    def tensor(self, fn):
        return self._add("tensor", fn)

    def scalar(self, fn):
        return self._add("scalar", fn)

    def vector(self, fn):
        return self._add("vector", fn)

    def gpsimd(self, fn):
        return self._add("gpsimd", fn)

    def sync(self, fn):
        return self._add("sync", fn)

    def flush(self):
        merged = {}
        order = []
        for eng, fns in self._q:
            if eng not in merged:
                merged[eng] = []
                order.append(eng)
            merged[eng].extend(fns)
        self._q = []
        for eng in order:
            fns = merged[eng]

            def body(e, fns=fns):
                for f in fns:
                    f(e)
            getattr(self._real, eng)(body)


def build_nc(S, DM, HEADS, INTER, n_cores=8):
    HD = 128
    assert DM == HEADS * HD and DM % 512 == 0 and INTER % 512 == 0 and S % 1024 == 0
    NH = HEADS // 2
    MC = DM // 128
    KC = INTER // 128
    FSH = INTER // n_cores
    JL = FSH // 128
    NSH = DM // n_cores
    NST = NSH // 128
    assert NST >= 1 and NSH % 128 == 0 and 512 % NSH == 0 or NSH % 512 == 0
    QT = 512
    NQT = S // QT
    CPT = QT // 128
    SH = S // 2
    TB = 512
    NB = SH // TB
    TSUB = TB // 128
    NGO = DM // 512
    VC = NH * HD
    TG = 512
    NTG = S // TG
    NSG = S // 512
    NT16 = SH // 128
    Q_SCALE = 1.0 / math.sqrt(float(HD))
    ALPHA_UG = 1.0 / math.sqrt(float(DM))
    ALPHA_DP = 1.0 / math.sqrt(float(INTER))
    MOFF = (CPT - 1) * 128
    MW = MOFF + QT

    nc = bass.Bass("TRN2", target_bir_lowering=False, debug=False)

    x_in = nc.dram_tensor("x_in", [S, DM], F32, kind="ExternalInput").ap()
    xh_in = nc.dram_tensor("xh_in", [SH, DM], F32, kind="ExternalInput").ap()
    wqkv_in = nc.dram_tensor("wqkv_in", [DM, 3 * VC], F32, kind="ExternalInput").ap()
    wout_in = nc.dram_tensor("wout_in", [DM, DM], F32, kind="ExternalInput").ap()
    wup_in = nc.dram_tensor("wup_in", [FSH, DM], F32, kind="ExternalInput").ap()
    wgate_in = nc.dram_tensor("wgate_in", [FSH, DM], F32, kind="ExternalInput").ap()
    wdown_in = nc.dram_tensor("wdown_in", [NSH, INTER], F32, kind="ExternalInput").ap()
    suug_in = nc.dram_tensor("suug_in", [MC, 128], F32, kind="ExternalInput").ap()
    sudp_in = nc.dram_tensor("sudp_in", [KC, 128], F32, kind="ExternalInput").ap()
    wsu_in = nc.dram_tensor("wsu_in", [JL, 128], F32, kind="ExternalInput").ap()
    wsg_in = nc.dram_tensor("wsg_in", [JL, 128], F32, kind="ExternalInput").ap()
    wsdp_in = nc.dram_tensor("wsdp_in", [NST, 128], F32, kind="ExternalInput").ap()
    out_ext = nc.dram_tensor("out_ext", [SH, DM], F32, kind="ExternalOutput").ap()

    h128_d = nc.inline_tensor(had(128).astype(ml_dtypes.bfloat16), name="h128c")
    identb_d = nc.inline_tensor(np.eye(128, dtype=ml_dtypes.bfloat16), name="identbc")
    identf_d = nc.inline_tensor(np.eye(128, dtype=np.float32), name="identfc")
    onesc_d = nc.inline_tensor(np.ones((128, 1), dtype=ml_dtypes.bfloat16), name="onescc")
    onesr_d = nc.inline_tensor(np.ones((1, 128), dtype=np.float32), name="onesrc")
    ones128_d = nc.inline_tensor(np.ones((128, 128), dtype=ml_dtypes.bfloat16),
                                 name="ones128c")
    mk = np.arange(128)[:, None] <= (np.arange(MW)[None, :] - MOFF)
    mask_d = nc.inline_tensor(mk.astype(ml_dtypes.bfloat16), name="maskc")

    xbf = nc.dram_tensor("xbf", [S, DM], BF16).ap()
    wqkvbf = nc.dram_tensor("wqkvbf", [DM, 3 * VC], BF16).ap()
    qscr = nc.dram_tensor("qscr", [NH * 128, S], BF16).ap()
    kscr = nc.dram_tensor("kscr", [NH * 128, S], BF16).ap()
    vscr = nc.dram_tensor("vscr", [NH * 128 * (S // 128), 128], BF16).ap()
    uscr = nc.dram_tensor("uscr", [SH, DM], BF16).ap()
    ag1u_i = nc.dram_tensor("ag1u_i", [2 * JL * MC * 128, 128], BF16).ap()
    ag1u_o = nc.dram_tensor("ag1u_o", [n_cores * 2 * JL * MC * 128, 128], BF16,
                            addr_space="Shared").ap()
    ag1d_i = nc.dram_tensor("ag1d_i", [KC * 128, NSH], BF16).ap()
    ag1d_o = nc.dram_tensor("ag1d_o", [n_cores * KC * 128, NSH], BF16,
                            addr_space="Shared").ap()
    ag2_i = nc.dram_tensor("ag2_i", [NH * 128, S], BF16).ap()
    ag2_o = nc.dram_tensor("ag2_o", [n_cores * NH * 128, S], BF16,
                            addr_space="Shared").ap()

    ES = ExitStack()
    with ES:
        h128 = ES.enter_context(nc.sbuf_tensor("h128", [128, 128], BF16))
        identb = ES.enter_context(nc.sbuf_tensor("identb", [128, 128], BF16))
        identf = ES.enter_context(nc.sbuf_tensor("identf", [128, 128], F32))
        onesc = ES.enter_context(nc.sbuf_tensor("onesc", [128, 1], BF16))
        onesr = ES.enter_context(nc.sbuf_tensor("onesr", [1, 128], F32))
        ones128 = ES.enter_context(nc.sbuf_tensor("ones128", [128, 128], BF16))
        maskt = ES.enter_context(nc.sbuf_tensor("maskt", [128, MW], BF16))
        su_ug = ES.enter_context(nc.sbuf_tensor("su_ug", [128, MC], F32))
        su_dp = ES.enter_context(nc.sbuf_tensor("su_dp", [128, KC], F32))
        ws_up = ES.enter_context(nc.sbuf_tensor("ws_up", [128, JL], F32))
        ws_gt = ES.enter_context(nc.sbuf_tensor("ws_gt", [128, JL], F32))
        ws_dp = ES.enter_context(nc.sbuf_tensor("ws_dp", [128, NST], F32))
        ubias = ES.enter_context(nc.sbuf_tensor("ubias", [128, NH], F32))

        s_pe = Sem(ES, nc, "s_pe")
        s_act = Sem(ES, nc, "s_act")
        s_dve = Sem(ES, nc, "s_dve")
        s_pool = Sem(ES, nc, "s_pool")
        s_sp = Sem(ES, nc, "s_sp")
        s_cc = Sem(ES, nc, "s_cc")
        d_gl = Sem(ES, nc, "d_gl")
        d_wc = Sem(ES, nc, "d_wc")
        d_wl = Sem(ES, nc, "d_wl")
        d_xb = Sem(ES, nc, "d_xb")
        d_xb2 = Sem(ES, nc, "d_xb2")
        d_c = Sem(ES, nc, "d_c")
        d_e = [Sem(ES, nc, f"d_e{i}") for i in range(4)]
        d_x = [Sem(ES, nc, f"d_x{i}") for i in range(2)]
        d_k = [Sem(ES, nc, f"d_k{i}") for i in range(2)]
        d_o = [Sem(ES, nc, f"d_o{i}") for i in range(2)]
        d_ag = Sem(ES, nc, "d_ag")
        d_ob = Sem(ES, nc, "d_ob")
        d_xh = [Sem(ES, nc, f"d_xh{i}") for i in range(2)]
        d_u = [Sem(ES, nc, f"d_u{i}") for i in range(2)]
        d_uf = Sem(ES, nc, "d_uf")
        d_un = [Sem(ES, nc, f"d_un{i}") for i in range(2)]
        d_f = [Sem(ES, nc, f"d_f{i}") for i in range(2)]
        d_d = [Sem(ES, nc, f"d_d{i}") for i in range(2)]
        d_out = [Sem(ES, nc, f"d_out{i}") for i in range(2)]

        block = BufferedBlock(ES.enter_context(nc.Block()))
        sb_pe = Sem(ES, nc, "sb_pe")
        sb_act = Sem(ES, nc, "sb_act")
        sb_dve = Sem(ES, nc, "sb_dve")
        sb_pool = Sem(ES, nc, "sb_pool")
        sb_sp = Sem(ES, nc, "sb_sp")

        def phase_barrier():
            block.flush()
            for s in (sb_pe, sb_act, sb_dve, sb_pool, sb_sp):
                s.n += 1

            @block.tensor
            def _(t):
                t.drain().then_inc(sb_pe.h, 1)

            @block.scalar
            def _(s):
                s.drain().then_inc(sb_act.h, 1)

            @block.vector
            def _(v):
                v.drain().then_inc(sb_dve.h, 1)

            @block.gpsimd
            def _(g):
                g.drain().then_inc(sb_pool.h, 1)

            @block.sync
            def _(sy):
                sy.drain().then_inc(sb_sp.h, 1)
            vals = [(s, s.n) for s in (sb_pe, sb_act, sb_dve, sb_pool, sb_sp)]
            for eng_name, own in (("tensor", sb_pe), ("scalar", sb_act), ("vector", sb_dve),
                                  ("gpsimd", sb_pool), ("sync", sb_sp)):
                def mk(own=own):
                    def f(eng):
                        for s, v in vals:
                            if s is not own:
                                eng.wait_ge(s.h, v)
                    return f
                getattr(block, eng_name)(mk())
            block.flush()

        @block.gpsimd
        def _(g):
            g.dma_start(out=xbf[0:S // 2, :], in_=x_in[0:S // 2, :]).then_inc(d_xb.h, 16)
        d_xb.n += 16
        XBF_LD1 = d_xb.n

        bg_casts = []
        for m4 in range(0, MC, 2):
            def mk_cast(m4=m4):
                def f(g):
                    g.dma_start(out=wqkvbf[m4 * 128:(m4 + 2) * 128, :],
                                in_=wqkv_in[m4 * 128:(m4 + 2) * 128, :]
                                ).then_inc(d_wc.h, 16)
                d_wc.n += 16
                return f
            bg_casts.append(mk_cast())
        for q4 in range(2):
            def mk_cast2(q4=q4):
                def f(g):
                    lo = S // 2 + q4 * (S // 4)
                    g.dma_start(out=xbf[lo:lo + S // 4, :],
                                in_=x_in[lo:lo + S // 4, :]).then_inc(d_xb2.h, 16)
                d_xb2.n += 16
                return f
            bg_casts.append(mk_cast2())

        def drip_cast(g):
            if bg_casts:
                bg_casts.pop(0)(g)

        @block.sync
        def _(sy):
            for dst, src in ((h128, h128_d), (identb, identb_d), (identf, identf_d),
                             (onesc, onesc_d), (onesr, onesr_d),
                             (ones128, ones128_d), (maskt, mask_d)):
                sy.dma_start(out=dst[:], in_=src.ap()).then_inc(d_c.h, 16)
        d_c.n += 7 * 16

        with ExitStack() as es0:
            tvin = es0.enter_context(nc.sbuf_tensor("tvin", [128, 128], F32))
            ptv = es0.enter_context(nc.psum_tensor("ptv", [128, 128], F32))
            vecs = [(suug_in, MC, su_ug, ALPHA_UG), (sudp_in, KC, su_dp, ALPHA_DP),
                    (wsu_in, JL, ws_up, 1.0), (wsg_in, JL, ws_gt, 1.0),
                    (wsdp_in, NST, ws_dp, 1.0)]
            for vi, (src, rows, dst, scale) in enumerate(vecs):
                @block.sync
                def _(sy, src=src, rows=rows, vi=vi, dvn=s_dve.n):
                    if vi > 0:
                        sy.wait_ge(s_dve.h, dvn)
                    sy.dma_start(out=tvin[0:rows, :], in_=src).then_inc(d_c.h, 16)
                d_c.n += 16

                @block.tensor
                def _(t, rows=rows, vi=vi, dcn=d_c.n, dvn=s_dve.n):
                    t.wait_ge(d_c.h, dcn)
                    if vi > 0:
                        t.wait_ge(s_dve.h, dvn)
                    t.transpose(ptv[:, 0:rows], tvin[0:rows, :],
                                identf[0:rows, 0:rows]).then_inc(s_pe.h, 1)
                s_pe.n += 1

                @block.vector
                def _(v, rows=rows, dst=dst, scale=scale, pen=s_pe.n):
                    v.wait_ge(s_pe.h, pen)
                    v.tensor_scalar(out=dst[:, 0:rows], in0=ptv[:, 0:rows],
                                    scalar1=float(scale), scalar2=None,
                                    op0=ALU.mult).then_inc(s_dve.h, 1)
                s_dve.n += 1
            block.flush()

        phase_barrier()

        with ExitStack() as es1:
            BW = max(MC * FSH, KC * NSH)
            b1 = es1.enter_context(nc.sbuf_tensor("b1", [128, BW], F32))
            b2 = es1.enter_context(nc.sbuf_tensor("b2", [128, BW], F32))
            wtb = es1.enter_context(nc.sbuf_tensor("wtb", [128, BW], BF16))
            wrow = es1.enter_context(nc.sbuf_tensor("wrow", [128, max(DM, INTER)], BF16))
            fout = es1.enter_context(nc.sbuf_tensor("fout", [128, max(FSH, NSH)], BF16))
            ptr4 = [es1.enter_context(nc.psum_tensor(f"ptr4{i}", [128, 512], BF16))
                    for i in range(2)]
            ph1 = [es1.enter_context(nc.psum_tensor(f"ph1{i}", [128, 1024], F32))
                   for i in range(2)]

            def transform(w_src, n_rows_tiles, n_chunks, chunk_w, row_w, sign_tile,
                          scale_rows_ap, ag_write):
                tr_evac = {}
                tgi = [0]
                for r in range(n_rows_tiles):
                    @block.gpsimd
                    def _(g, r=r, pen=s_pe.n):
                        g.wait_ge(s_pe.h, pen)
                        g.dma_start(out=wrow[:, 0:row_w],
                                    in_=w_src[r * 128:(r + 1) * 128, :]).then_inc(d_gl.h, 16)
                        drip_cast(g)
                    d_gl.n += 16

                    if scale_rows_ap is not None:
                        @block.vector
                        def _(v, r=r, dgn=d_gl.n):
                            v.wait_ge(d_gl.h, dgn)
                            v.tensor_scalar(out=wrow[:, 0:row_w],
                                            in0=wrow[:, 0:row_w],
                                            scalar1=scale_rows_ap(r),
                                            scalar2=None, op0=ALU.mult
                                            ).then_inc(s_dve.h, 1)
                        s_dve.n += 1

                    nch = row_w // 128
                    for cc4 in range(0, nch, 4):
                        ng4 = min(4, nch - cc4)
                        gslot = tgi[0] % 2

                        @block.tensor
                        def _(t, r=r, cc4=cc4, ng4=ng4, gslot=gslot, dgn=d_gl.n,
                              dvn=s_dve.n, dcn=d_c.n,
                              pfree=tr_evac.get(tgi[0] - 2, 0)):
                            t.wait_ge(d_c.h, dcn)
                            if scale_rows_ap is not None:
                                t.wait_ge(s_dve.h, dvn)
                            else:
                                t.wait_ge(d_gl.h, dgn)
                            t.wait_ge(s_act.h, pfree)
                            for i in range(ng4):
                                cc = cc4 + i
                                last = t.transpose(ptr4[gslot][:, i * 128:(i + 1) * 128],
                                                   wrow[:, cc * 128:(cc + 1) * 128],
                                                   identb[:])
                            last.then_inc(s_pe.h, 1)
                        s_pe.n += 1

                        @block.scalar
                        def _(s, r=r, cc4=cc4, ng4=ng4, gslot=gslot, pen=s_pe.n):
                            s.wait_ge(s_pe.h, pen)
                            for i in range(ng4):
                                cc = cc4 + i
                                last = s.activation(
                                    wtb[:, cc * chunk_w + r * 128:
                                        cc * chunk_w + (r + 1) * 128],
                                    ptr4[gslot][:, i * 128:(i + 1) * 128], AF.Copy)
                            last.then_inc(s_act.h, 1)
                        s_act.n += 1
                        tr_evac[tgi[0]] = s_act.n
                        tgi[0] += 1
                h_evac = {}
                hgi = [0]
                for cc in range(n_chunks):
                    for off, w in cgroups(chunk_w, 1024):
                        gslot = hgi[0] % 2

                        @block.tensor
                        def _(t, cc=cc, off=off, w=w, gslot=gslot, acn=s_act.n,
                              dvn=h_evac.get(hgi[0] - 2, 0)):
                            t.wait_ge(s_act.h, acn)
                            t.wait_ge(s_dve.h, dvn)
                            for o2, w2 in cgroups(w, 512):
                                last = t.matmul(
                                    ph1[gslot][:, o2:o2 + w2], h128[:],
                                    wtb[:, cc * chunk_w + off + o2:
                                        cc * chunk_w + off + o2 + w2],
                                    start=True, stop=True)
                            last.then_inc(s_pe.h, 1)
                        s_pe.n += 1

                        @block.vector
                        def _(v, cc=cc, off=off, w=w, gslot=gslot, pen=s_pe.n):
                            v.wait_ge(s_pe.h, pen)
                            v.tensor_copy(
                                out=b1[:, cc * chunk_w + off: cc * chunk_w + off + w],
                                in_=ph1[gslot][:, 0:w]).then_inc(s_dve.h, 1)
                        s_dve.n += 1
                        h_evac[hgi[0]] = s_dve.n
                        hgi[0] += 1
                nst = int(round(math.log2(n_chunks)))
                hw = chunk_w // 2

                def bfly(eng, sem, c0, c1):
                    cur, nxt = b1, b2
                    for st in range(nst):
                        stride = 1 << st
                        cv = cur[:, 0:n_chunks * chunk_w].rearrange(
                            "p (g two s w) -> p g two s w",
                            two=2, s=stride, w=chunk_w)
                        nv = nxt[:, 0:n_chunks * chunk_w].rearrange(
                            "p (g two s w) -> p g two s w",
                            two=2, s=stride, w=chunk_w)
                        eng.tensor_tensor(out=nv[:, :, 0, :, c0:c1],
                                          in0=cv[:, :, 0, :, c0:c1],
                                          in1=cv[:, :, 1, :, c0:c1],
                                          op=ALU.add)
                        eng.tensor_tensor(out=nv[:, :, 1, :, c0:c1],
                                          in0=cv[:, :, 0, :, c0:c1],
                                          in1=cv[:, :, 1, :, c0:c1],
                                          op=ALU.subtract)
                        eng.drain()
                        cur, nxt = nxt, cur
                    eng.drain().then_inc(sem, 1)

                @block.vector
                def _(v, dvn=s_dve.n, pon=s_pool.n):
                    v.wait_ge(s_dve.h, dvn)
                    v.wait_ge(s_pool.h, pon)
                    bfly(v, s_dve.h, 0, hw)
                s_dve.n += 1

                @block.gpsimd
                def _(g, dvn=s_dve.n - 1, pon=s_pool.n):
                    g.wait_ge(s_dve.h, dvn)
                    g.wait_ge(s_pool.h, pon)
                    bfly(g, s_pool.h, hw, chunk_w)
                s_pool.n += 1
                cur = b1 if nst % 2 == 0 else b2
                nxt = b2 if nst % 2 == 0 else b1
                for cc in range(n_chunks):
                    @block.vector
                    def _(v, cc=cc, cur=cur, dvn=s_dve.n, pon=s_pool.n, dan=d_ag.n):
                        v.wait_ge(s_dve.h, dvn)
                        v.wait_ge(s_pool.h, pon)
                        v.wait_ge(d_ag.h, dan)
                        v.tensor_scalar(
                            out=fout[:, 0:chunk_w], in0=cur[:, cc * chunk_w:(cc + 1) * chunk_w],
                            scalar1=sign_tile[:, cc:cc + 1], scalar2=None,
                            op0=ALU.mult).then_inc(s_dve.h, 1)
                    s_dve.n += 1

                    @block.gpsimd
                    def _(g, cc=cc, dvn=s_dve.n):
                        g.wait_ge(s_dve.h, dvn)
                        ag_write(g, cc)
                    d_ag.n += 16

            for w_i, (w_src, wsrows, base) in enumerate(
                    ((wup_in, ws_up, 0), (wgate_in, ws_gt, JL * MC * 128))):
                def agw_ug(g, mc, base=base):
                    dst = ag1u_i[base:base + JL * MC * 128, :]
                    dst = dst.rearrange("(jl m p) c -> p jl m c", jl=JL, m=MC)
                    g.dma_start(out=dst[:, :, mc, :],
                                in_=fout[:, 0:FSH].rearrange("p (jl c) -> p jl c", jl=JL)
                                ).then_inc(d_ag.h, 16)
                transform(w_src, JL, MC, FSH, DM, su_ug,
                          (lambda r, wsrows=wsrows: wsrows[:, r:r + 1]), agw_ug)
            def agw_dn(g, cc):
                g.dma_start(out=ag1d_i[cc * 128:(cc + 1) * 128, :],
                            in_=fout[:, 0:NSH]).then_inc(d_ag.h, 16)
            transform(wdown_in, NST, KC, NSH, INTER, su_dp,
                      (lambda r: ws_dp[:, r:r + 1]), agw_dn)

            if bg_casts:
                @block.gpsimd
                def _(g):
                    while bg_casts:
                        bg_casts.pop(0)(g)

            block.flush()

        @block.gpsimd
        def _(g, dan=d_ag.n):
            g.wait_ge(d_ag.h, dan)
            g.collective_compute("AllGather", ALU.bypass,
                                 replica_groups=[list(range(n_cores))],
                                 ins=[ag1u_i.opt()],
                                 outs=[ag1u_o.opt()]).then_inc(s_cc.h, 1)
            g.collective_compute("AllGather", ALU.bypass,
                                 replica_groups=[list(range(n_cores))],
                                 ins=[ag1d_i.opt()],
                                 outs=[ag1d_o.opt()]).then_inc(s_cc.h, 1)
        s_cc.n += 2
        CC1 = s_cc.n
        XBF_LD2 = d_xb2.n

        phase_barrier()

        with ExitStack() as es2:
            wqkv_sb = es2.enter_context(nc.sbuf_tensor("wqkv_sb", [128, MC * 3 * VC], BF16))
            xfm = [es2.enter_context(nc.sbuf_tensor(f"xfm{i}", [128, MC * TG], BF16))
                   for i in range(2)]
            ev = es2.enter_context(nc.sbuf_tensor("ev", [128, 4 * TG], BF16))
            p_qk = [es2.enter_context(nc.psum_tensor(f"p_qk{i}", [128, TG], F32))
                    for i in range(2)]
            p_v = [es2.enter_context(nc.psum_tensor(f"p_v{i}", [128, 512], F32))
                   for i in range(2)]

            nstrip = es2.enter_context(nc.sbuf_tensor("nstrip", [1, 2 * NH * NTG], F32))
            nsq2 = [es2.enter_context(nc.sbuf_tensor(f"nsq2{i}", [128, TG], BF16))
                    for i in range(2)]
            p_nn2 = [es2.enter_context(nc.psum_tensor(f"p_nn2{i}", [128, 512], F32))
                     for i in range(2)]
            nrm_ctr = [0]
            nsq_dve = {}
            nred_dve = {}

            @block.sync
            def _(sy, wcn=d_wc.n):
                sy.wait_ge(d_wc.h, wcn)
                src = wqkvbf.rearrange("(m p) f -> p m f", p=128)
                sy.dma_start(out=wqkv_sb[:].rearrange("p (m f) -> p m f", m=MC),
                             in_=src).then_inc(d_wl.h, 16)
            d_wl.n += 16
            WQKV_LD = d_wl.n

            ev_cnt = [0]
            ev_slot_uses = [0, 0, 0, 0]
            ev_slot_sq = {}
            last_act_for_pqk = {}
            last_act_for_pv = {}

            def load_xfm(tg):
                buf = xfm[tg % 2]
                sem = d_x[tg % 2]

                @block.sync
                def _(sy, tg=tg, buf=buf, sem=sem, pen=s_pe.n):
                    sy.wait_ge(s_pe.h, pen)
                    if (tg + 1) * TG <= S // 2:
                        sy.wait_ge(d_xb.h, XBF_LD1)
                    else:
                        sy.wait_ge(d_xb2.h, XBF_LD2)
                    for mc in range(MC):
                        sy.dma_start_transpose(
                            out=buf[:, mc * TG:(mc + 1) * TG],
                            in_=xbf[tg * TG:(tg + 1) * TG, mc * 128:(mc + 1) * 128]
                        ).then_inc(sem.h, 16)
                sem.n += 16 * MC

            def qkv_tile(kind, idx, tg, lhs_fn, rhs_fn, psl, last_act_map, scale, dst_fn):
                p = psl[idx % 2]

                @block.tensor
                def _(t, p=p, kind=kind, idx=idx, tg=tg,
                      xn=d_x[tg % 2].n, wq=WQKV_LD,
                      pfree=last_act_map.get(idx % 2, 0)):
                    t.wait_ge(d_x[tg % 2].h, xn)
                    t.wait_ge(d_wl.h, wq)
                    t.wait_ge(s_act.h, pfree)
                    for mc in range(MC):
                        last = t.matmul(p[:, 0:rhs_w], lhs_fn(t, mc), rhs_fn(t, mc),
                                        start=(mc == 0), stop=(mc == MC - 1))
                    last.then_inc(s_pe.h, 1)
                s_pe.n += 1

                slot = ev_cnt[0] % 4

                @block.scalar
                def _(s, p=p, slot=slot, pen=s_pe.n, su=ev_slot_uses[slot],
                      sqfree=ev_slot_sq.get(slot, None)):
                    s.wait_ge(s_pe.h, pen)
                    s.wait_ge(d_e[slot].h, su * 16)
                    if sqfree is not None:
                        s.wait_ge(s_dve.h, sqfree)
                    s.activation(ev[:, slot * TG: slot * TG + rhs_w],
                                 p[:, 0:rhs_w], AF.Copy, scale=scale
                                 ).then_inc(s_act.h, 1)
                s_act.n += 1
                last_act_map[idx % 2] = s_act.n

                @block.sync
                def _(sy, slot=slot, acn=s_act.n):
                    sy.wait_ge(s_act.h, acn)
                    dst_fn(sy, ev[:, slot * TG: slot * TG + rhs_w], d_e[slot])
                d_e[slot].n += 16
                if kind == "qk":
                    u = nrm_ctr[0]

                    @block.vector
                    def _(v, slot=slot, u=u, acn=s_act.n,
                          sqf=nsq_dve.get(u - 2, 0), pef=s_pe.n):
                        v.wait_ge(s_act.h, acn)
                        v.wait_ge(s_pe.h, pef)
                        v.tensor_tensor(out=nsq2[u % 2][:],
                                        in0=ev[:, slot * TG: (slot + 1) * TG],
                                        in1=ev[:, slot * TG: (slot + 1) * TG],
                                        op=ALU.mult).then_inc(s_dve.h, 1)
                    s_dve.n += 1
                    nsq_dve[u] = s_dve.n
                    ev_slot_sq[slot] = s_dve.n

                    @block.tensor
                    def _(t, u=u, dvn=s_dve.n, rfree=nred_dve.get(u - 2, 0)):
                        t.wait_ge(s_dve.h, dvn)
                        t.wait_ge(s_dve.h, rfree)
                        t.matmul(p_nn2[u % 2][:], ones128[:], nsq2[u % 2][:],
                                 start=True, stop=True).then_inc(s_pe.h, 1)
                    s_pe.n += 1

                    @block.vector
                    def _(v, u=u, pen=s_pe.n):
                        v.wait_ge(s_pe.h, pen)
                        v.tensor_reduce(out=nstrip[0:1, u:u + 1],
                                        in_=p_nn2[u % 2][0:1, :], axis=AXL.X,
                                        op=ALU.max).then_inc(s_dve.h, 1)
                    s_dve.n += 1
                    nred_dve[u] = s_dve.n
                    nrm_ctr[0] += 1
                ev_slot_uses[slot] += 1
                ev_cnt[0] += 1

            load_xfm(0)
            for tg in range(NTG):
                if tg + 1 < NTG:
                    load_xfm(tg + 1)
                buf = xfm[tg % 2]
                rhs_w = TG
                for j in range(2 * NH):
                    def dst_qk(g, src_ap, dsem, j=j, tg=tg):
                        scr = qscr if j < NH else kscr
                        hh = j % NH
                        g.dma_start(out=scr[hh * 128:(hh + 1) * 128, tg * TG:(tg + 1) * TG],
                                    in_=src_ap).then_inc(dsem.h, 16)
                    qkv_tile("qk", j, tg,
                             (lambda t, mc, j=j: wqkv_sb[:, mc * 3 * VC + j * 128:
                                                         mc * 3 * VC + (j + 1) * 128]),
                             (lambda t, mc, buf=buf: buf[:, mc * TG:(mc + 1) * TG]),
                             p_qk, last_act_for_pqk,
                             Q_SCALE if j < NH else 1.0, dst_qk)
                rhs_w = 512
                vi = 0
                for tt in range(TG // 128):
                    for fh in range(VC // 512):
                        def dst_v(g, src_ap, dsem, tt=tt, fh=fh, tg=tg):
                            cc0 = (tg * TG + tt * 128) // 128
                            dst = vscr.rearrange("(h p cc) d -> p h cc d",
                                                 h=NH, p=128)
                            g.dma_start(out=dst[:, fh * 4:(fh + 1) * 4, cc0, :],
                                        in_=src_ap.rearrange("p (h d) -> p h d", d=128)
                                        ).then_inc(dsem.h, 16)
                        qkv_tile("v", vi, tg,
                                 (lambda t, mc, buf=buf, tt=tt:
                                  buf[:, mc * TG + tt * 128: mc * TG + (tt + 1) * 128]),
                                 (lambda t, mc, fh=fh:
                                  wqkv_sb[:, mc * 3 * VC + 2 * VC + fh * 512:
                                          mc * 3 * VC + 2 * VC + (fh + 1) * 512]),
                                 p_v, last_act_for_pv, 1.0, dst_v)
                        vi += 1

            with ExitStack() as esn:
                nrm = esn.enter_context(nc.sbuf_tensor("nrm", [1, 6 * NH], F32))
                p_bb = esn.enter_context(nc.psum_tensor("p_bb", [128, NH], F32))

                @block.vector
                def _(v, dvn=s_dve.n):
                    v.wait_ge(s_dve.h, dvn)
                    v.drain()
                    for h in range(NH):
                        v.tensor_reduce(out=nrm[0:1, h:h + 1],
                                        in_=nstrip[0:1, h:2 * NH * NTG:2 * NH],
                                        axis=AXL.X, op=ALU.max)
                        v.tensor_reduce(out=nrm[0:1, NH + h:NH + h + 1],
                                        in_=nstrip[0:1, NH + h:2 * NH * NTG:2 * NH],
                                        axis=AXL.X, op=ALU.max)
                    v.drain()
                    v.tensor_tensor(out=nrm[0:1, 2 * NH:3 * NH],
                                    in0=nrm[0:1, 0:NH],
                                    in1=nrm[0:1, NH:2 * NH],
                                    op=ALU.mult).then_inc(s_dve.h, 1)
                s_dve.n += 1

                @block.scalar
                def _(s, dvn=s_dve.n):
                    s.wait_ge(s_dve.h, dvn)
                    s.activation(nrm[0:1, 3 * NH:4 * NH], nrm[0:1, 2 * NH:3 * NH],
                                 AF.Ln)
                    s.drain()
                    s.activation(nrm[0:1, 4 * NH:5 * NH], nrm[0:1, 3 * NH:4 * NH],
                                 AF.Exp, scale=0.5).then_inc(s_act.h, 1)
                s_act.n += 1

                @block.vector
                def _(v, acn=s_act.n):
                    v.wait_ge(s_act.h, acn)
                    v.tensor_scalar(out=nrm[0:1, 5 * NH:6 * NH],
                                    in0=nrm[0:1, 4 * NH:5 * NH],
                                    scalar1=-1.0, scalar2=None,
                                    op0=ALU.mult).then_inc(s_dve.h, 1)
                s_dve.n += 1

                @block.tensor
                def _(t, dvn=s_dve.n):
                    t.wait_ge(s_dve.h, dvn)
                    t.matmul(p_bb[:], onesr[:], nrm[0:1, 5 * NH:6 * NH],
                             start=True, stop=True).then_inc(s_pe.h, 1)
                s_pe.n += 1

                @block.vector
                def _(v, pen=s_pe.n):
                    v.wait_ge(s_pe.h, pen)
                    v.tensor_copy(out=ubias[:, 0:NH],
                                  in_=p_bb[:]).then_inc(s_dve.h, 1)
                s_dve.n += 1
                block.flush()

        phase_barrier()

        with ExitStack() as es3:
            K_h = [es3.enter_context(nc.sbuf_tensor(f"K_h{i}", [128, S], BF16)) for i in range(2)]
            Q_h = [es3.enter_context(nc.sbuf_tensor(f"Q_h{i}", [128, S], BF16)) for i in range(2)]
            V_h = [es3.enter_context(nc.sbuf_tensor(f"V_h{i}", [128, S], BF16)) for i in range(2)]
            A_t = es3.enter_context(nc.sbuf_tensor("A_t", [128, 2 * 1024], BF16))
            r_ln = es3.enter_context(nc.sbuf_tensor("r_ln", [128, 512], F32))
            r_rec = es3.enter_context(nc.sbuf_tensor("r_rec", [128, 2 * 512], F32))
            O_t = es3.enter_context(nc.sbuf_tensor("O_t", [128, 2 * 512], BF16))
            p_S = [es3.enter_context(nc.psum_tensor(f"p_S{i}", [128, 1024], F32))
                   for i in range(3)]
            p_O = es3.enter_context(nc.psum_tensor("p_O", [128, 512], F32))
            p_rb = es3.enter_context(nc.psum_tensor("p_rb", [128, 512], F32))

            head_done_pe = {}
            exp_act = {}
            cons_pe = {}
            ep = {}
            gp_ctr = [0]
            tl_ctr = [0]

            def load_head(h):
                par = h % 2
                sem = d_k[par]

                @block.gpsimd
                def _(g, h=h, par=par, sem=sem,
                      free=head_done_pe.get(h - 2, 0),
                      devals=[(dd.h, dd.n) for dd in d_e] if h < 2 else []):
                    g.wait_ge(s_pe.h, free)
                    for hh2, vv2 in devals:
                        g.wait_ge(hh2, vv2)
                    g.dma_start(out=K_h[par][:], in_=kscr[h * 128:(h + 1) * 128, :]
                                ).then_inc(sem.h, 16)
                    g.dma_start(out=Q_h[par][:], in_=qscr[h * 128:(h + 1) * 128, :]
                                ).then_inc(sem.h, 16)
                    vsrc = vscr.rearrange("(h p cc) d -> h p cc d", h=NH, p=128)
                    vdst = V_h[par][:].rearrange("p (cc d) -> p cc d", d=128)
                    g.dma_start(out=vdst, in_=vsrc[h]).then_inc(sem.h, 16)
                sem.n += 48

            def emit_epilogue(tile):
                if tile not in ep:
                    return
                e = ep[tile]

                @block.vector
                def _(v, e=e, don=d_o[e["tp"]].n):
                    v.wait_ge(s_act.h, e["recip"])
                    v.wait_ge(d_o[e["tp"]].h, don)
                    v.tensor_tensor(out=O_t[:, e["tp"] * 512:(e["tp"] + 1) * 512],
                                    in0=p_O[:],
                                    in1=r_rec[:, e["tp"] * 512:(e["tp"] + 1) * 512],
                                    op=ALU.mult).then_inc(s_dve.h, 1)
                s_dve.n += 1
                e["mult"] = s_dve.n

                @block.gpsimd
                def _(g, e=e):
                    g.wait_ge(s_dve.h, e["mult"])
                    g.dma_start(out=ag2_i[e["h"] * 128:(e["h"] + 1) * 128,
                                          e["q0"]:e["q0"] + 512],
                                in_=O_t[:, e["tp"] * 512:(e["tp"] + 1) * 512]
                                ).then_inc(d_o[e["tp"]].h, 16)
                d_o[e["tp"]].n += 16

            load_head(0)
            if NH > 1:
                load_head(1)
            for h in range(NH):
                par = h % 2
                ksem_val = d_k[par].n
                for qt in range(NQT):
                    tile = tl_ctr[0]
                    tp = tile % 2
                    q0 = qt * QT
                    cmax = (qt + 1) * CPT
                    G = cmax // 2
                    gp0 = gp_ctr[0]

                    emit_epilogue(tile - 1)

                    def s_pair(g, h=h, par=par, q0=q0, gp0=gp0, G=G, ksem_val=ksem_val):
                        if g >= G:
                            return
                        gp = gp0 + g

                        @block.tensor
                        def _(t, g=g, gp=gp,
                              psfree=exp_act.get(gp - 3, 0)):
                            t.wait_ge(s_act.h, psfree)
                            if g == 0:
                                t.wait_ge(d_k[par].h, ksem_val)
                            for ci in range(2):
                                c = 2 * g + ci
                                last = t.matmul(p_S[gp % 3][:, ci * 512:(ci + 1) * 512],
                                                K_h[par][:, c * 128:(c + 1) * 128],
                                                Q_h[par][:, q0:q0 + 512],
                                                start=True, stop=True)
                            last.then_inc(s_pe.h, 1)
                        s_pe.n += 1
                        return s_pe.n

                    smm = {}
                    smm[0] = s_pair(0)
                    if G > 1:
                        smm[1] = s_pair(1)
                    if G > 2:
                        smm[2] = s_pair(2)
                    for g in range(G):
                        gp = gp0 + g
                        @block.scalar
                        def _(s, g=g, gp=gp, h=h, sv=smm[g],
                              afree=cons_pe.get(gp - 2, 0)):
                            s.wait_ge(s_pe.h, sv)
                            s.wait_ge(s_pe.h, afree)
                            s.activation(
                                A_t[:, (gp % 2) * 1024:(gp % 2) * 1024 + 1024],
                                p_S[gp % 3][:], AF.Exp,
                                bias=ubias[:, h:h + 1]).then_inc(s_act.h, 1)
                        s_act.n += 1
                        exp_act[gp] = s_act.n
                        gate_sem, gate_val = s_act, s_act.n
                        if 2 * g >= cmax - CPT:
                            @block.vector
                            def _(v, g=g, gp=gp, qt=qt, cmax=cmax, acn=s_act.n):
                                v.wait_ge(s_act.h, acn)
                                for ci in range(2):
                                    c = 2 * g + ci
                                    jj = c - (cmax - CPT)
                                    sl = A_t[:, (gp % 2) * 1024 + ci * 512:
                                             (gp % 2) * 1024 + (ci + 1) * 512]
                                    last = v.tensor_tensor(out=sl, in0=sl,
                                                           in1=maskt[:, MOFF - 128 * jj:
                                                                     MOFF - 128 * jj + QT],
                                                           op=ALU.mult)
                                last.then_inc(s_dve.h, 1)
                            s_dve.n += 1
                            gate_sem, gate_val = s_dve, s_dve.n
                        @block.tensor
                        def _(t, g=g, gp=gp, tile=tile, tp=tp, par=par, cmax=cmax,
                              gs=gate_sem, gv=gate_val,
                              pofree=(ep.get(tile - 1, {}).get("mult", 0)),
                              prfree=(ep.get(tile - 1, {}).get("recip", 0))):
                            t.wait_ge(gs.h, gv)
                            if g == 0:
                                t.wait_ge(s_dve.h, pofree)
                                t.wait_ge(s_act.h, prfree)
                            for ci in range(2):
                                c = 2 * g + ci
                                sl = A_t[:, (gp % 2) * 1024 + ci * 512:
                                         (gp % 2) * 1024 + (ci + 1) * 512]
                                t.matmul(p_rb[:], ones128[:], sl,
                                         start=(c == 0), stop=(c == cmax - 1))
                                last = t.matmul(p_O[:],
                                                V_h[par][:, c * 128:(c + 1) * 128],
                                                sl, start=(c == 0), stop=(c == cmax - 1))
                            last.then_inc(s_pe.h, 1)
                        s_pe.n += 1
                        cons_pe[gp] = s_pe.n
                        if g + 3 < G:
                            smm[g + 3] = s_pair(g + 3)
                    @block.scalar
                    def _(s, tp=tp, pen=s_pe.n,
                          rfree=(ep.get(tile - 2, {}).get("mult", 0))):
                        s.wait_ge(s_pe.h, pen)
                        s.wait_ge(s_dve.h, rfree)
                        s.activation(r_ln[:], p_rb[:], AF.Ln)
                        s.drain()
                        s.activation(r_rec[:, tp * 512:(tp + 1) * 512], r_ln[:],
                                     AF.Exp, scale=-1.0).then_inc(s_act.h, 1)
                    s_act.n += 1
                    ep[tile] = {"recip": s_act.n, "tp": tp, "h": h, "q0": q0}
                    gp_ctr[0] += G
                    tl_ctr[0] += 1
                head_done_pe[h] = s_pe.n
                if h + 2 < NH:
                    load_head(h + 2)
            emit_epilogue(tl_ctr[0] - 1)
            block.flush()

        @block.gpsimd
        def _(g, don0=d_o[0].n, don1=d_o[1].n):
            g.wait_ge(d_o[0].h, don0)
            g.wait_ge(d_o[1].h, don1)
            g.collective_compute("AllGather", ALU.bypass,
                                 replica_groups=[list(range(n_cores))],
                                 ins=[ag2_i.opt()],
                                 outs=[ag2_o.opt()]).then_inc(s_cc.h, 1)
        s_cc.n += 1
        CC2 = s_cc.n

        phase_barrier()

        with ExitStack() as es4:
            o_sb = es4.enter_context(nc.sbuf_tensor("o_sb", [128, 2 * NH * SH], BF16))
            wout_sb = es4.enter_context(nc.sbuf_tensor("wout_sb", [128, MC * DM], BF16))
            xh_sb = [es4.enter_context(nc.sbuf_tensor(f"xh_sb{i}", [128, DM], F32))
                     for i in range(2)]
            upre = [es4.enter_context(nc.sbuf_tensor(f"upre{i}", [128, DM], F32))
                    for i in range(2)]
            sqscr = es4.enter_context(nc.sbuf_tensor("sqscr", [128, DM], F32))
            statp = es4.enter_context(nc.sbuf_tensor("statp", [128, 8], F32))
            u_t = [es4.enter_context(nc.sbuf_tensor(f"u_t{i}", [128, DM], BF16))
                   for i in range(2)]
            p_pr = [es4.enter_context(nc.psum_tensor(f"p_pr{i}", [128, DM], F32))
                    for i in range(2)]

            @block.gpsimd
            def _(g, ccv=CC2):
                g.dma_start(out=wout_sb[:].rearrange("p (m f) -> p m f", m=MC),
                            in_=wout_in.rearrange("(m p) f -> p m f", p=128)
                            ).then_inc(d_ob.h, 16)
                g.wait_ge(s_cc.h, ccv)
                pid = g.compute_val(nc.partition_id())
                with g.register("sibr") as rs, g.register("ownb") as ro, \
                        g.register("sibb") as rb, g.register("colr") as rc:
                    g.reg_alu(rs, pid, 1, ALU.bitwise_xor)
                    g.reg_alu(ro, pid, NH * 128, ALU.mult)
                    g.reg_alu(rb, rs, NH * 128, ALU.mult)
                    g.reg_alu(rc, pid, 1, ALU.bitwise_and)
                    g.reg_alu(rc, rc, SH, ALU.mult)
                    col_rv = RuntimeValue(rc)
                    for si, rreg in ((0, ro), (1, rb)):
                        src = ag2_o[ds(RuntimeValue(rreg), NH * 128), ds(col_rv, SH)]
                        src = src.rearrange("(hh p) q -> p hh q", p=128)
                        dst = o_sb[:, si * NH * SH:(si + 1) * NH * SH]
                        dst = dst.rearrange("p (hh q) -> p hh q", hh=NH)
                        g.dma_start(out=dst, in_=src).then_inc(d_ob.h, 16)
            d_ob.n += 16 + 32
            OSB_LD = d_ob.n

            ln_evac = {}
            norm_act = {}
            for ti in range(NT16):
                tpp = ti % 2

                @block.sync
                def _(sy, ti=ti, tpp=tpp, free=ln_evac.get(ti - 2, 0)):
                    sy.wait_ge(s_dve.h, free)
                    sy.dma_start(out=xh_sb[tpp][:], in_=xh_in[ti * 128:(ti + 1) * 128, :]
                                 ).then_inc(d_xh[tpp].h, 16)
                d_xh[tpp].n += 16

                @block.tensor
                def _(t, ti=ti, tpp=tpp, obn=OSB_LD, free=ln_evac.get(ti - 2, 0)):
                    t.wait_ge(d_ob.h, obn)
                    t.wait_ge(s_dve.h, free)
                    for ng in range(NGO):
                        for i in range(2 * NH):
                            last = t.matmul(
                                p_pr[tpp][:, ng * 512:(ng + 1) * 512],
                                o_sb[:, i * SH + ti * 128: i * SH + (ti + 1) * 128],
                                wout_sb[:, i * DM + ng * 512: i * DM + (ng + 1) * 512],
                                start=(i == 0), stop=(i == 2 * NH - 1))
                    last.then_inc(s_pe.h, 1)
                s_pe.n += 1

                @block.vector
                def _(v, ti=ti, tpp=tpp, pen=s_pe.n, xn=d_xh[tpp].n,
                      nfree=norm_act.get(ti - 2, 0)):
                    v.wait_ge(s_pe.h, pen)
                    v.wait_ge(d_xh[tpp].h, xn)
                    v.wait_ge(s_dve.h, nfree)
                    v.drain()
                    v.tensor_tensor(out=upre[tpp][:], in0=p_pr[tpp][:], in1=xh_sb[tpp][:],
                                    op=ALU.add)
                    v.drain()
                    v.tensor_reduce(out=statp[:, 0:1], in_=upre[tpp][:],
                                    axis=AXL.X, op=ALU.add,
                                    negate=True).then_inc(s_dve.h, 1)
                s_dve.n += 1
                ln_evac[ti] = s_dve.n

                @block.vector
                def _(v, tpp=tpp, dvn=s_dve.n):
                    v.wait_ge(s_dve.h, dvn)
                    v.drain()
                    v.tensor_tensor(out=sqscr[:], in0=upre[tpp][:], in1=upre[tpp][:],
                                    op=ALU.mult)
                    v.drain()
                    v.tensor_reduce(out=statp[:, 1:2], in_=sqscr[:], axis=AXL.X,
                                    op=ALU.add)
                    v.drain()
                    v.tensor_scalar(out=statp[:, 2:3], in0=statp[:, 0:1],
                                    scalar1=1.0 / DM, scalar2=None, op0=ALU.mult)
                    v.tensor_scalar(out=statp[:, 3:4], in0=statp[:, 1:2],
                                    scalar1=1.0 / DM, scalar2=None, op0=ALU.mult)
                    v.drain()
                    v.tensor_tensor(out=statp[:, 4:5], in0=statp[:, 2:3],
                                    in1=statp[:, 2:3], op=ALU.mult)
                    v.drain()
                    v.tensor_tensor(out=statp[:, 5:6], in0=statp[:, 3:4],
                                    in1=statp[:, 4:5], op=ALU.subtract)
                    v.drain()
                    v.tensor_scalar(out=statp[:, 5:6], in0=statp[:, 5:6],
                                    scalar1=EPS, scalar2=None,
                                    op0=ALU.add).then_inc(s_dve.h, 1)
                s_dve.n += 1

                @block.scalar
                def _(s, dvn=s_dve.n):
                    s.wait_ge(s_dve.h, dvn)
                    s.drain()
                    s.activation(statp[:, 6:7], statp[:, 5:6], AF.Ln)
                    s.drain()
                    s.activation(statp[:, 7:8], statp[:, 6:7], AF.Exp,
                                 scale=-0.5).then_inc(s_act.h, 1)
                s_act.n += 1

                @block.vector
                def _(v, acn=s_act.n):
                    v.wait_ge(s_act.h, acn)
                    v.drain()
                    v.tensor_tensor(out=statp[:, 2:3], in0=statp[:, 2:3],
                                    in1=statp[:, 7:8],
                                    op=ALU.mult).then_inc(s_dve.h, 1)
                s_dve.n += 1

                @block.vector
                def _(v, ti=ti, tpp=tpp, dvn=s_dve.n, dun=d_u[tpp].n):
                    v.wait_ge(s_dve.h, dvn)
                    v.wait_ge(d_u[tpp].h, dun)
                    v.drain()
                    v.tensor_scalar(out=u_t[tpp][:], in0=upre[tpp][:],
                                    scalar1=statp[:, 7:8], scalar2=statp[:, 2:3],
                                    op0=ALU.mult, op1=ALU.add).then_inc(s_dve.h, 1)
                s_dve.n += 1

                norm_act[ti] = s_dve.n

                @block.gpsimd
                def _(g, ti=ti, tpp=tpp, acn=s_dve.n):
                    g.wait_ge(s_dve.h, acn)
                    g.dma_start(out=uscr[ti * 128:(ti + 1) * 128, :], in_=u_t[tpp][:]
                                ).then_inc(d_u[tpp].h, 16)
                d_u[tpp].n += 16
            block.flush()

        phase_barrier()

        with ExitStack() as es5:
            hbuf = es5.enter_context(nc.sbuf_tensor("hbuf", [128, KC * TB], BF16))
            u_fm = es5.enter_context(nc.sbuf_tensor("u_fm", [128, MC * TB], BF16))
            wu_sb = [es5.enter_context(nc.sbuf_tensor(f"wu_sb{i}", [128, MC * 128], BF16))
                     for i in range(2)]
            wg_sb = [es5.enter_context(nc.sbuf_tensor(f"wg_sb{i}", [128, MC * 128], BF16))
                     for i in range(2)]
            sgb = [es5.enter_context(nc.sbuf_tensor(f"sgb{i}", [128, TB], F32))
                   for i in range(2)]
            wd_sb = [es5.enter_context(nc.sbuf_tensor(f"wd_sb{i}", [128, 8 * 512], BF16))
                     for i in range(2)]
            u_ng = [es5.enter_context(nc.sbuf_tensor(f"u_ng{i}", [128, TSUB * 512], BF16))
                    for i in range(2)]
            res = es5.enter_context(nc.sbuf_tensor("res", [128, TSUB * DM], BF16))
            sums = es5.enter_context(nc.sbuf_tensor("sums", [128, TSUB * NGO], F32))
            ssq = es5.enter_context(nc.sbuf_tensor("ssq", [128, TSUB * NGO], F32))
            st2 = es5.enter_context(nc.sbuf_tensor("st2", [128, 4 * TSUB], F32))
            sq2 = es5.enter_context(nc.sbuf_tensor("sq2", [128, 512], F32))
            outt = [es5.enter_context(nc.sbuf_tensor(f"outt{i}", [128, DM], F32))
                    for i in range(2)]
            p_up = [es5.enter_context(nc.psum_tensor(f"p_up{i}", [128, TB], F32))
                    for i in range(2)]
            p_gt = [es5.enter_context(nc.psum_tensor(f"p_gt{i}", [128, TB], F32))
                    for i in range(2)]
            p_dn = [es5.enter_context(nc.psum_tensor(f"p_dn{i}", [128, 512], F32))
                    for i in range(TSUB)]

            UG_SLOT = 2 * JL * MC * 128
            ug_last_pe = {0: 0, 1: 0}
            wd_last_pe = {0: 0, 1: 0}
            ug_end_pe = {}
            for nb in range(NB):
                t0 = nb * TB
                @block.sync
                def _(sy, nb=nb, t0=t0,
                      pen=(ug_end_pe.get(nb - 1, 0) if nb > 0 else s_pe.n),
                      du0=d_u[0].n, du1=d_u[1].n):
                    sy.wait_ge(s_pe.h, pen)
                    sy.wait_ge(d_u[0].h, du0)
                    sy.wait_ge(d_u[1].h, du1)
                    for mc in range(MC):
                        sy.dma_start_transpose(
                            out=u_fm[:, mc * TB:(mc + 1) * TB],
                            in_=uscr[t0:t0 + TB, mc * 128:(mc + 1) * 128]
                        ).then_inc(d_uf.h, 16)
                d_uf.n += 16 * MC
                UFM_LD = d_uf.n

                hdve = {}
                ugact = {}
                wld = {}
                ACT0B = s_act.n
                DVE0B = s_dve.n

                def load_ug(j):
                    sl = j % 2

                    @block.sync
                    def _(sy, j=j, sl=sl, ccv=CC1, pefree=ug_last_pe[j % 2]):
                        sy.wait_ge(s_cc.h, ccv)
                        sy.wait_ge(s_pe.h, pefree)
                        slot = j // JL
                        jl = j % JL
                        for w_i, wbuf in ((0, wu_sb[sl]), (1, wg_sb[sl])):
                            rbase = slot * UG_SLOT + (w_i * JL + jl) * MC * 128
                            src = ag1u_o[rbase:rbase + MC * 128, :]
                            sy.dma_start(out=wbuf[:].rearrange("p (m c) -> p m c", m=MC),
                                         in_=src.rearrange("(m p) c -> p m c", p=128)
                                         ).then_inc(d_f[sl].h, 16)
                    d_f[j % 2].n += 32
                    wld[("ld", j)] = d_f[j % 2].n

                load_ug(0)
                if KC > 1:
                    load_ug(1)
                for j in range(KC):
                    sl = j % 2

                    @block.tensor
                    def _(t, j=j, sl=sl, ldn=wld[("ld", j)], un=UFM_LD,
                          pfree=ugact.get(j - 2, ACT0B), pfree2=hdve.get(j - 2, DVE0B)):
                        t.wait_ge(d_f[sl].h, ldn)
                        t.wait_ge(d_uf.h, un)
                        t.wait_ge(s_act.h, pfree)
                        t.wait_ge(s_dve.h, pfree2)
                        for mc in range(MC):
                            t.matmul(p_up[sl][:], wu_sb[sl][:, mc * 128:(mc + 1) * 128],
                                     u_fm[:, mc * TB:(mc + 1) * TB],
                                     start=(mc == 0), stop=(mc == MC - 1))
                        for mc in range(MC):
                            last = t.matmul(p_gt[sl][:], wg_sb[sl][:, mc * 128:(mc + 1) * 128],
                                            u_fm[:, mc * TB:(mc + 1) * TB],
                                            start=(mc == 0), stop=(mc == MC - 1))
                        last.then_inc(s_pe.h, 1)
                    s_pe.n += 1
                    wld[("pe", j)] = s_pe.n
                    ug_last_pe[j % 2] = s_pe.n

                    @block.scalar
                    def _(s, j=j, sl=sl, pen=s_pe.n, hfree=hdve.get(j - 2, DVE0B)):
                        s.wait_ge(s_pe.h, pen)
                        s.wait_ge(s_dve.h, hfree)
                        s.activation(sgb[sl][:], p_gt[sl][:],
                                     AF.Silu).then_inc(s_act.h, 1)
                    s_act.n += 1
                    ugact[j] = s_act.n

                    @block.vector
                    def _(v, j=j, sl=sl, acn=s_act.n):
                        v.wait_ge(s_act.h, acn)
                        v.scalar_tensor_tensor(
                            out=hbuf[:, j * TB:(j + 1) * TB], in0=p_up[sl][:],
                            scalar=1.0, in1=sgb[sl][:], op0=ALU.mult,
                            op1=ALU.mult).then_inc(s_dve.h, 1)
                    s_dve.n += 1
                    hdve[j] = s_dve.n
                    if j + 2 < KC:
                        load_ug(j + 2)
                H_DONE = s_dve.n
                ug_end_pe[nb] = s_pe.n

                KG = KC // 8
                dn_evac = {}
                dwld = {}
                ACT0D = s_act.n
                DVE0D = s_dve.n

                def load_wd(ng, kg):
                    sl = kg % 2

                    @block.sync
                    def _(sy, ng=ng, kg=kg, sl=sl, ccv=CC1, free=wd_last_pe[kg % 2]):
                        sy.wait_ge(s_cc.h, ccv)
                        sy.wait_ge(s_pe.h, free)
                        nsub = 512 // NSH if NSH < 512 else 1
                        csub = min(NSH, 512)
                        for si in range(nsub):
                            shard = (ng * 512) // NSH + si
                            rbase = shard * KC * 128 + kg * 8 * 128
                            src = ag1d_o[rbase:rbase + 8 * 128, :]
                            dstv = wd_sb[sl][:].rearrange("p (k n) -> p k n", k=8)
                            sy.dma_start(out=dstv[:, :, si * csub:(si + 1) * csub],
                                         in_=src.rearrange("(k p) n -> p k n", p=128)
                                         ).then_inc(d_d[sl].h, 16)
                    d_d[kg % 2].n += 16 * (512 // NSH if NSH < 512 else 1)
                    dwld[("ld", ng, kg)] = d_d[kg % 2].n

                for ng in range(NGO):
                    load_wd(ng, 0)
                    if KG > 1:
                        load_wd(ng, 1)
                    usl = ng % 2

                    @block.sync
                    def _(sy, ng=ng, t0=t0, usl=usl, free=dn_evac.get(ng - 2, DVE0D),
                          du0=d_u[0].n, du1=d_u[1].n):
                        sy.wait_ge(s_dve.h, free)
                        sy.wait_ge(d_u[0].h, du0)
                        sy.wait_ge(d_u[1].h, du1)
                        sy.dma_start(out=u_ng[usl][:].rearrange("p (t n) -> p t n", t=TSUB),
                                     in_=uscr[t0:t0 + TB, ng * 512:(ng + 1) * 512]
                                     .rearrange("(t p) n -> p t n", p=128)
                                     ).then_inc(d_un[usl].h, 16)
                    d_un[usl].n += 16

                    for kg in range(KG):
                        sl = kg % 2

                        @block.tensor
                        def _(t, ng=ng, kg=kg, sl=sl, ldn=dwld[("ld", ng, kg)],
                              hn=H_DONE, efree=dn_evac.get(ng - 1, DVE0D),
                              afree=dn_evac.get(("a", ng - 1), ACT0D)):
                            t.wait_ge(d_d[sl].h, ldn)
                            t.wait_ge(s_dve.h, hn)
                            if kg == 0:
                                t.wait_ge(s_dve.h, efree)
                                t.wait_ge(s_act.h, afree)
                            for k8 in range(8):
                                k = kg * 8 + k8
                                for ts in range(TSUB):
                                    last = t.matmul(
                                        p_dn[ts][:],
                                        hbuf[:, k * TB + ts * 128: k * TB + (ts + 1) * 128],
                                        wd_sb[sl][:, k8 * 512:(k8 + 1) * 512],
                                        start=(k == 0), stop=(k == KC - 1))
                            last.then_inc(s_pe.h, 1)
                        s_pe.n += 1
                        dwld[("pe", ng, kg)] = s_pe.n
                        wd_last_pe[kg % 2] = s_pe.n
                        if kg + 2 < KG:
                            load_wd(ng, kg + 2)
                    for ts in range(TSUB):
                        @block.vector
                        def _(v, ng=ng, ts=ts, usl=usl, pen=s_pe.n, dn=d_un[usl].n):
                            v.wait_ge(s_pe.h, pen)
                            v.wait_ge(d_un[usl].h, dn)
                            v.drain()
                            v.tensor_tensor(
                                out=res[:, ts * DM + ng * 512: ts * DM + (ng + 1) * 512],
                                in0=p_dn[ts][:], in1=u_ng[usl][:, ts * 512:(ts + 1) * 512],
                                op=ALU.add)
                            v.drain()
                            v.tensor_reduce(
                                out=sums[:, ts * NGO + ng: ts * NGO + ng + 1],
                                in_=res[:, ts * DM + ng * 512: ts * DM + (ng + 1) * 512],
                                axis=AXL.X, op=ALU.add).then_inc(s_dve.h, 1)
                        s_dve.n += 1

                        @block.vector
                        def _(v, ng=ng, ts=ts, dvn=s_dve.n):
                            v.drain()
                            rslice = res[:, ts * DM + ng * 512: ts * DM + (ng + 1) * 512]
                            v.tensor_tensor(out=sq2[:], in0=rslice, in1=rslice,
                                            op=ALU.mult)
                            v.drain()
                            v.tensor_reduce(
                                out=ssq[:, ts * NGO + ng: ts * NGO + ng + 1],
                                in_=sq2[:], axis=AXL.X,
                                op=ALU.add).then_inc(s_dve.h, 1)
                        s_dve.n += 1
                    dn_evac[ng] = s_dve.n
                    dn_evac[("a", ng)] = s_act.n

                @block.vector
                def _(v, dvn=s_dve.n):
                    v.wait_ge(s_dve.h, dvn)
                    v.drain()
                    for ts in range(TSUB):
                        v.tensor_reduce(out=st2[:, ts:ts + 1],
                                        in_=sums[:, ts * NGO:(ts + 1) * NGO],
                                        axis=AXL.X, op=ALU.add, negate=True)
                    for ts in range(TSUB):
                        v.tensor_reduce(out=st2[:, TSUB + ts:TSUB + ts + 1],
                                        in_=ssq[:, ts * NGO:(ts + 1) * NGO],
                                        axis=AXL.X, op=ALU.add)
                    v.drain()
                    v.tensor_scalar(out=st2[:, 0:TSUB], in0=st2[:, 0:TSUB],
                                    scalar1=1.0 / DM, scalar2=None, op0=ALU.mult)
                    v.tensor_scalar(out=st2[:, TSUB:2 * TSUB], in0=st2[:, TSUB:2 * TSUB],
                                    scalar1=1.0 / DM, scalar2=None, op0=ALU.mult)
                    v.drain()
                    v.tensor_tensor(out=st2[:, 2 * TSUB:3 * TSUB], in0=st2[:, 0:TSUB],
                                    in1=st2[:, 0:TSUB], op=ALU.mult)
                    v.drain()
                    v.tensor_tensor(out=st2[:, 2 * TSUB:3 * TSUB],
                                    in0=st2[:, TSUB:2 * TSUB],
                                    in1=st2[:, 2 * TSUB:3 * TSUB], op=ALU.subtract)
                    v.drain()
                    v.tensor_scalar(out=st2[:, 2 * TSUB:3 * TSUB],
                                    in0=st2[:, 2 * TSUB:3 * TSUB],
                                    scalar1=EPS, scalar2=None,
                                    op0=ALU.add).then_inc(s_dve.h, 1)
                s_dve.n += 1

                @block.scalar
                def _(s, dvn=s_dve.n):
                    s.wait_ge(s_dve.h, dvn)
                    s.drain()
                    s.activation(st2[:, 3 * TSUB:4 * TSUB], st2[:, 2 * TSUB:3 * TSUB],
                                 AF.Ln)
                    s.drain()
                    s.activation(st2[:, 2 * TSUB:3 * TSUB], st2[:, 3 * TSUB:4 * TSUB],
                                 AF.Exp, scale=-0.5).then_inc(s_act.h, 1)
                s_act.n += 1

                @block.vector
                def _(v, acn=s_act.n):
                    v.wait_ge(s_act.h, acn)
                    v.drain()
                    v.tensor_tensor(out=st2[:, 0:TSUB], in0=st2[:, 0:TSUB],
                                    in1=st2[:, 2 * TSUB:3 * TSUB],
                                    op=ALU.mult).then_inc(s_dve.h, 1)
                s_dve.n += 1

                for ts in range(TSUB):
                    @block.vector
                    def _(v, ts=ts, dvn=s_dve.n, don=d_out[ts % 2].n):
                        v.wait_ge(s_dve.h, dvn)
                        v.wait_ge(d_out[ts % 2].h, don)
                        v.drain()
                        v.tensor_scalar(out=outt[ts % 2][:],
                                        in0=res[:, ts * DM:(ts + 1) * DM],
                                        scalar1=st2[:, 2 * TSUB + ts:2 * TSUB + ts + 1],
                                        scalar2=st2[:, ts:ts + 1],
                                        op0=ALU.mult, op1=ALU.add).then_inc(s_dve.h, 1)
                    s_dve.n += 1

                    @block.gpsimd
                    def _(g, nb=nb, ts=ts, t0=t0, acn=s_dve.n):
                        g.wait_ge(s_dve.h, acn)
                        g.dma_start(out=out_ext[t0 + ts * 128: t0 + (ts + 1) * 128, :],
                                    in_=outt[ts % 2][:]).then_inc(d_out[ts % 2].h, 16)
                    d_out[ts % 2].n += 16
            block.flush()

        @block.gpsimd
        def _(g, don0=d_out[0].n, don1=d_out[1].n):
            g.wait_ge(d_out[0].h, don0)
            g.wait_ge(d_out[1].h, don1)
        block.flush()

    return nc


_NC_CACHE = {}


def shard_inputs(x, w_qkv, w_out, su_ug, su_dp, ws_ug, ws_dp, w_up, w_gate, w_down,
                 n_cores=8):
    B, S, DM = x.shape
    HEADS = DM // 128
    INTER = w_up.shape[0]
    NH = HEADS // 2
    VC = NH * 128
    SH = S // 2
    FSH = INTER // n_cores
    NSH = DM // n_cores
    MC, KC, NST = DM // 128, INTER // 128, NSH // 128
    wq = w_qkv[:, 0:DM]
    wk = w_qkv[:, DM:2 * DM]
    wv = w_qkv[:, 2 * DM:3 * DM]
    c = np.ascontiguousarray
    in_maps = []
    for core in range(n_cores):
        b, hs = core // 2, core % 2
        hsl = slice(hs * VC, (hs + 1) * VC)
        in_maps.append({
            "x_in": c(x[b]),
            "xh_in": c(x[b, hs * SH:(hs + 1) * SH]),
            "wqkv_in": c(np.concatenate([wq[:, hsl], wk[:, hsl], wv[:, hsl]], axis=1)),
            "wout_in": c(np.concatenate([w_out[hs * VC:(hs + 1) * VC],
                                         w_out[(1 - hs) * VC:(2 - hs) * VC]], axis=0)),
            "wup_in": c(w_up[core * FSH:(core + 1) * FSH]),
            "wgate_in": c(w_gate[core * FSH:(core + 1) * FSH]),
            "wdown_in": c(w_down[core * NSH:(core + 1) * NSH]),
            "suug_in": c(su_ug.reshape(MC, 128)),
            "sudp_in": c(su_dp.reshape(KC, 128)),
            "wsu_in": c(ws_ug[core * FSH:(core + 1) * FSH].reshape(FSH // 128, 128)),
            "wsg_in": c(ws_ug[INTER + core * FSH:INTER + (core + 1) * FSH]
                        .reshape(FSH // 128, 128)),
            "wsdp_in": c(ws_dp[core * NSH:(core + 1) * NSH].reshape(NST, 128)),
        })
    return in_maps


def kernel(L0_attn_x, L0_attn_w_qkv, L0_attn_w_out, L0_ffn_SU_ug, L0_ffn_SU_dp,
           L0_ffn_Wscale_ug, L0_ffn_Wscale_dp, L0_ffn_W_up, L0_ffn_W_gate, L0_ffn_W_down):
    x = np.asarray(L0_attn_x)
    B, S, DM = x.shape
    HEADS = DM // 128
    INTER = np.asarray(L0_ffn_W_up).shape[0]
    n_cores = 8
    NH = HEADS // 2
    VC = NH * 128
    SH = S // 2
    FSH = INTER // n_cores
    NSH = DM // n_cores
    MC, KC, NST = DM // 128, INTER // 128, NSH // 128

    key = (S, DM, HEADS, INTER)
    if key not in _NC_CACHE:
        _NC_CACHE[key] = build_nc(S, DM, HEADS, INTER, n_cores)
    nc = _NC_CACHE[key]

    in_maps = shard_inputs(
        x, np.asarray(L0_attn_w_qkv), np.asarray(L0_attn_w_out),
        np.asarray(L0_ffn_SU_ug), np.asarray(L0_ffn_SU_dp),
        np.asarray(L0_ffn_Wscale_ug), np.asarray(L0_ffn_Wscale_dp),
        np.asarray(L0_ffn_W_up), np.asarray(L0_ffn_W_gate), np.asarray(L0_ffn_W_down),
        n_cores)

    res = run_bass_kernel_spmd(nc, in_maps, core_ids=list(range(n_cores)))
    out = np.empty((B, S, DM), np.float32)
    for core in range(n_cores):
        b, hs = core // 2, core % 2
        out[b, hs * SH:(hs + 1) * SH] = res.results[core]["out_ext"]
    return out


# version 14
# speedup vs baseline: 1.0347x; 1.0162x over previous
import math
import numpy as np
import ml_dtypes
from contextlib import ExitStack

import concourse.bass as bass
import concourse.mybir as mybir
from concourse.bass_utils import run_bass_kernel_spmd
from concourse.expressions import RuntimeValue

F32 = mybir.dt.float32
BF16 = mybir.dt.bfloat16
AF = mybir.ActivationFunctionType
ALU = mybir.AluOpType
AXL = mybir.AxisListType
ds = bass.ds

EPS = 1e-5


def had(n):
    a = np.arange(n)
    bits = a[:, None] & a[None, :]
    pc = np.zeros((n, n), np.int64)
    x = bits
    while x.max() > 0:
        pc += x & 1
        x >>= 1
    return np.where(pc % 2 == 0, 1.0, -1.0).astype(np.float32)


def cgroups(total, w):
    off = 0
    while off < total:
        yield off, min(w, total - off)
        off += w


class Sem:
    def __init__(self, es, nc, name):
        self.h = es.enter_context(nc.semaphore(name))
        self.n = 0

    def inc(self, instr, k=1):
        instr.then_inc(self.h, k)
        self.n += k
        return self.n


class BufferedBlock:

    def __init__(self, real):
        self._real = real
        self._q = []

    def _add(self, eng, fn):
        if self._q and self._q[-1][0] == eng:
            self._q[-1][1].append(fn)
        else:
            self._q.append((eng, [fn]))
        return fn

    def tensor(self, fn):
        return self._add("tensor", fn)

    def scalar(self, fn):
        return self._add("scalar", fn)

    def vector(self, fn):
        return self._add("vector", fn)

    def gpsimd(self, fn):
        return self._add("gpsimd", fn)

    def sync(self, fn):
        return self._add("sync", fn)

    def flush(self):
        merged = {}
        order = []
        for eng, fns in self._q:
            if eng not in merged:
                merged[eng] = []
                order.append(eng)
            merged[eng].extend(fns)
        self._q = []
        for eng in order:
            fns = merged[eng]

            def body(e, fns=fns):
                for f in fns:
                    f(e)
            getattr(self._real, eng)(body)


def build_nc(S, DM, HEADS, INTER, n_cores=8):
    HD = 128
    assert DM == HEADS * HD and DM % 512 == 0 and INTER % 512 == 0 and S % 1024 == 0
    NH = HEADS // 2
    MC = DM // 128
    KC = INTER // 128
    FSH = INTER // n_cores
    JL = FSH // 128
    NSH = DM // n_cores
    NST = NSH // 128
    assert NST >= 1 and NSH % 128 == 0 and 512 % NSH == 0 or NSH % 512 == 0
    QT = 512
    NQT = S // QT
    CPT = QT // 128
    SH = S // 2
    TB = 512
    NB = SH // TB
    TSUB = TB // 128
    NGO = DM // 512
    VC = NH * HD
    TG = 512
    NTG = S // TG
    NSG = S // 512
    NT16 = SH // 128
    Q_SCALE = 1.0 / math.sqrt(float(HD))
    ALPHA_UG = 1.0 / math.sqrt(float(DM))
    ALPHA_DP = 1.0 / math.sqrt(float(INTER))
    MOFF = (CPT - 1) * 128
    MW = MOFF + QT

    nc = bass.Bass("TRN2", target_bir_lowering=False, debug=False)

    x_in = nc.dram_tensor("x_in", [S, DM], F32, kind="ExternalInput").ap()
    xh_in = nc.dram_tensor("xh_in", [SH, DM], F32, kind="ExternalInput").ap()
    wqkv_in = nc.dram_tensor("wqkv_in", [DM, 3 * VC], F32, kind="ExternalInput").ap()
    wout_in = nc.dram_tensor("wout_in", [DM, DM], F32, kind="ExternalInput").ap()
    wup_in = nc.dram_tensor("wup_in", [FSH, DM], F32, kind="ExternalInput").ap()
    wgate_in = nc.dram_tensor("wgate_in", [FSH, DM], F32, kind="ExternalInput").ap()
    wdown_in = nc.dram_tensor("wdown_in", [NSH, INTER], F32, kind="ExternalInput").ap()
    suug_in = nc.dram_tensor("suug_in", [MC, 128], F32, kind="ExternalInput").ap()
    sudp_in = nc.dram_tensor("sudp_in", [KC, 128], F32, kind="ExternalInput").ap()
    wsu_in = nc.dram_tensor("wsu_in", [JL, 128], F32, kind="ExternalInput").ap()
    wsg_in = nc.dram_tensor("wsg_in", [JL, 128], F32, kind="ExternalInput").ap()
    wsdp_in = nc.dram_tensor("wsdp_in", [NST, 128], F32, kind="ExternalInput").ap()
    out_ext = nc.dram_tensor("out_ext", [SH, DM], F32, kind="ExternalOutput").ap()

    h128_d = nc.inline_tensor(had(128).astype(ml_dtypes.bfloat16), name="h128c")
    identb_d = nc.inline_tensor(np.eye(128, dtype=ml_dtypes.bfloat16), name="identbc")
    identf_d = nc.inline_tensor(np.eye(128, dtype=np.float32), name="identfc")
    onesc_d = nc.inline_tensor(np.ones((128, 1), dtype=ml_dtypes.bfloat16), name="onescc")
    onesr_d = nc.inline_tensor(np.ones((1, 128), dtype=np.float32), name="onesrc")
    ones128_d = nc.inline_tensor(np.ones((128, 128), dtype=ml_dtypes.bfloat16),
                                 name="ones128c")
    mk = np.arange(128)[:, None] <= (np.arange(MW)[None, :] - MOFF)
    mask_d = nc.inline_tensor(mk.astype(ml_dtypes.bfloat16), name="maskc")

    xbf = nc.dram_tensor("xbf", [S, DM], BF16).ap()
    wqkvbf = nc.dram_tensor("wqkvbf", [DM, 3 * VC], BF16).ap()
    qscr = nc.dram_tensor("qscr", [NH * 128, S], BF16).ap()
    kscr = nc.dram_tensor("kscr", [NH * 128, S], BF16).ap()
    vscr = nc.dram_tensor("vscr", [NH * 128 * (S // 128), 128], BF16).ap()
    uscr = nc.dram_tensor("uscr", [SH, DM], BF16).ap()
    ag1u_i = nc.dram_tensor("ag1u_i", [2 * JL * MC * 128, 128], BF16).ap()
    ag1u_o = nc.dram_tensor("ag1u_o", [n_cores * 2 * JL * MC * 128, 128], BF16,
                            addr_space="Shared").ap()
    ag1d_i = nc.dram_tensor("ag1d_i", [KC * 128, NSH], BF16).ap()
    ag1d_o = nc.dram_tensor("ag1d_o", [n_cores * KC * 128, NSH], BF16,
                            addr_space="Shared").ap()
    ag2_i = nc.dram_tensor("ag2_i", [NH * 128, S], BF16).ap()
    ag2_o = nc.dram_tensor("ag2_o", [n_cores * NH * 128, S], BF16,
                            addr_space="Shared").ap()

    ES = ExitStack()
    with ES:
        h128 = ES.enter_context(nc.sbuf_tensor("h128", [128, 128], BF16))
        identb = ES.enter_context(nc.sbuf_tensor("identb", [128, 128], BF16))
        identf = ES.enter_context(nc.sbuf_tensor("identf", [128, 128], F32))
        onesc = ES.enter_context(nc.sbuf_tensor("onesc", [128, 1], BF16))
        onesr = ES.enter_context(nc.sbuf_tensor("onesr", [1, 128], F32))
        ones128 = ES.enter_context(nc.sbuf_tensor("ones128", [128, 128], BF16))
        maskt = ES.enter_context(nc.sbuf_tensor("maskt", [128, MW], BF16))
        su_ug = ES.enter_context(nc.sbuf_tensor("su_ug", [128, MC], F32))
        su_dp = ES.enter_context(nc.sbuf_tensor("su_dp", [128, KC], F32))
        ws_up = ES.enter_context(nc.sbuf_tensor("ws_up", [128, JL], F32))
        ws_gt = ES.enter_context(nc.sbuf_tensor("ws_gt", [128, JL], F32))
        ws_dp = ES.enter_context(nc.sbuf_tensor("ws_dp", [128, NST], F32))
        ubias = ES.enter_context(nc.sbuf_tensor("ubias", [128, NH], F32))

        s_pe = Sem(ES, nc, "s_pe")
        s_act = Sem(ES, nc, "s_act")
        s_dve = Sem(ES, nc, "s_dve")
        s_pool = Sem(ES, nc, "s_pool")
        s_sp = Sem(ES, nc, "s_sp")
        s_cc = Sem(ES, nc, "s_cc")
        d_gl = Sem(ES, nc, "d_gl")
        d_gl2 = [Sem(ES, nc, f"d_gl2{i}") for i in range(2)]
        d_wc = Sem(ES, nc, "d_wc")
        d_wl = Sem(ES, nc, "d_wl")
        d_xb = Sem(ES, nc, "d_xb")
        d_xb2 = Sem(ES, nc, "d_xb2")
        d_c = Sem(ES, nc, "d_c")
        d_e = [Sem(ES, nc, f"d_e{i}") for i in range(4)]
        d_x = [Sem(ES, nc, f"d_x{i}") for i in range(2)]
        d_k = [Sem(ES, nc, f"d_k{i}") for i in range(2)]
        d_o = [Sem(ES, nc, f"d_o{i}") for i in range(2)]
        d_ag = Sem(ES, nc, "d_ag")
        d_fo = [Sem(ES, nc, f"d_fo{i}") for i in range(2)]
        d_ob = Sem(ES, nc, "d_ob")
        d_xh = [Sem(ES, nc, f"d_xh{i}") for i in range(2)]
        d_u = [Sem(ES, nc, f"d_u{i}") for i in range(2)]
        d_uf = Sem(ES, nc, "d_uf")
        d_un = [Sem(ES, nc, f"d_un{i}") for i in range(2)]
        d_f = [Sem(ES, nc, f"d_f{i}") for i in range(2)]
        d_d = [Sem(ES, nc, f"d_d{i}") for i in range(2)]
        d_out = [Sem(ES, nc, f"d_out{i}") for i in range(2)]

        block = BufferedBlock(ES.enter_context(nc.Block()))
        sb_pe = Sem(ES, nc, "sb_pe")
        sb_act = Sem(ES, nc, "sb_act")
        sb_dve = Sem(ES, nc, "sb_dve")
        sb_pool = Sem(ES, nc, "sb_pool")
        sb_sp = Sem(ES, nc, "sb_sp")

        def phase_barrier():
            block.flush()
            for s in (sb_pe, sb_act, sb_dve, sb_pool, sb_sp):
                s.n += 1

            @block.tensor
            def _(t):
                t.drain().then_inc(sb_pe.h, 1)

            @block.scalar
            def _(s):
                s.drain().then_inc(sb_act.h, 1)

            @block.vector
            def _(v):
                v.drain().then_inc(sb_dve.h, 1)

            @block.gpsimd
            def _(g):
                g.drain().then_inc(sb_pool.h, 1)

            @block.sync
            def _(sy):
                sy.drain().then_inc(sb_sp.h, 1)
            vals = [(s, s.n) for s in (sb_pe, sb_act, sb_dve, sb_pool, sb_sp)]
            for eng_name, own in (("tensor", sb_pe), ("scalar", sb_act), ("vector", sb_dve),
                                  ("gpsimd", sb_pool), ("sync", sb_sp)):
                def mk(own=own):
                    def f(eng):
                        for s, v in vals:
                            if s is not own:
                                eng.wait_ge(s.h, v)
                    return f
                getattr(block, eng_name)(mk())
            block.flush()

        @block.gpsimd
        def _(g):
            g.dma_start(out=xbf[0:S // 2, :], in_=x_in[0:S // 2, :]).then_inc(d_xb.h, 16)
        d_xb.n += 16
        XBF_LD1 = d_xb.n

        bg_casts = []
        for m4 in range(0, MC, 2):
            def mk_cast(m4=m4):
                def f(g):
                    g.dma_start(out=wqkvbf[m4 * 128:(m4 + 2) * 128, :],
                                in_=wqkv_in[m4 * 128:(m4 + 2) * 128, :]
                                ).then_inc(d_wc.h, 16)
                d_wc.n += 16
                return f
            bg_casts.append(mk_cast())
        for q4 in range(2):
            def mk_cast2(q4=q4):
                def f(g):
                    lo = S // 2 + q4 * (S // 4)
                    g.dma_start(out=xbf[lo:lo + S // 4, :],
                                in_=x_in[lo:lo + S // 4, :]).then_inc(d_xb2.h, 16)
                d_xb2.n += 16
                return f
            bg_casts.append(mk_cast2())

        def drip_cast(g):
            if bg_casts:
                bg_casts.pop(0)(g)

        @block.sync
        def _(sy):
            for dst, src in ((h128, h128_d), (identb, identb_d), (identf, identf_d),
                             (onesc, onesc_d), (onesr, onesr_d),
                             (ones128, ones128_d), (maskt, mask_d)):
                sy.dma_start(out=dst[:], in_=src.ap()).then_inc(d_c.h, 16)
        d_c.n += 7 * 16

        with ExitStack() as es0:
            tvin = es0.enter_context(nc.sbuf_tensor("tvin", [128, 128], F32))
            ptv = es0.enter_context(nc.psum_tensor("ptv", [128, 128], F32))
            vecs = [(suug_in, MC, su_ug, ALPHA_UG), (sudp_in, KC, su_dp, ALPHA_DP),
                    (wsu_in, JL, ws_up, 1.0), (wsg_in, JL, ws_gt, 1.0),
                    (wsdp_in, NST, ws_dp, 1.0)]
            for vi, (src, rows, dst, scale) in enumerate(vecs):
                @block.sync
                def _(sy, src=src, rows=rows, vi=vi, dvn=s_dve.n):
                    if vi > 0:
                        sy.wait_ge(s_dve.h, dvn)
                    sy.dma_start(out=tvin[0:rows, :], in_=src).then_inc(d_c.h, 16)
                d_c.n += 16

                @block.tensor
                def _(t, rows=rows, vi=vi, dcn=d_c.n, dvn=s_dve.n):
                    t.wait_ge(d_c.h, dcn)
                    if vi > 0:
                        t.wait_ge(s_dve.h, dvn)
                    t.transpose(ptv[:, 0:rows], tvin[0:rows, :],
                                identf[0:rows, 0:rows]).then_inc(s_pe.h, 1)
                s_pe.n += 1

                @block.vector
                def _(v, rows=rows, dst=dst, scale=scale, pen=s_pe.n):
                    v.wait_ge(s_pe.h, pen)
                    v.tensor_scalar(out=dst[:, 0:rows], in0=ptv[:, 0:rows],
                                    scalar1=float(scale), scalar2=None,
                                    op0=ALU.mult).then_inc(s_dve.h, 1)
                s_dve.n += 1
            block.flush()

        phase_barrier()

        with ExitStack() as es1:
            BW = max(MC * FSH, KC * NSH)
            b1 = es1.enter_context(nc.sbuf_tensor("b1", [128, BW], F32))
            b2 = es1.enter_context(nc.sbuf_tensor("b2", [128, BW], F32))
            wtb = es1.enter_context(nc.sbuf_tensor("wtb", [128, BW], BF16))
            wrow = [es1.enter_context(nc.sbuf_tensor(f"wrow{i}", [128, max(DM, INTER)],
                                                     BF16)) for i in range(2)]
            fout = [es1.enter_context(nc.sbuf_tensor(f"fout{i}", [128, max(FSH, NSH)],
                                                     BF16)) for i in range(2)]
            ptr4 = [es1.enter_context(nc.psum_tensor(f"ptr4{i}", [128, 512], BF16))
                    for i in range(2)]
            ph1 = [es1.enter_context(nc.psum_tensor(f"ph1{i}", [128, 1024], F32))
                   for i in range(2)]

            g_rt = [0]
            tr_pe_by_rt = {}
            g_cc = [0]

            def transform(w_src, n_rows_tiles, n_chunks, chunk_w, row_w, sign_tile,
                          scale_rows_ap, ag_write):
                tr_evac = {}
                tgi = [0]
                for r in range(n_rows_tiles):
                    rsl = g_rt[0] % 2
                    wsl = wrow[rsl]
                    dg = d_gl2[rsl]

                    @block.gpsimd
                    def _(g, r=r, wsl=wsl, dg=dg, pen=tr_pe_by_rt.get(g_rt[0] - 2, 0)):
                        g.wait_ge(s_pe.h, pen)
                        g.dma_start(out=wsl[:, 0:row_w],
                                    in_=w_src[r * 128:(r + 1) * 128, :]).then_inc(dg.h, 16)
                        drip_cast(g)
                    dg.n += 16

                    if scale_rows_ap is not None:
                        @block.vector
                        def _(v, r=r, wsl=wsl, dgn=dg.n, dg=dg):
                            v.wait_ge(dg.h, dgn)
                            v.tensor_scalar(out=wsl[:, 0:row_w],
                                            in0=wsl[:, 0:row_w],
                                            scalar1=scale_rows_ap(r),
                                            scalar2=None, op0=ALU.mult
                                            ).then_inc(s_dve.h, 1)
                        s_dve.n += 1

                    nch = row_w // 128
                    for cc4 in range(0, nch, 4):
                        ng4 = min(4, nch - cc4)
                        gslot = tgi[0] % 2

                        @block.tensor
                        def _(t, r=r, wsl=wsl, dg=dg, cc4=cc4, ng4=ng4, gslot=gslot,
                              dgn=dg.n, dvn=s_dve.n, dcn=d_c.n,
                              pfree=tr_evac.get(tgi[0] - 2, 0)):
                            t.wait_ge(d_c.h, dcn)
                            if scale_rows_ap is not None:
                                t.wait_ge(s_dve.h, dvn)
                            else:
                                t.wait_ge(dg.h, dgn)
                            t.wait_ge(s_act.h, pfree)
                            for i in range(ng4):
                                cc = cc4 + i
                                last = t.transpose(ptr4[gslot][:, i * 128:(i + 1) * 128],
                                                   wsl[:, cc * 128:(cc + 1) * 128],
                                                   identb[:])
                            last.then_inc(s_pe.h, 1)
                        s_pe.n += 1

                        @block.scalar
                        def _(s, r=r, cc4=cc4, ng4=ng4, gslot=gslot, pen=s_pe.n):
                            s.wait_ge(s_pe.h, pen)
                            for i in range(ng4):
                                cc = cc4 + i
                                last = s.activation(
                                    wtb[:, cc * chunk_w + r * 128:
                                        cc * chunk_w + (r + 1) * 128],
                                    ptr4[gslot][:, i * 128:(i + 1) * 128], AF.Copy)
                            last.then_inc(s_act.h, 1)
                        s_act.n += 1
                        tr_evac[tgi[0]] = s_act.n
                        tgi[0] += 1
                    tr_pe_by_rt[g_rt[0]] = s_pe.n
                    g_rt[0] += 1
                h_evac = {}
                hgi = [0]
                for cc in range(n_chunks):
                    for off, w in cgroups(chunk_w, 1024):
                        gslot = hgi[0] % 2

                        @block.tensor
                        def _(t, cc=cc, off=off, w=w, gslot=gslot, acn=s_act.n,
                              dvn=h_evac.get(hgi[0] - 2, 0)):
                            t.wait_ge(s_act.h, acn)
                            t.wait_ge(s_dve.h, dvn)
                            for o2, w2 in cgroups(w, 512):
                                last = t.matmul(
                                    ph1[gslot][:, o2:o2 + w2], h128[:],
                                    wtb[:, cc * chunk_w + off + o2:
                                        cc * chunk_w + off + o2 + w2],
                                    start=True, stop=True)
                            last.then_inc(s_pe.h, 1)
                        s_pe.n += 1

                        @block.vector
                        def _(v, cc=cc, off=off, w=w, gslot=gslot, pen=s_pe.n):
                            v.wait_ge(s_pe.h, pen)
                            v.tensor_copy(
                                out=b1[:, cc * chunk_w + off: cc * chunk_w + off + w],
                                in_=ph1[gslot][:, 0:w]).then_inc(s_dve.h, 1)
                        s_dve.n += 1
                        h_evac[hgi[0]] = s_dve.n
                        hgi[0] += 1
                nst = int(round(math.log2(n_chunks)))
                hw = chunk_w // 2

                def bfly(eng, sem, c0, c1):
                    cur, nxt = b1, b2
                    for st in range(nst):
                        stride = 1 << st
                        cv = cur[:, 0:n_chunks * chunk_w].rearrange(
                            "p (g two s w) -> p g two s w",
                            two=2, s=stride, w=chunk_w)
                        nv = nxt[:, 0:n_chunks * chunk_w].rearrange(
                            "p (g two s w) -> p g two s w",
                            two=2, s=stride, w=chunk_w)
                        eng.tensor_tensor(out=nv[:, :, 0, :, c0:c1],
                                          in0=cv[:, :, 0, :, c0:c1],
                                          in1=cv[:, :, 1, :, c0:c1],
                                          op=ALU.add)
                        eng.tensor_tensor(out=nv[:, :, 1, :, c0:c1],
                                          in0=cv[:, :, 0, :, c0:c1],
                                          in1=cv[:, :, 1, :, c0:c1],
                                          op=ALU.subtract)
                        eng.drain()
                        cur, nxt = nxt, cur
                    eng.drain().then_inc(sem, 1)

                @block.vector
                def _(v, dvn=s_dve.n, pon=s_pool.n):
                    v.wait_ge(s_dve.h, dvn)
                    v.wait_ge(s_pool.h, pon)
                    bfly(v, s_dve.h, 0, hw)
                s_dve.n += 1

                @block.gpsimd
                def _(g, dvn=s_dve.n - 1, pon=s_pool.n):
                    g.wait_ge(s_dve.h, dvn)
                    g.wait_ge(s_pool.h, pon)
                    bfly(g, s_pool.h, hw, chunk_w)
                s_pool.n += 1
                cur = b1 if nst % 2 == 0 else b2
                nxt = b2 if nst % 2 == 0 else b1
                for cc in range(n_chunks):
                    fsl = g_cc[0] % 2
                    fo = fout[fsl]

                    @block.vector
                    def _(v, cc=cc, cur=cur, fo=fo, fsl=fsl, dvn=s_dve.n,
                          pon=s_pool.n, dan=d_fo[fsl].n):
                        v.wait_ge(s_dve.h, dvn)
                        v.wait_ge(s_pool.h, pon)
                        v.wait_ge(d_fo[fsl].h, dan)
                        v.tensor_scalar(
                            out=fo[:, 0:chunk_w], in0=cur[:, cc * chunk_w:(cc + 1) * chunk_w],
                            scalar1=sign_tile[:, cc:cc + 1], scalar2=None,
                            op0=ALU.mult).then_inc(s_dve.h, 1)
                    s_dve.n += 1

                    @block.gpsimd
                    def _(g, cc=cc, fo=fo, fsl=fsl, dvn=s_dve.n):
                        g.wait_ge(s_dve.h, dvn)
                        ag_write(g, cc, fo, d_fo[fsl].h)
                    d_fo[fsl].n += 16
                    g_cc[0] += 1

            for w_i, (w_src, wsrows, base) in enumerate(
                    ((wup_in, ws_up, 0), (wgate_in, ws_gt, JL * MC * 128))):
                def agw_ug(g, mc, fo, semh, base=base):
                    dst = ag1u_i[base:base + JL * MC * 128, :]
                    dst = dst.rearrange("(jl m p) c -> p jl m c", jl=JL, m=MC)
                    g.dma_start(out=dst[:, :, mc, :],
                                in_=fo[:, 0:FSH].rearrange("p (jl c) -> p jl c", jl=JL)
                                ).then_inc(semh, 16)
                transform(w_src, JL, MC, FSH, DM, su_ug,
                          (lambda r, wsrows=wsrows: wsrows[:, r:r + 1]), agw_ug)
            def agw_dn(g, cc, fo, semh):
                g.dma_start(out=ag1d_i[cc * 128:(cc + 1) * 128, :],
                            in_=fo[:, 0:NSH]).then_inc(semh, 16)
            transform(wdown_in, NST, KC, NSH, INTER, su_dp,
                      (lambda r: ws_dp[:, r:r + 1]), agw_dn)

            if bg_casts:
                @block.gpsimd
                def _(g):
                    while bg_casts:
                        bg_casts.pop(0)(g)

            block.flush()

        @block.gpsimd
        def _(g, f0=d_fo[0].n, f1=d_fo[1].n):
            g.wait_ge(d_fo[0].h, f0)
            g.wait_ge(d_fo[1].h, f1)
            g.collective_compute("AllGather", ALU.bypass,
                                 replica_groups=[list(range(n_cores))],
                                 ins=[ag1u_i.opt()],
                                 outs=[ag1u_o.opt()]).then_inc(s_cc.h, 1)
            g.collective_compute("AllGather", ALU.bypass,
                                 replica_groups=[list(range(n_cores))],
                                 ins=[ag1d_i.opt()],
                                 outs=[ag1d_o.opt()]).then_inc(s_cc.h, 1)
        s_cc.n += 2
        CC1 = s_cc.n
        XBF_LD2 = d_xb2.n

        phase_barrier()

        with ExitStack() as es2:
            wqkv_sb = es2.enter_context(nc.sbuf_tensor("wqkv_sb", [128, MC * 3 * VC], BF16))
            xfm = [es2.enter_context(nc.sbuf_tensor(f"xfm{i}", [128, MC * TG], BF16))
                   for i in range(2)]
            ev = es2.enter_context(nc.sbuf_tensor("ev", [128, 4 * TG], BF16))
            p_qk = [es2.enter_context(nc.psum_tensor(f"p_qk{i}", [128, TG], F32))
                    for i in range(2)]
            p_v = [es2.enter_context(nc.psum_tensor(f"p_v{i}", [128, 512], F32))
                   for i in range(2)]

            nstrip = es2.enter_context(nc.sbuf_tensor("nstrip", [1, 2 * NH * NTG], F32))
            nsq2 = [es2.enter_context(nc.sbuf_tensor(f"nsq2{i}", [128, TG], BF16))
                    for i in range(3)]
            p_nn2 = [es2.enter_context(nc.psum_tensor(f"p_nn2{i}", [128, 512], F32))
                     for i in range(3)]
            nrm_ctr = [0]
            nsq_dve = {}
            nred_dve = {}

            @block.sync
            def _(sy, wcn=d_wc.n):
                sy.wait_ge(d_wc.h, wcn)
                src = wqkvbf.rearrange("(m p) f -> p m f", p=128)
                sy.dma_start(out=wqkv_sb[:].rearrange("p (m f) -> p m f", m=MC),
                             in_=src).then_inc(d_wl.h, 16)
            d_wl.n += 16
            WQKV_LD = d_wl.n

            ev_cnt = [0]
            ev_slot_uses = [0, 0, 0, 0]
            ev_slot_sq = {}
            last_act_for_pqk = {}
            last_act_for_pv = {}

            def load_xfm(tg):
                buf = xfm[tg % 2]
                sem = d_x[tg % 2]

                @block.sync
                def _(sy, tg=tg, buf=buf, sem=sem, pen=s_pe.n):
                    sy.wait_ge(s_pe.h, pen)
                    if (tg + 1) * TG <= S // 2:
                        sy.wait_ge(d_xb.h, XBF_LD1)
                    else:
                        sy.wait_ge(d_xb2.h, XBF_LD2)
                    for mc in range(MC):
                        sy.dma_start_transpose(
                            out=buf[:, mc * TG:(mc + 1) * TG],
                            in_=xbf[tg * TG:(tg + 1) * TG, mc * 128:(mc + 1) * 128]
                        ).then_inc(sem.h, 16)
                sem.n += 16 * MC

            def qkv_tile(kind, idx, tg, lhs_fn, rhs_fn, psl, last_act_map, scale, dst_fn):
                p = psl[idx % 2]

                @block.tensor
                def _(t, p=p, kind=kind, idx=idx, tg=tg,
                      xn=d_x[tg % 2].n, wq=WQKV_LD,
                      pfree=last_act_map.get(idx % 2, 0)):
                    t.wait_ge(d_x[tg % 2].h, xn)
                    t.wait_ge(d_wl.h, wq)
                    t.wait_ge(s_act.h, pfree)
                    for mc in range(MC):
                        last = t.matmul(p[:, 0:rhs_w], lhs_fn(t, mc), rhs_fn(t, mc),
                                        start=(mc == 0), stop=(mc == MC - 1))
                    last.then_inc(s_pe.h, 1)
                s_pe.n += 1

                slot = ev_cnt[0] % 4

                @block.scalar
                def _(s, p=p, slot=slot, pen=s_pe.n, su=ev_slot_uses[slot],
                      sqfree=ev_slot_sq.get(slot, None)):
                    s.wait_ge(s_pe.h, pen)
                    s.wait_ge(d_e[slot].h, su * 16)
                    if sqfree is not None:
                        s.wait_ge(s_dve.h, sqfree)
                    s.activation(ev[:, slot * TG: slot * TG + rhs_w],
                                 p[:, 0:rhs_w], AF.Copy, scale=scale
                                 ).then_inc(s_act.h, 1)
                s_act.n += 1
                last_act_map[idx % 2] = s_act.n

                @block.sync
                def _(sy, slot=slot, acn=s_act.n):
                    sy.wait_ge(s_act.h, acn)
                    dst_fn(sy, ev[:, slot * TG: slot * TG + rhs_w], d_e[slot])
                d_e[slot].n += 16
                if kind == "qk":
                    u = nrm_ctr[0]

                    @block.vector
                    def _(v, slot=slot, u=u, acn=s_act.n,
                          sqf=nsq_dve.get(u - 3, 0), pef=s_pe.n):
                        v.wait_ge(s_act.h, acn)
                        v.wait_ge(s_pe.h, pef)
                        v.tensor_tensor(out=nsq2[u % 3][:],
                                        in0=ev[:, slot * TG: (slot + 1) * TG],
                                        in1=ev[:, slot * TG: (slot + 1) * TG],
                                        op=ALU.mult).then_inc(s_dve.h, 1)
                    s_dve.n += 1
                    nsq_dve[u] = s_dve.n
                    ev_slot_sq[slot] = s_dve.n

                    @block.tensor
                    def _(t, u=u, dvn=s_dve.n, rfree=nred_dve.get(u - 3, 0)):
                        t.wait_ge(s_dve.h, dvn)
                        t.wait_ge(s_dve.h, rfree)
                        t.matmul(p_nn2[u % 3][:], ones128[:], nsq2[u % 3][:],
                                 start=True, stop=True).then_inc(s_pe.h, 1)
                    s_pe.n += 1

                    @block.vector
                    def _(v, u=u, pen=s_pe.n):
                        v.wait_ge(s_pe.h, pen)
                        v.tensor_reduce(out=nstrip[0:1, u:u + 1],
                                        in_=p_nn2[u % 3][0:1, :], axis=AXL.X,
                                        op=ALU.max).then_inc(s_dve.h, 1)
                    s_dve.n += 1
                    nred_dve[u] = s_dve.n
                    nrm_ctr[0] += 1
                ev_slot_uses[slot] += 1
                ev_cnt[0] += 1

            load_xfm(0)
            for tg in range(NTG):
                if tg + 1 < NTG:
                    load_xfm(tg + 1)
                buf = xfm[tg % 2]
                rhs_w = TG
                for j in range(2 * NH):
                    def dst_qk(g, src_ap, dsem, j=j, tg=tg):
                        scr = qscr if j < NH else kscr
                        hh = j % NH
                        g.dma_start(out=scr[hh * 128:(hh + 1) * 128, tg * TG:(tg + 1) * TG],
                                    in_=src_ap).then_inc(dsem.h, 16)
                    qkv_tile("qk", j, tg,
                             (lambda t, mc, j=j: wqkv_sb[:, mc * 3 * VC + j * 128:
                                                         mc * 3 * VC + (j + 1) * 128]),
                             (lambda t, mc, buf=buf: buf[:, mc * TG:(mc + 1) * TG]),
                             p_qk, last_act_for_pqk,
                             Q_SCALE if j < NH else 1.0, dst_qk)
                rhs_w = 512
                vi = 0
                for tt in range(TG // 128):
                    for fh in range(VC // 512):
                        def dst_v(g, src_ap, dsem, tt=tt, fh=fh, tg=tg):
                            cc0 = (tg * TG + tt * 128) // 128
                            dst = vscr.rearrange("(h p cc) d -> p h cc d",
                                                 h=NH, p=128)
                            g.dma_start(out=dst[:, fh * 4:(fh + 1) * 4, cc0, :],
                                        in_=src_ap.rearrange("p (h d) -> p h d", d=128)
                                        ).then_inc(dsem.h, 16)
                        qkv_tile("v", vi, tg,
                                 (lambda t, mc, buf=buf, tt=tt:
                                  buf[:, mc * TG + tt * 128: mc * TG + (tt + 1) * 128]),
                                 (lambda t, mc, fh=fh:
                                  wqkv_sb[:, mc * 3 * VC + 2 * VC + fh * 512:
                                          mc * 3 * VC + 2 * VC + (fh + 1) * 512]),
                                 p_v, last_act_for_pv, 1.0, dst_v)
                        vi += 1

            with ExitStack() as esn:
                nrm = esn.enter_context(nc.sbuf_tensor("nrm", [1, 6 * NH], F32))
                p_bb = esn.enter_context(nc.psum_tensor("p_bb", [128, NH], F32))

                @block.vector
                def _(v, dvn=s_dve.n):
                    v.wait_ge(s_dve.h, dvn)
                    v.drain()
                    for h in range(NH):
                        v.tensor_reduce(out=nrm[0:1, h:h + 1],
                                        in_=nstrip[0:1, h:2 * NH * NTG:2 * NH],
                                        axis=AXL.X, op=ALU.max)
                        v.tensor_reduce(out=nrm[0:1, NH + h:NH + h + 1],
                                        in_=nstrip[0:1, NH + h:2 * NH * NTG:2 * NH],
                                        axis=AXL.X, op=ALU.max)
                    v.drain()
                    v.tensor_tensor(out=nrm[0:1, 2 * NH:3 * NH],
                                    in0=nrm[0:1, 0:NH],
                                    in1=nrm[0:1, NH:2 * NH],
                                    op=ALU.mult).then_inc(s_dve.h, 1)
                s_dve.n += 1

                @block.scalar
                def _(s, dvn=s_dve.n):
                    s.wait_ge(s_dve.h, dvn)
                    s.activation(nrm[0:1, 3 * NH:4 * NH], nrm[0:1, 2 * NH:3 * NH],
                                 AF.Ln)
                    s.drain()
                    s.activation(nrm[0:1, 4 * NH:5 * NH], nrm[0:1, 3 * NH:4 * NH],
                                 AF.Exp, scale=0.5).then_inc(s_act.h, 1)
                s_act.n += 1

                @block.vector
                def _(v, acn=s_act.n):
                    v.wait_ge(s_act.h, acn)
                    v.tensor_scalar(out=nrm[0:1, 5 * NH:6 * NH],
                                    in0=nrm[0:1, 4 * NH:5 * NH],
                                    scalar1=-1.0, scalar2=None,
                                    op0=ALU.mult).then_inc(s_dve.h, 1)
                s_dve.n += 1

                @block.tensor
                def _(t, dvn=s_dve.n):
                    t.wait_ge(s_dve.h, dvn)
                    t.matmul(p_bb[:], onesr[:], nrm[0:1, 5 * NH:6 * NH],
                             start=True, stop=True).then_inc(s_pe.h, 1)
                s_pe.n += 1

                @block.vector
                def _(v, pen=s_pe.n):
                    v.wait_ge(s_pe.h, pen)
                    v.tensor_copy(out=ubias[:, 0:NH],
                                  in_=p_bb[:]).then_inc(s_dve.h, 1)
                s_dve.n += 1
                block.flush()

        phase_barrier()

        with ExitStack() as es3:
            K_h = [es3.enter_context(nc.sbuf_tensor(f"K_h{i}", [128, S], BF16)) for i in range(2)]
            Q_h = [es3.enter_context(nc.sbuf_tensor(f"Q_h{i}", [128, S], BF16)) for i in range(2)]
            V_h = [es3.enter_context(nc.sbuf_tensor(f"V_h{i}", [128, S], BF16)) for i in range(2)]
            A_t = es3.enter_context(nc.sbuf_tensor("A_t", [128, 2 * 1024], BF16))
            r_ln = es3.enter_context(nc.sbuf_tensor("r_ln", [128, 512], F32))
            r_rec = es3.enter_context(nc.sbuf_tensor("r_rec", [128, 2 * 512], F32))
            O_t = es3.enter_context(nc.sbuf_tensor("O_t", [128, 2 * 512], BF16))
            p_S = [es3.enter_context(nc.psum_tensor(f"p_S{i}", [128, 1024], F32))
                   for i in range(3)]
            p_O = es3.enter_context(nc.psum_tensor("p_O", [128, 512], F32))
            p_rb = es3.enter_context(nc.psum_tensor("p_rb", [128, 512], F32))

            head_done_pe = {}
            exp_act = {}
            cons_pe = {}
            ep = {}
            gp_ctr = [0]
            tl_ctr = [0]

            def load_head(h):
                par = h % 2
                sem = d_k[par]

                @block.gpsimd
                def _(g, h=h, par=par, sem=sem,
                      free=head_done_pe.get(h - 2, 0),
                      devals=[(dd.h, dd.n) for dd in d_e] if h < 2 else []):
                    g.wait_ge(s_pe.h, free)
                    for hh2, vv2 in devals:
                        g.wait_ge(hh2, vv2)
                    g.dma_start(out=K_h[par][:], in_=kscr[h * 128:(h + 1) * 128, :]
                                ).then_inc(sem.h, 16)
                    g.dma_start(out=Q_h[par][:], in_=qscr[h * 128:(h + 1) * 128, :]
                                ).then_inc(sem.h, 16)
                    vsrc = vscr.rearrange("(h p cc) d -> h p cc d", h=NH, p=128)
                    vdst = V_h[par][:].rearrange("p (cc d) -> p cc d", d=128)
                    g.dma_start(out=vdst, in_=vsrc[h]).then_inc(sem.h, 16)
                sem.n += 48

            def emit_epilogue(tile):
                if tile not in ep:
                    return
                e = ep[tile]

                @block.vector
                def _(v, e=e, don=d_o[e["tp"]].n):
                    v.wait_ge(s_act.h, e["recip"])
                    v.wait_ge(d_o[e["tp"]].h, don)
                    v.tensor_tensor(out=O_t[:, e["tp"] * 512:(e["tp"] + 1) * 512],
                                    in0=p_O[:],
                                    in1=r_rec[:, e["tp"] * 512:(e["tp"] + 1) * 512],
                                    op=ALU.mult).then_inc(s_dve.h, 1)
                s_dve.n += 1
                e["mult"] = s_dve.n

                @block.gpsimd
                def _(g, e=e):
                    g.wait_ge(s_dve.h, e["mult"])
                    g.dma_start(out=ag2_i[e["h"] * 128:(e["h"] + 1) * 128,
                                          e["q0"]:e["q0"] + 512],
                                in_=O_t[:, e["tp"] * 512:(e["tp"] + 1) * 512]
                                ).then_inc(d_o[e["tp"]].h, 16)
                d_o[e["tp"]].n += 16

            load_head(0)
            if NH > 1:
                load_head(1)
            for h in range(NH):
                par = h % 2
                ksem_val = d_k[par].n
                for qt in range(NQT):
                    tile = tl_ctr[0]
                    tp = tile % 2
                    q0 = qt * QT
                    cmax = (qt + 1) * CPT
                    G = cmax // 2
                    gp0 = gp_ctr[0]

                    emit_epilogue(tile - 1)

                    def s_pair(g, h=h, par=par, q0=q0, gp0=gp0, G=G, ksem_val=ksem_val):
                        if g >= G:
                            return
                        gp = gp0 + g

                        @block.tensor
                        def _(t, g=g, gp=gp,
                              psfree=exp_act.get(gp - 3, 0)):
                            t.wait_ge(s_act.h, psfree)
                            if g == 0:
                                t.wait_ge(d_k[par].h, ksem_val)
                            for ci in range(2):
                                c = 2 * g + ci
                                last = t.matmul(p_S[gp % 3][:, ci * 512:(ci + 1) * 512],
                                                K_h[par][:, c * 128:(c + 1) * 128],
                                                Q_h[par][:, q0:q0 + 512],
                                                start=True, stop=True)
                            last.then_inc(s_pe.h, 1)
                        s_pe.n += 1
                        return s_pe.n

                    smm = {}
                    smm[0] = s_pair(0)
                    if G > 1:
                        smm[1] = s_pair(1)
                    if G > 2:
                        smm[2] = s_pair(2)
                    for g in range(G):
                        gp = gp0 + g
                        @block.scalar
                        def _(s, g=g, gp=gp, h=h, sv=smm[g],
                              afree=cons_pe.get(gp - 2, 0)):
                            s.wait_ge(s_pe.h, sv)
                            s.wait_ge(s_pe.h, afree)
                            s.activation(
                                A_t[:, (gp % 2) * 1024:(gp % 2) * 1024 + 1024],
                                p_S[gp % 3][:], AF.Exp,
                                bias=ubias[:, h:h + 1]).then_inc(s_act.h, 1)
                        s_act.n += 1
                        exp_act[gp] = s_act.n
                        gate_sem, gate_val = s_act, s_act.n
                        if 2 * g >= cmax - CPT:
                            @block.vector
                            def _(v, g=g, gp=gp, qt=qt, cmax=cmax, acn=s_act.n):
                                v.wait_ge(s_act.h, acn)
                                for ci in range(2):
                                    c = 2 * g + ci
                                    jj = c - (cmax - CPT)
                                    sl = A_t[:, (gp % 2) * 1024 + ci * 512:
                                             (gp % 2) * 1024 + (ci + 1) * 512]
                                    last = v.tensor_tensor(out=sl, in0=sl,
                                                           in1=maskt[:, MOFF - 128 * jj:
                                                                     MOFF - 128 * jj + QT],
                                                           op=ALU.mult)
                                last.then_inc(s_dve.h, 1)
                            s_dve.n += 1
                            gate_sem, gate_val = s_dve, s_dve.n
                        @block.tensor
                        def _(t, g=g, gp=gp, tile=tile, tp=tp, par=par, cmax=cmax,
                              gs=gate_sem, gv=gate_val,
                              pofree=(ep.get(tile - 1, {}).get("mult", 0)),
                              prfree=(ep.get(tile - 1, {}).get("recip", 0))):
                            t.wait_ge(gs.h, gv)
                            if g == 0:
                                t.wait_ge(s_dve.h, pofree)
                                t.wait_ge(s_act.h, prfree)
                            for ci in range(2):
                                c = 2 * g + ci
                                sl = A_t[:, (gp % 2) * 1024 + ci * 512:
                                         (gp % 2) * 1024 + (ci + 1) * 512]
                                t.matmul(p_rb[:], ones128[:], sl,
                                         start=(c == 0), stop=(c == cmax - 1))
                                last = t.matmul(p_O[:],
                                                V_h[par][:, c * 128:(c + 1) * 128],
                                                sl, start=(c == 0), stop=(c == cmax - 1))
                            last.then_inc(s_pe.h, 1)
                        s_pe.n += 1
                        cons_pe[gp] = s_pe.n
                        if g + 3 < G:
                            smm[g + 3] = s_pair(g + 3)
                    @block.scalar
                    def _(s, tp=tp, pen=s_pe.n,
                          rfree=(ep.get(tile - 2, {}).get("mult", 0))):
                        s.wait_ge(s_pe.h, pen)
                        s.wait_ge(s_dve.h, rfree)
                        s.activation(r_ln[:], p_rb[:], AF.Ln)
                        s.drain()
                        s.activation(r_rec[:, tp * 512:(tp + 1) * 512], r_ln[:],
                                     AF.Exp, scale=-1.0).then_inc(s_act.h, 1)
                    s_act.n += 1
                    ep[tile] = {"recip": s_act.n, "tp": tp, "h": h, "q0": q0}
                    gp_ctr[0] += G
                    tl_ctr[0] += 1
                head_done_pe[h] = s_pe.n
                if h + 2 < NH:
                    load_head(h + 2)
            emit_epilogue(tl_ctr[0] - 1)
            block.flush()

        @block.gpsimd
        def _(g, don0=d_o[0].n, don1=d_o[1].n):
            g.wait_ge(d_o[0].h, don0)
            g.wait_ge(d_o[1].h, don1)
            g.collective_compute("AllGather", ALU.bypass,
                                 replica_groups=[list(range(n_cores))],
                                 ins=[ag2_i.opt()],
                                 outs=[ag2_o.opt()]).then_inc(s_cc.h, 1)
        s_cc.n += 1
        CC2 = s_cc.n

        phase_barrier()

        with ExitStack() as es4:
            o_sb = es4.enter_context(nc.sbuf_tensor("o_sb", [128, 2 * NH * SH], BF16))
            wout_sb = es4.enter_context(nc.sbuf_tensor("wout_sb", [128, MC * DM], BF16))
            xh_sb = [es4.enter_context(nc.sbuf_tensor(f"xh_sb{i}", [128, DM], F32))
                     for i in range(2)]
            upre = [es4.enter_context(nc.sbuf_tensor(f"upre{i}", [128, DM], F32))
                    for i in range(2)]
            sqscr = es4.enter_context(nc.sbuf_tensor("sqscr", [128, DM], F32))
            statp = es4.enter_context(nc.sbuf_tensor("statp", [128, 8], F32))
            u_t = [es4.enter_context(nc.sbuf_tensor(f"u_t{i}", [128, DM], BF16))
                   for i in range(2)]
            p_pr = [es4.enter_context(nc.psum_tensor(f"p_pr{i}", [128, DM], F32))
                    for i in range(2)]

            @block.gpsimd
            def _(g, ccv=CC2):
                g.dma_start(out=wout_sb[:].rearrange("p (m f) -> p m f", m=MC),
                            in_=wout_in.rearrange("(m p) f -> p m f", p=128)
                            ).then_inc(d_ob.h, 16)
                g.wait_ge(s_cc.h, ccv)
                pid = g.compute_val(nc.partition_id())
                with g.register("sibr") as rs, g.register("ownb") as ro, \
                        g.register("sibb") as rb, g.register("colr") as rc:
                    g.reg_alu(rs, pid, 1, ALU.bitwise_xor)
                    g.reg_alu(ro, pid, NH * 128, ALU.mult)
                    g.reg_alu(rb, rs, NH * 128, ALU.mult)
                    g.reg_alu(rc, pid, 1, ALU.bitwise_and)
                    g.reg_alu(rc, rc, SH, ALU.mult)
                    col_rv = RuntimeValue(rc)
                    for si, rreg in ((0, ro), (1, rb)):
                        src = ag2_o[ds(RuntimeValue(rreg), NH * 128), ds(col_rv, SH)]
                        src = src.rearrange("(hh p) q -> p hh q", p=128)
                        dst = o_sb[:, si * NH * SH:(si + 1) * NH * SH]
                        dst = dst.rearrange("p (hh q) -> p hh q", hh=NH)
                        g.dma_start(out=dst, in_=src).then_inc(d_ob.h, 16)
            d_ob.n += 16 + 32
            OSB_LD = d_ob.n

            ln_evac = {}
            norm_act = {}
            for ti in range(NT16):
                tpp = ti % 2

                @block.sync
                def _(sy, ti=ti, tpp=tpp, free=ln_evac.get(ti - 2, 0)):
                    sy.wait_ge(s_dve.h, free)
                    sy.dma_start(out=xh_sb[tpp][:], in_=xh_in[ti * 128:(ti + 1) * 128, :]
                                 ).then_inc(d_xh[tpp].h, 16)
                d_xh[tpp].n += 16

                @block.tensor
                def _(t, ti=ti, tpp=tpp, obn=OSB_LD, free=ln_evac.get(ti - 2, 0)):
                    t.wait_ge(d_ob.h, obn)
                    t.wait_ge(s_dve.h, free)
                    for ng in range(NGO):
                        for i in range(2 * NH):
                            last = t.matmul(
                                p_pr[tpp][:, ng * 512:(ng + 1) * 512],
                                o_sb[:, i * SH + ti * 128: i * SH + (ti + 1) * 128],
                                wout_sb[:, i * DM + ng * 512: i * DM + (ng + 1) * 512],
                                start=(i == 0), stop=(i == 2 * NH - 1))
                    last.then_inc(s_pe.h, 1)
                s_pe.n += 1

                @block.vector
                def _(v, ti=ti, tpp=tpp, pen=s_pe.n, xn=d_xh[tpp].n,
                      nfree=norm_act.get(ti - 2, 0)):
                    v.wait_ge(s_pe.h, pen)
                    v.wait_ge(d_xh[tpp].h, xn)
                    v.wait_ge(s_dve.h, nfree)
                    v.drain()
                    v.tensor_tensor(out=upre[tpp][:], in0=p_pr[tpp][:], in1=xh_sb[tpp][:],
                                    op=ALU.add)
                    v.drain()
                    v.tensor_reduce(out=statp[:, 0:1], in_=upre[tpp][:],
                                    axis=AXL.X, op=ALU.add,
                                    negate=True).then_inc(s_dve.h, 1)
                s_dve.n += 1
                ln_evac[ti] = s_dve.n

                @block.vector
                def _(v, tpp=tpp, dvn=s_dve.n):
                    v.wait_ge(s_dve.h, dvn)
                    v.drain()
                    v.tensor_tensor(out=sqscr[:], in0=upre[tpp][:], in1=upre[tpp][:],
                                    op=ALU.mult)
                    v.drain()
                    v.tensor_reduce(out=statp[:, 1:2], in_=sqscr[:], axis=AXL.X,
                                    op=ALU.add)
                    v.drain()
                    v.tensor_scalar(out=statp[:, 2:3], in0=statp[:, 0:1],
                                    scalar1=1.0 / DM, scalar2=None, op0=ALU.mult)
                    v.tensor_scalar(out=statp[:, 3:4], in0=statp[:, 1:2],
                                    scalar1=1.0 / DM, scalar2=None, op0=ALU.mult)
                    v.drain()
                    v.tensor_tensor(out=statp[:, 4:5], in0=statp[:, 2:3],
                                    in1=statp[:, 2:3], op=ALU.mult)
                    v.drain()
                    v.tensor_tensor(out=statp[:, 5:6], in0=statp[:, 3:4],
                                    in1=statp[:, 4:5], op=ALU.subtract)
                    v.drain()
                    v.tensor_scalar(out=statp[:, 5:6], in0=statp[:, 5:6],
                                    scalar1=EPS, scalar2=None,
                                    op0=ALU.add).then_inc(s_dve.h, 1)
                s_dve.n += 1

                @block.scalar
                def _(s, dvn=s_dve.n):
                    s.wait_ge(s_dve.h, dvn)
                    s.drain()
                    s.activation(statp[:, 6:7], statp[:, 5:6], AF.Ln)
                    s.drain()
                    s.activation(statp[:, 7:8], statp[:, 6:7], AF.Exp,
                                 scale=-0.5).then_inc(s_act.h, 1)
                s_act.n += 1

                @block.vector
                def _(v, acn=s_act.n):
                    v.wait_ge(s_act.h, acn)
                    v.drain()
                    v.tensor_tensor(out=statp[:, 2:3], in0=statp[:, 2:3],
                                    in1=statp[:, 7:8],
                                    op=ALU.mult).then_inc(s_dve.h, 1)
                s_dve.n += 1

                @block.vector
                def _(v, ti=ti, tpp=tpp, dvn=s_dve.n, dun=d_u[tpp].n):
                    v.wait_ge(s_dve.h, dvn)
                    v.wait_ge(d_u[tpp].h, dun)
                    v.drain()
                    v.tensor_scalar(out=u_t[tpp][:], in0=upre[tpp][:],
                                    scalar1=statp[:, 7:8], scalar2=statp[:, 2:3],
                                    op0=ALU.mult, op1=ALU.add).then_inc(s_dve.h, 1)
                s_dve.n += 1

                norm_act[ti] = s_dve.n

                @block.gpsimd
                def _(g, ti=ti, tpp=tpp, acn=s_dve.n):
                    g.wait_ge(s_dve.h, acn)
                    g.dma_start(out=uscr[ti * 128:(ti + 1) * 128, :], in_=u_t[tpp][:]
                                ).then_inc(d_u[tpp].h, 16)
                d_u[tpp].n += 16
            block.flush()

        phase_barrier()

        with ExitStack() as es5:
            hbuf = es5.enter_context(nc.sbuf_tensor("hbuf", [128, KC * TB], BF16))
            u_fm = es5.enter_context(nc.sbuf_tensor("u_fm", [128, MC * TB], BF16))
            wu_sb = [es5.enter_context(nc.sbuf_tensor(f"wu_sb{i}", [128, MC * 128], BF16))
                     for i in range(2)]
            wg_sb = [es5.enter_context(nc.sbuf_tensor(f"wg_sb{i}", [128, MC * 128], BF16))
                     for i in range(2)]
            sgb = [es5.enter_context(nc.sbuf_tensor(f"sgb{i}", [128, TB], F32))
                   for i in range(2)]
            wd_sb = [es5.enter_context(nc.sbuf_tensor(f"wd_sb{i}", [128, 8 * 512], BF16))
                     for i in range(2)]
            u_ng = [es5.enter_context(nc.sbuf_tensor(f"u_ng{i}", [128, TSUB * 512], BF16))
                    for i in range(2)]
            res = es5.enter_context(nc.sbuf_tensor("res", [128, TSUB * DM], BF16))
            sums = es5.enter_context(nc.sbuf_tensor("sums", [128, TSUB * NGO], F32))
            ssq = es5.enter_context(nc.sbuf_tensor("ssq", [128, TSUB * NGO], F32))
            st2 = es5.enter_context(nc.sbuf_tensor("st2", [128, 4 * TSUB], F32))
            sq2 = es5.enter_context(nc.sbuf_tensor("sq2", [128, 512], F32))
            outt = [es5.enter_context(nc.sbuf_tensor(f"outt{i}", [128, DM], F32))
                    for i in range(2)]
            p_up = [es5.enter_context(nc.psum_tensor(f"p_up{i}", [128, TB], F32))
                    for i in range(2)]
            p_gt = [es5.enter_context(nc.psum_tensor(f"p_gt{i}", [128, TB], F32))
                    for i in range(2)]
            p_dn = [es5.enter_context(nc.psum_tensor(f"p_dn{i}", [128, 512], F32))
                    for i in range(TSUB)]

            UG_SLOT = 2 * JL * MC * 128
            ug_last_pe = {0: 0, 1: 0}
            wd_last_pe = {0: 0, 1: 0}
            ug_end_pe = {}
            for nb in range(NB):
                t0 = nb * TB
                @block.sync
                def _(sy, nb=nb, t0=t0,
                      pen=(ug_end_pe.get(nb - 1, 0) if nb > 0 else s_pe.n),
                      du0=d_u[0].n, du1=d_u[1].n):
                    sy.wait_ge(s_pe.h, pen)
                    sy.wait_ge(d_u[0].h, du0)
                    sy.wait_ge(d_u[1].h, du1)
                    for mc in range(MC):
                        sy.dma_start_transpose(
                            out=u_fm[:, mc * TB:(mc + 1) * TB],
                            in_=uscr[t0:t0 + TB, mc * 128:(mc + 1) * 128]
                        ).then_inc(d_uf.h, 16)
                d_uf.n += 16 * MC
                UFM_LD = d_uf.n

                hdve = {}
                ugact = {}
                wld = {}
                ACT0B = s_act.n
                DVE0B = s_dve.n

                def load_ug(j):
                    sl = j % 2

                    @block.sync
                    def _(sy, j=j, sl=sl, ccv=CC1, pefree=ug_last_pe[j % 2]):
                        sy.wait_ge(s_cc.h, ccv)
                        sy.wait_ge(s_pe.h, pefree)
                        slot = j // JL
                        jl = j % JL
                        for w_i, wbuf in ((0, wu_sb[sl]), (1, wg_sb[sl])):
                            rbase = slot * UG_SLOT + (w_i * JL + jl) * MC * 128
                            src = ag1u_o[rbase:rbase + MC * 128, :]
                            sy.dma_start(out=wbuf[:].rearrange("p (m c) -> p m c", m=MC),
                                         in_=src.rearrange("(m p) c -> p m c", p=128)
                                         ).then_inc(d_f[sl].h, 16)
                    d_f[j % 2].n += 32
                    wld[("ld", j)] = d_f[j % 2].n

                load_ug(0)
                if KC > 1:
                    load_ug(1)
                for j in range(KC):
                    sl = j % 2

                    @block.tensor
                    def _(t, j=j, sl=sl, ldn=wld[("ld", j)], un=UFM_LD,
                          pfree=ugact.get(j - 2, ACT0B), pfree2=hdve.get(j - 2, DVE0B)):
                        t.wait_ge(d_f[sl].h, ldn)
                        t.wait_ge(d_uf.h, un)
                        t.wait_ge(s_act.h, pfree)
                        t.wait_ge(s_dve.h, pfree2)
                        for mc in range(MC):
                            t.matmul(p_up[sl][:], wu_sb[sl][:, mc * 128:(mc + 1) * 128],
                                     u_fm[:, mc * TB:(mc + 1) * TB],
                                     start=(mc == 0), stop=(mc == MC - 1))
                        for mc in range(MC):
                            last = t.matmul(p_gt[sl][:], wg_sb[sl][:, mc * 128:(mc + 1) * 128],
                                            u_fm[:, mc * TB:(mc + 1) * TB],
                                            start=(mc == 0), stop=(mc == MC - 1))
                        last.then_inc(s_pe.h, 1)
                    s_pe.n += 1
                    wld[("pe", j)] = s_pe.n
                    ug_last_pe[j % 2] = s_pe.n

                    @block.scalar
                    def _(s, j=j, sl=sl, pen=s_pe.n, hfree=hdve.get(j - 2, DVE0B)):
                        s.wait_ge(s_pe.h, pen)
                        s.wait_ge(s_dve.h, hfree)
                        s.activation(sgb[sl][:], p_gt[sl][:],
                                     AF.Silu).then_inc(s_act.h, 1)
                    s_act.n += 1
                    ugact[j] = s_act.n

                    @block.vector
                    def _(v, j=j, sl=sl, acn=s_act.n):
                        v.wait_ge(s_act.h, acn)
                        v.scalar_tensor_tensor(
                            out=hbuf[:, j * TB:(j + 1) * TB], in0=p_up[sl][:],
                            scalar=1.0, in1=sgb[sl][:], op0=ALU.mult,
                            op1=ALU.mult).then_inc(s_dve.h, 1)
                    s_dve.n += 1
                    hdve[j] = s_dve.n
                    if j + 2 < KC:
                        load_ug(j + 2)
                H_DONE = s_dve.n
                ug_end_pe[nb] = s_pe.n

                KG = KC // 8
                dn_evac = {}
                dwld = {}
                ACT0D = s_act.n
                DVE0D = s_dve.n

                def load_wd(ng, kg):
                    sl = kg % 2

                    @block.sync
                    def _(sy, ng=ng, kg=kg, sl=sl, ccv=CC1, free=wd_last_pe[kg % 2]):
                        sy.wait_ge(s_cc.h, ccv)
                        sy.wait_ge(s_pe.h, free)
                        nsub = 512 // NSH if NSH < 512 else 1
                        csub = min(NSH, 512)
                        for si in range(nsub):
                            shard = (ng * 512) // NSH + si
                            rbase = shard * KC * 128 + kg * 8 * 128
                            src = ag1d_o[rbase:rbase + 8 * 128, :]
                            dstv = wd_sb[sl][:].rearrange("p (k n) -> p k n", k=8)
                            sy.dma_start(out=dstv[:, :, si * csub:(si + 1) * csub],
                                         in_=src.rearrange("(k p) n -> p k n", p=128)
                                         ).then_inc(d_d[sl].h, 16)
                    d_d[kg % 2].n += 16 * (512 // NSH if NSH < 512 else 1)
                    dwld[("ld", ng, kg)] = d_d[kg % 2].n

                for ng in range(NGO):
                    load_wd(ng, 0)
                    if KG > 1:
                        load_wd(ng, 1)
                    usl = ng % 2

                    @block.sync
                    def _(sy, ng=ng, t0=t0, usl=usl, free=dn_evac.get(ng - 2, DVE0D),
                          du0=d_u[0].n, du1=d_u[1].n):
                        sy.wait_ge(s_dve.h, free)
                        sy.wait_ge(d_u[0].h, du0)
                        sy.wait_ge(d_u[1].h, du1)
                        sy.dma_start(out=u_ng[usl][:].rearrange("p (t n) -> p t n", t=TSUB),
                                     in_=uscr[t0:t0 + TB, ng * 512:(ng + 1) * 512]
                                     .rearrange("(t p) n -> p t n", p=128)
                                     ).then_inc(d_un[usl].h, 16)
                    d_un[usl].n += 16

                    for kg in range(KG):
                        sl = kg % 2

                        @block.tensor
                        def _(t, ng=ng, kg=kg, sl=sl, ldn=dwld[("ld", ng, kg)],
                              hn=H_DONE, efree=dn_evac.get(ng - 1, DVE0D),
                              afree=dn_evac.get(("a", ng - 1), ACT0D)):
                            t.wait_ge(d_d[sl].h, ldn)
                            t.wait_ge(s_dve.h, hn)
                            if kg == 0:
                                t.wait_ge(s_dve.h, efree)
                                t.wait_ge(s_act.h, afree)
                            for k8 in range(8):
                                k = kg * 8 + k8
                                for ts in range(TSUB):
                                    last = t.matmul(
                                        p_dn[ts][:],
                                        hbuf[:, k * TB + ts * 128: k * TB + (ts + 1) * 128],
                                        wd_sb[sl][:, k8 * 512:(k8 + 1) * 512],
                                        start=(k == 0), stop=(k == KC - 1))
                            last.then_inc(s_pe.h, 1)
                        s_pe.n += 1
                        dwld[("pe", ng, kg)] = s_pe.n
                        wd_last_pe[kg % 2] = s_pe.n
                        if kg + 2 < KG:
                            load_wd(ng, kg + 2)
                    for ts in range(TSUB):
                        @block.vector
                        def _(v, ng=ng, ts=ts, usl=usl, pen=s_pe.n, dn=d_un[usl].n):
                            v.wait_ge(s_pe.h, pen)
                            v.wait_ge(d_un[usl].h, dn)
                            v.drain()
                            v.tensor_tensor(
                                out=res[:, ts * DM + ng * 512: ts * DM + (ng + 1) * 512],
                                in0=p_dn[ts][:], in1=u_ng[usl][:, ts * 512:(ts + 1) * 512],
                                op=ALU.add)
                            v.drain()
                            v.tensor_reduce(
                                out=sums[:, ts * NGO + ng: ts * NGO + ng + 1],
                                in_=res[:, ts * DM + ng * 512: ts * DM + (ng + 1) * 512],
                                axis=AXL.X, op=ALU.add).then_inc(s_dve.h, 1)
                        s_dve.n += 1

                        @block.vector
                        def _(v, ng=ng, ts=ts, dvn=s_dve.n):
                            v.drain()
                            rslice = res[:, ts * DM + ng * 512: ts * DM + (ng + 1) * 512]
                            v.tensor_tensor(out=sq2[:], in0=rslice, in1=rslice,
                                            op=ALU.mult)
                            v.drain()
                            v.tensor_reduce(
                                out=ssq[:, ts * NGO + ng: ts * NGO + ng + 1],
                                in_=sq2[:], axis=AXL.X,
                                op=ALU.add).then_inc(s_dve.h, 1)
                        s_dve.n += 1
                    dn_evac[ng] = s_dve.n
                    dn_evac[("a", ng)] = s_act.n

                @block.vector
                def _(v, dvn=s_dve.n):
                    v.wait_ge(s_dve.h, dvn)
                    v.drain()
                    for ts in range(TSUB):
                        v.tensor_reduce(out=st2[:, ts:ts + 1],
                                        in_=sums[:, ts * NGO:(ts + 1) * NGO],
                                        axis=AXL.X, op=ALU.add, negate=True)
                    for ts in range(TSUB):
                        v.tensor_reduce(out=st2[:, TSUB + ts:TSUB + ts + 1],
                                        in_=ssq[:, ts * NGO:(ts + 1) * NGO],
                                        axis=AXL.X, op=ALU.add)
                    v.drain()
                    v.tensor_scalar(out=st2[:, 0:TSUB], in0=st2[:, 0:TSUB],
                                    scalar1=1.0 / DM, scalar2=None, op0=ALU.mult)
                    v.tensor_scalar(out=st2[:, TSUB:2 * TSUB], in0=st2[:, TSUB:2 * TSUB],
                                    scalar1=1.0 / DM, scalar2=None, op0=ALU.mult)
                    v.drain()
                    v.tensor_tensor(out=st2[:, 2 * TSUB:3 * TSUB], in0=st2[:, 0:TSUB],
                                    in1=st2[:, 0:TSUB], op=ALU.mult)
                    v.drain()
                    v.tensor_tensor(out=st2[:, 2 * TSUB:3 * TSUB],
                                    in0=st2[:, TSUB:2 * TSUB],
                                    in1=st2[:, 2 * TSUB:3 * TSUB], op=ALU.subtract)
                    v.drain()
                    v.tensor_scalar(out=st2[:, 2 * TSUB:3 * TSUB],
                                    in0=st2[:, 2 * TSUB:3 * TSUB],
                                    scalar1=EPS, scalar2=None,
                                    op0=ALU.add).then_inc(s_dve.h, 1)
                s_dve.n += 1

                @block.scalar
                def _(s, dvn=s_dve.n):
                    s.wait_ge(s_dve.h, dvn)
                    s.drain()
                    s.activation(st2[:, 3 * TSUB:4 * TSUB], st2[:, 2 * TSUB:3 * TSUB],
                                 AF.Ln)
                    s.drain()
                    s.activation(st2[:, 2 * TSUB:3 * TSUB], st2[:, 3 * TSUB:4 * TSUB],
                                 AF.Exp, scale=-0.5).then_inc(s_act.h, 1)
                s_act.n += 1

                @block.vector
                def _(v, acn=s_act.n):
                    v.wait_ge(s_act.h, acn)
                    v.drain()
                    v.tensor_tensor(out=st2[:, 0:TSUB], in0=st2[:, 0:TSUB],
                                    in1=st2[:, 2 * TSUB:3 * TSUB],
                                    op=ALU.mult).then_inc(s_dve.h, 1)
                s_dve.n += 1

                for ts in range(TSUB):
                    @block.vector
                    def _(v, ts=ts, dvn=s_dve.n, don=d_out[ts % 2].n):
                        v.wait_ge(s_dve.h, dvn)
                        v.wait_ge(d_out[ts % 2].h, don)
                        v.drain()
                        v.tensor_scalar(out=outt[ts % 2][:],
                                        in0=res[:, ts * DM:(ts + 1) * DM],
                                        scalar1=st2[:, 2 * TSUB + ts:2 * TSUB + ts + 1],
                                        scalar2=st2[:, ts:ts + 1],
                                        op0=ALU.mult, op1=ALU.add).then_inc(s_dve.h, 1)
                    s_dve.n += 1

                    @block.gpsimd
                    def _(g, nb=nb, ts=ts, t0=t0, acn=s_dve.n):
                        g.wait_ge(s_dve.h, acn)
                        g.dma_start(out=out_ext[t0 + ts * 128: t0 + (ts + 1) * 128, :],
                                    in_=outt[ts % 2][:]).then_inc(d_out[ts % 2].h, 16)
                    d_out[ts % 2].n += 16
            block.flush()

        @block.gpsimd
        def _(g, don0=d_out[0].n, don1=d_out[1].n):
            g.wait_ge(d_out[0].h, don0)
            g.wait_ge(d_out[1].h, don1)
        block.flush()

    return nc


_NC_CACHE = {}


def shard_inputs(x, w_qkv, w_out, su_ug, su_dp, ws_ug, ws_dp, w_up, w_gate, w_down,
                 n_cores=8):
    B, S, DM = x.shape
    HEADS = DM // 128
    INTER = w_up.shape[0]
    NH = HEADS // 2
    VC = NH * 128
    SH = S // 2
    FSH = INTER // n_cores
    NSH = DM // n_cores
    MC, KC, NST = DM // 128, INTER // 128, NSH // 128
    wq = w_qkv[:, 0:DM]
    wk = w_qkv[:, DM:2 * DM]
    wv = w_qkv[:, 2 * DM:3 * DM]
    c = np.ascontiguousarray
    in_maps = []
    for core in range(n_cores):
        b, hs = core // 2, core % 2
        hsl = slice(hs * VC, (hs + 1) * VC)
        in_maps.append({
            "x_in": c(x[b]),
            "xh_in": c(x[b, hs * SH:(hs + 1) * SH]),
            "wqkv_in": c(np.concatenate([wq[:, hsl], wk[:, hsl], wv[:, hsl]], axis=1)),
            "wout_in": c(np.concatenate([w_out[hs * VC:(hs + 1) * VC],
                                         w_out[(1 - hs) * VC:(2 - hs) * VC]], axis=0)),
            "wup_in": c(w_up[core * FSH:(core + 1) * FSH]),
            "wgate_in": c(w_gate[core * FSH:(core + 1) * FSH]),
            "wdown_in": c(w_down[core * NSH:(core + 1) * NSH]),
            "suug_in": c(su_ug.reshape(MC, 128)),
            "sudp_in": c(su_dp.reshape(KC, 128)),
            "wsu_in": c(ws_ug[core * FSH:(core + 1) * FSH].reshape(FSH // 128, 128)),
            "wsg_in": c(ws_ug[INTER + core * FSH:INTER + (core + 1) * FSH]
                        .reshape(FSH // 128, 128)),
            "wsdp_in": c(ws_dp[core * NSH:(core + 1) * NSH].reshape(NST, 128)),
        })
    return in_maps


def kernel(L0_attn_x, L0_attn_w_qkv, L0_attn_w_out, L0_ffn_SU_ug, L0_ffn_SU_dp,
           L0_ffn_Wscale_ug, L0_ffn_Wscale_dp, L0_ffn_W_up, L0_ffn_W_gate, L0_ffn_W_down):
    x = np.asarray(L0_attn_x)
    B, S, DM = x.shape
    HEADS = DM // 128
    INTER = np.asarray(L0_ffn_W_up).shape[0]
    n_cores = 8
    NH = HEADS // 2
    VC = NH * 128
    SH = S // 2
    FSH = INTER // n_cores
    NSH = DM // n_cores
    MC, KC, NST = DM // 128, INTER // 128, NSH // 128

    key = (S, DM, HEADS, INTER)
    if key not in _NC_CACHE:
        _NC_CACHE[key] = build_nc(S, DM, HEADS, INTER, n_cores)
    nc = _NC_CACHE[key]

    in_maps = shard_inputs(
        x, np.asarray(L0_attn_w_qkv), np.asarray(L0_attn_w_out),
        np.asarray(L0_ffn_SU_ug), np.asarray(L0_ffn_SU_dp),
        np.asarray(L0_ffn_Wscale_ug), np.asarray(L0_ffn_Wscale_dp),
        np.asarray(L0_ffn_W_up), np.asarray(L0_ffn_W_gate), np.asarray(L0_ffn_W_down),
        n_cores)

    res = run_bass_kernel_spmd(nc, in_maps, core_ids=list(range(n_cores)))
    out = np.empty((B, S, DM), np.float32)
    for core in range(n_cores):
        b, hs = core // 2, core % 2
        out[b, hs * SH:(hs + 1) * SH] = res.results[core]["out_ext"]
    return out


# version 15
# speedup vs baseline: 1.0485x; 1.0133x over previous
import math
import numpy as np
import ml_dtypes
from contextlib import ExitStack

import concourse.bass as bass
import concourse.mybir as mybir
from concourse.bass_utils import run_bass_kernel_spmd
from concourse.expressions import RuntimeValue

F32 = mybir.dt.float32
BF16 = mybir.dt.bfloat16
AF = mybir.ActivationFunctionType
ALU = mybir.AluOpType
AXL = mybir.AxisListType
ds = bass.ds

EPS = 1e-5


def had(n):
    a = np.arange(n)
    bits = a[:, None] & a[None, :]
    pc = np.zeros((n, n), np.int64)
    x = bits
    while x.max() > 0:
        pc += x & 1
        x >>= 1
    return np.where(pc % 2 == 0, 1.0, -1.0).astype(np.float32)


def cgroups(total, w):
    off = 0
    while off < total:
        yield off, min(w, total - off)
        off += w


class Sem:
    def __init__(self, es, nc, name):
        self.h = es.enter_context(nc.semaphore(name))
        self.n = 0

    def inc(self, instr, k=1):
        instr.then_inc(self.h, k)
        self.n += k
        return self.n


class BufferedBlock:

    def __init__(self, real):
        self._real = real
        self._q = []

    def _add(self, eng, fn):
        if self._q and self._q[-1][0] == eng:
            self._q[-1][1].append(fn)
        else:
            self._q.append((eng, [fn]))
        return fn

    def tensor(self, fn):
        return self._add("tensor", fn)

    def scalar(self, fn):
        return self._add("scalar", fn)

    def vector(self, fn):
        return self._add("vector", fn)

    def gpsimd(self, fn):
        return self._add("gpsimd", fn)

    def sync(self, fn):
        return self._add("sync", fn)

    def flush(self):
        merged = {}
        order = []
        for eng, fns in self._q:
            if eng not in merged:
                merged[eng] = []
                order.append(eng)
            merged[eng].extend(fns)
        self._q = []
        for eng in order:
            fns = merged[eng]

            def body(e, fns=fns):
                for f in fns:
                    f(e)
            getattr(self._real, eng)(body)


def build_nc(S, DM, HEADS, INTER, n_cores=8):
    HD = 128
    assert DM == HEADS * HD and DM % 512 == 0 and INTER % 512 == 0 and S % 1024 == 0
    NH = HEADS // 2
    MC = DM // 128
    KC = INTER // 128
    FSH = INTER // n_cores
    JL = FSH // 128
    NSH = DM // n_cores
    NST = NSH // 128
    assert NST >= 1 and NSH % 128 == 0 and 512 % NSH == 0 or NSH % 512 == 0
    QT = 512
    NQT = S // QT
    CPT = QT // 128
    SH = S // 2
    TB = 512
    NB = SH // TB
    TSUB = TB // 128
    NGO = DM // 512
    VC = NH * HD
    TG = 512
    NTG = S // TG
    NSG = S // 512
    NT16 = SH // 128
    Q_SCALE = 1.0 / math.sqrt(float(HD))
    ALPHA_UG = 1.0 / math.sqrt(float(DM))
    ALPHA_DP = 1.0 / math.sqrt(float(INTER))
    MOFF = (CPT - 1) * 128
    MW = MOFF + QT

    nc = bass.Bass("TRN2", target_bir_lowering=False, debug=False)

    x_in = nc.dram_tensor("x_in", [S, DM], F32, kind="ExternalInput").ap()
    xh_in = nc.dram_tensor("xh_in", [SH, DM], F32, kind="ExternalInput").ap()
    wqkv_in = nc.dram_tensor("wqkv_in", [DM, 3 * VC], F32, kind="ExternalInput").ap()
    wout_in = nc.dram_tensor("wout_in", [DM, DM], F32, kind="ExternalInput").ap()
    wup_in = nc.dram_tensor("wup_in", [FSH, DM], F32, kind="ExternalInput").ap()
    wgate_in = nc.dram_tensor("wgate_in", [FSH, DM], F32, kind="ExternalInput").ap()
    wdown_in = nc.dram_tensor("wdown_in", [NSH, INTER], F32, kind="ExternalInput").ap()
    suug_in = nc.dram_tensor("suug_in", [MC, 128], F32, kind="ExternalInput").ap()
    sudp_in = nc.dram_tensor("sudp_in", [KC, 128], F32, kind="ExternalInput").ap()
    wsu_in = nc.dram_tensor("wsu_in", [JL, 128], F32, kind="ExternalInput").ap()
    wsg_in = nc.dram_tensor("wsg_in", [JL, 128], F32, kind="ExternalInput").ap()
    wsdp_in = nc.dram_tensor("wsdp_in", [NST, 128], F32, kind="ExternalInput").ap()
    out_ext = nc.dram_tensor("out_ext", [SH, DM], F32, kind="ExternalOutput").ap()

    h128_d = nc.inline_tensor(had(128).astype(ml_dtypes.bfloat16), name="h128c")
    identb_d = nc.inline_tensor(np.eye(128, dtype=ml_dtypes.bfloat16), name="identbc")
    identf_d = nc.inline_tensor(np.eye(128, dtype=np.float32), name="identfc")
    onesc_d = nc.inline_tensor(np.ones((128, 1), dtype=ml_dtypes.bfloat16), name="onescc")
    onesr_d = nc.inline_tensor(np.ones((1, 128), dtype=np.float32), name="onesrc")
    ones128_d = nc.inline_tensor(np.ones((128, 128), dtype=ml_dtypes.bfloat16),
                                 name="ones128c")
    mk = np.arange(128)[:, None] <= (np.arange(MW)[None, :] - MOFF)
    mask_d = nc.inline_tensor(mk.astype(ml_dtypes.bfloat16), name="maskc")

    xbf = nc.dram_tensor("xbf", [S, DM], BF16).ap()
    wqkvbf = nc.dram_tensor("wqkvbf", [DM, 3 * VC], BF16).ap()
    qscr = nc.dram_tensor("qscr", [NH * 128, S], BF16).ap()
    kscr = nc.dram_tensor("kscr", [NH * 128, S], BF16).ap()
    vscr = nc.dram_tensor("vscr", [NH * 128 * (S // 128), 128], BF16).ap()
    uscr = nc.dram_tensor("uscr", [SH, DM], BF16).ap()
    ag1u_i = nc.dram_tensor("ag1u_i", [2 * JL * MC * 128, 128], BF16).ap()
    ag1u_o = nc.dram_tensor("ag1u_o", [n_cores * 2 * JL * MC * 128, 128], BF16,
                            addr_space="Shared").ap()
    ag1d_i = nc.dram_tensor("ag1d_i", [KC * 128, NSH], BF16).ap()
    ag1d_o = nc.dram_tensor("ag1d_o", [n_cores * KC * 128, NSH], BF16,
                            addr_space="Shared").ap()
    ag2_i = nc.dram_tensor("ag2_i", [NH * 128, S], BF16).ap()
    HH = NH // 2
    ag2_oa = nc.dram_tensor("ag2_oa", [n_cores * HH * 128, S], BF16,
                            addr_space="Shared").ap()
    ag2_ob = nc.dram_tensor("ag2_ob", [n_cores * HH * 128, S], BF16,
                            addr_space="Shared").ap()

    ES = ExitStack()
    with ES:
        h128 = ES.enter_context(nc.sbuf_tensor("h128", [128, 128], BF16))
        identb = ES.enter_context(nc.sbuf_tensor("identb", [128, 128], BF16))
        identf = ES.enter_context(nc.sbuf_tensor("identf", [128, 128], F32))
        onesc = ES.enter_context(nc.sbuf_tensor("onesc", [128, 1], BF16))
        onesr = ES.enter_context(nc.sbuf_tensor("onesr", [1, 128], F32))
        ones128 = ES.enter_context(nc.sbuf_tensor("ones128", [128, 128], BF16))
        maskt = ES.enter_context(nc.sbuf_tensor("maskt", [128, MW], BF16))
        su_ug = ES.enter_context(nc.sbuf_tensor("su_ug", [128, MC], F32))
        su_dp = ES.enter_context(nc.sbuf_tensor("su_dp", [128, KC], F32))
        ws_up = ES.enter_context(nc.sbuf_tensor("ws_up", [128, JL], F32))
        ws_gt = ES.enter_context(nc.sbuf_tensor("ws_gt", [128, JL], F32))
        ws_dp = ES.enter_context(nc.sbuf_tensor("ws_dp", [128, NST], F32))
        ubias = ES.enter_context(nc.sbuf_tensor("ubias", [128, NH], F32))

        s_pe = Sem(ES, nc, "s_pe")
        s_act = Sem(ES, nc, "s_act")
        s_dve = Sem(ES, nc, "s_dve")
        s_pool = Sem(ES, nc, "s_pool")
        s_sp = Sem(ES, nc, "s_sp")
        s_cc = Sem(ES, nc, "s_cc")
        d_gl = Sem(ES, nc, "d_gl")
        d_gl2 = [Sem(ES, nc, f"d_gl2{i}") for i in range(2)]
        d_wc = Sem(ES, nc, "d_wc")
        d_wl = [Sem(ES, nc, f"d_wl{i}") for i in range(2)]
        d_xb = Sem(ES, nc, "d_xb")
        d_xb2 = Sem(ES, nc, "d_xb2")
        d_c = Sem(ES, nc, "d_c")
        d_e = [Sem(ES, nc, f"d_e{i}") for i in range(4)]
        d_x = [Sem(ES, nc, f"d_x{i}") for i in range(2)]
        d_k = [Sem(ES, nc, f"d_k{i}") for i in range(2)]
        d_o = [Sem(ES, nc, f"d_o{i}") for i in range(2)]
        d_ag = Sem(ES, nc, "d_ag")
        d_fo = [Sem(ES, nc, f"d_fo{i}") for i in range(2)]
        d_ob = Sem(ES, nc, "d_ob")
        d_ob2 = Sem(ES, nc, "d_ob2")
        d_xh = [Sem(ES, nc, f"d_xh{i}") for i in range(2)]
        d_u = [Sem(ES, nc, f"d_u{i}") for i in range(2)]
        d_uf = Sem(ES, nc, "d_uf")
        d_un = [Sem(ES, nc, f"d_un{i}") for i in range(2)]
        d_f = [Sem(ES, nc, f"d_f{i}") for i in range(2)]
        d_d = [Sem(ES, nc, f"d_d{i}") for i in range(2)]
        d_out = [Sem(ES, nc, f"d_out{i}") for i in range(2)]

        block = BufferedBlock(ES.enter_context(nc.Block()))
        sb_pe = Sem(ES, nc, "sb_pe")
        sb_act = Sem(ES, nc, "sb_act")
        sb_dve = Sem(ES, nc, "sb_dve")
        sb_pool = Sem(ES, nc, "sb_pool")
        sb_sp = Sem(ES, nc, "sb_sp")

        def phase_barrier():
            block.flush()
            for s in (sb_pe, sb_act, sb_dve, sb_pool, sb_sp):
                s.n += 1

            @block.tensor
            def _(t):
                t.drain().then_inc(sb_pe.h, 1)

            @block.scalar
            def _(s):
                s.drain().then_inc(sb_act.h, 1)

            @block.vector
            def _(v):
                v.drain().then_inc(sb_dve.h, 1)

            @block.gpsimd
            def _(g):
                g.drain().then_inc(sb_pool.h, 1)

            @block.sync
            def _(sy):
                sy.drain().then_inc(sb_sp.h, 1)
            vals = [(s, s.n) for s in (sb_pe, sb_act, sb_dve, sb_pool, sb_sp)]
            for eng_name, own in (("tensor", sb_pe), ("scalar", sb_act), ("vector", sb_dve),
                                  ("gpsimd", sb_pool), ("sync", sb_sp)):
                def mk(own=own):
                    def f(eng):
                        for s, v in vals:
                            if s is not own:
                                eng.wait_ge(s.h, v)
                    return f
                getattr(block, eng_name)(mk())
            block.flush()

        @block.gpsimd
        def _(g):
            g.dma_start(out=xbf[0:S // 2, :], in_=x_in[0:S // 2, :]).then_inc(d_xb.h, 16)
        d_xb.n += 16
        XBF_LD1 = d_xb.n

        bg_casts = []
        for m4 in range(0, MC, 2):
            def mk_cast(m4=m4):
                def f(g):
                    g.dma_start(out=wqkvbf[m4 * 128:(m4 + 2) * 128, :],
                                in_=wqkv_in[m4 * 128:(m4 + 2) * 128, :]
                                ).then_inc(d_wc.h, 16)
                d_wc.n += 16
                return f
            bg_casts.append(mk_cast())
        for q4 in range(2):
            def mk_cast2(q4=q4):
                def f(g):
                    lo = S // 2 + q4 * (S // 4)
                    g.dma_start(out=xbf[lo:lo + S // 4, :],
                                in_=x_in[lo:lo + S // 4, :]).then_inc(d_xb2.h, 16)
                d_xb2.n += 16
                return f
            bg_casts.append(mk_cast2())

        def drip_cast(g):
            if bg_casts:
                bg_casts.pop(0)(g)

        @block.sync
        def _(sy):
            for dst, src in ((h128, h128_d), (identb, identb_d), (identf, identf_d),
                             (onesc, onesc_d), (onesr, onesr_d),
                             (ones128, ones128_d), (maskt, mask_d)):
                sy.dma_start(out=dst[:], in_=src.ap()).then_inc(d_c.h, 16)
        d_c.n += 7 * 16

        with ExitStack() as es0:
            tvin = es0.enter_context(nc.sbuf_tensor("tvin", [128, 128], F32))
            ptv = es0.enter_context(nc.psum_tensor("ptv", [128, 128], F32))
            vecs = [(suug_in, MC, su_ug, ALPHA_UG), (sudp_in, KC, su_dp, ALPHA_DP),
                    (wsu_in, JL, ws_up, 1.0), (wsg_in, JL, ws_gt, 1.0),
                    (wsdp_in, NST, ws_dp, 1.0)]
            for vi, (src, rows, dst, scale) in enumerate(vecs):
                @block.sync
                def _(sy, src=src, rows=rows, vi=vi, dvn=s_dve.n):
                    if vi > 0:
                        sy.wait_ge(s_dve.h, dvn)
                    sy.dma_start(out=tvin[0:rows, :], in_=src).then_inc(d_c.h, 16)
                d_c.n += 16

                @block.tensor
                def _(t, rows=rows, vi=vi, dcn=d_c.n, dvn=s_dve.n):
                    t.wait_ge(d_c.h, dcn)
                    if vi > 0:
                        t.wait_ge(s_dve.h, dvn)
                    t.transpose(ptv[:, 0:rows], tvin[0:rows, :],
                                identf[0:rows, 0:rows]).then_inc(s_pe.h, 1)
                s_pe.n += 1

                @block.vector
                def _(v, rows=rows, dst=dst, scale=scale, pen=s_pe.n):
                    v.wait_ge(s_pe.h, pen)
                    v.tensor_scalar(out=dst[:, 0:rows], in0=ptv[:, 0:rows],
                                    scalar1=float(scale), scalar2=None,
                                    op0=ALU.mult).then_inc(s_dve.h, 1)
                s_dve.n += 1
            block.flush()

        phase_barrier()

        with ExitStack() as es1:
            BW = max(MC * FSH, KC * NSH)
            b1 = es1.enter_context(nc.sbuf_tensor("b1", [128, BW], F32))
            b2 = es1.enter_context(nc.sbuf_tensor("b2", [128, BW], F32))
            wtb = es1.enter_context(nc.sbuf_tensor("wtb", [128, BW], BF16))
            wrow = [es1.enter_context(nc.sbuf_tensor(f"wrow{i}", [128, max(DM, INTER)],
                                                     BF16)) for i in range(2)]
            fout = [es1.enter_context(nc.sbuf_tensor(f"fout{i}", [128, max(FSH, NSH)],
                                                     BF16)) for i in range(2)]
            ptr4 = [es1.enter_context(nc.psum_tensor(f"ptr4{i}", [128, 512], BF16))
                    for i in range(2)]
            ph1 = [es1.enter_context(nc.psum_tensor(f"ph1{i}", [128, 1024], F32))
                   for i in range(2)]

            g_rt = [0]
            tr_pe_by_rt = {}
            g_cc = [0]

            def transform(w_src, n_rows_tiles, n_chunks, chunk_w, row_w, sign_tile,
                          scale_rows_ap, ag_write):
                tr_evac = {}
                tgi = [0]
                for r in range(n_rows_tiles):
                    rsl = g_rt[0] % 2
                    wsl = wrow[rsl]
                    dg = d_gl2[rsl]

                    @block.gpsimd
                    def _(g, r=r, wsl=wsl, dg=dg, pen=tr_pe_by_rt.get(g_rt[0] - 2, 0)):
                        g.wait_ge(s_pe.h, pen)
                        g.dma_start(out=wsl[:, 0:row_w],
                                    in_=w_src[r * 128:(r + 1) * 128, :]).then_inc(dg.h, 16)
                        drip_cast(g)
                    dg.n += 16

                    if scale_rows_ap is not None:
                        @block.vector
                        def _(v, r=r, wsl=wsl, dgn=dg.n, dg=dg):
                            v.wait_ge(dg.h, dgn)
                            v.tensor_scalar(out=wsl[:, 0:row_w],
                                            in0=wsl[:, 0:row_w],
                                            scalar1=scale_rows_ap(r),
                                            scalar2=None, op0=ALU.mult
                                            ).then_inc(s_dve.h, 1)
                        s_dve.n += 1

                    nch = row_w // 128
                    for cc4 in range(0, nch, 4):
                        ng4 = min(4, nch - cc4)
                        gslot = tgi[0] % 2

                        @block.tensor
                        def _(t, r=r, wsl=wsl, dg=dg, cc4=cc4, ng4=ng4, gslot=gslot,
                              dgn=dg.n, dvn=s_dve.n, dcn=d_c.n,
                              pfree=tr_evac.get(tgi[0] - 2, 0)):
                            t.wait_ge(d_c.h, dcn)
                            if scale_rows_ap is not None:
                                t.wait_ge(s_dve.h, dvn)
                            else:
                                t.wait_ge(dg.h, dgn)
                            t.wait_ge(s_act.h, pfree)
                            for i in range(ng4):
                                cc = cc4 + i
                                last = t.transpose(ptr4[gslot][:, i * 128:(i + 1) * 128],
                                                   wsl[:, cc * 128:(cc + 1) * 128],
                                                   identb[:])
                            last.then_inc(s_pe.h, 1)
                        s_pe.n += 1

                        @block.scalar
                        def _(s, r=r, cc4=cc4, ng4=ng4, gslot=gslot, pen=s_pe.n):
                            s.wait_ge(s_pe.h, pen)
                            for i in range(ng4):
                                cc = cc4 + i
                                last = s.activation(
                                    wtb[:, cc * chunk_w + r * 128:
                                        cc * chunk_w + (r + 1) * 128],
                                    ptr4[gslot][:, i * 128:(i + 1) * 128], AF.Copy)
                            last.then_inc(s_act.h, 1)
                        s_act.n += 1
                        tr_evac[tgi[0]] = s_act.n
                        tgi[0] += 1
                    tr_pe_by_rt[g_rt[0]] = s_pe.n
                    g_rt[0] += 1
                h_evac = {}
                hgi = [0]
                for cc in range(n_chunks):
                    for off, w in cgroups(chunk_w, 1024):
                        gslot = hgi[0] % 2

                        @block.tensor
                        def _(t, cc=cc, off=off, w=w, gslot=gslot, acn=s_act.n,
                              dvn=h_evac.get(hgi[0] - 2, 0)):
                            t.wait_ge(s_act.h, acn)
                            t.wait_ge(s_dve.h, dvn)
                            for o2, w2 in cgroups(w, 512):
                                last = t.matmul(
                                    ph1[gslot][:, o2:o2 + w2], h128[:],
                                    wtb[:, cc * chunk_w + off + o2:
                                        cc * chunk_w + off + o2 + w2],
                                    start=True, stop=True)
                            last.then_inc(s_pe.h, 1)
                        s_pe.n += 1

                        @block.vector
                        def _(v, cc=cc, off=off, w=w, gslot=gslot, pen=s_pe.n):
                            v.wait_ge(s_pe.h, pen)
                            v.tensor_copy(
                                out=b1[:, cc * chunk_w + off: cc * chunk_w + off + w],
                                in_=ph1[gslot][:, 0:w]).then_inc(s_dve.h, 1)
                        s_dve.n += 1
                        h_evac[hgi[0]] = s_dve.n
                        hgi[0] += 1
                nst = int(round(math.log2(n_chunks)))
                hw = chunk_w // 2

                def bfly(eng, sem, c0, c1):
                    cur, nxt = b1, b2
                    for st in range(nst):
                        stride = 1 << st
                        cv = cur[:, 0:n_chunks * chunk_w].rearrange(
                            "p (g two s w) -> p g two s w",
                            two=2, s=stride, w=chunk_w)
                        nv = nxt[:, 0:n_chunks * chunk_w].rearrange(
                            "p (g two s w) -> p g two s w",
                            two=2, s=stride, w=chunk_w)
                        eng.tensor_tensor(out=nv[:, :, 0, :, c0:c1],
                                          in0=cv[:, :, 0, :, c0:c1],
                                          in1=cv[:, :, 1, :, c0:c1],
                                          op=ALU.add)
                        eng.tensor_tensor(out=nv[:, :, 1, :, c0:c1],
                                          in0=cv[:, :, 0, :, c0:c1],
                                          in1=cv[:, :, 1, :, c0:c1],
                                          op=ALU.subtract)
                        eng.drain()
                        cur, nxt = nxt, cur
                    eng.drain().then_inc(sem, 1)

                @block.vector
                def _(v, dvn=s_dve.n, pon=s_pool.n):
                    v.wait_ge(s_dve.h, dvn)
                    v.wait_ge(s_pool.h, pon)
                    bfly(v, s_dve.h, 0, hw)
                s_dve.n += 1

                @block.gpsimd
                def _(g, dvn=s_dve.n - 1, pon=s_pool.n):
                    g.wait_ge(s_dve.h, dvn)
                    g.wait_ge(s_pool.h, pon)
                    bfly(g, s_pool.h, hw, chunk_w)
                s_pool.n += 1
                cur = b1 if nst % 2 == 0 else b2
                nxt = b2 if nst % 2 == 0 else b1
                for cc in range(n_chunks):
                    fsl = g_cc[0] % 2
                    fo = fout[fsl]

                    @block.vector
                    def _(v, cc=cc, cur=cur, fo=fo, fsl=fsl, dvn=s_dve.n,
                          pon=s_pool.n, dan=d_fo[fsl].n):
                        v.wait_ge(s_dve.h, dvn)
                        v.wait_ge(s_pool.h, pon)
                        v.wait_ge(d_fo[fsl].h, dan)
                        v.tensor_scalar(
                            out=fo[:, 0:chunk_w], in0=cur[:, cc * chunk_w:(cc + 1) * chunk_w],
                            scalar1=sign_tile[:, cc:cc + 1], scalar2=None,
                            op0=ALU.mult).then_inc(s_dve.h, 1)
                    s_dve.n += 1

                    @block.gpsimd
                    def _(g, cc=cc, fo=fo, fsl=fsl, dvn=s_dve.n):
                        g.wait_ge(s_dve.h, dvn)
                        ag_write(g, cc, fo, d_fo[fsl].h)
                    d_fo[fsl].n += 16
                    g_cc[0] += 1

            for w_i, (w_src, wsrows, base) in enumerate(
                    ((wup_in, ws_up, 0), (wgate_in, ws_gt, JL * MC * 128))):
                def agw_ug(g, mc, fo, semh, base=base):
                    dst = ag1u_i[base:base + JL * MC * 128, :]
                    dst = dst.rearrange("(jl m p) c -> p jl m c", jl=JL, m=MC)
                    g.dma_start(out=dst[:, :, mc, :],
                                in_=fo[:, 0:FSH].rearrange("p (jl c) -> p jl c", jl=JL)
                                ).then_inc(semh, 16)
                transform(w_src, JL, MC, FSH, DM, su_ug,
                          (lambda r, wsrows=wsrows: wsrows[:, r:r + 1]), agw_ug)
            def agw_dn(g, cc, fo, semh):
                g.dma_start(out=ag1d_i[cc * 128:(cc + 1) * 128, :],
                            in_=fo[:, 0:NSH]).then_inc(semh, 16)
            transform(wdown_in, NST, KC, NSH, INTER, su_dp,
                      (lambda r: ws_dp[:, r:r + 1]), agw_dn)

            if bg_casts:
                @block.gpsimd
                def _(g):
                    while bg_casts:
                        bg_casts.pop(0)(g)

            block.flush()

        @block.gpsimd
        def _(g, f0=d_fo[0].n, f1=d_fo[1].n):
            g.wait_ge(d_fo[0].h, f0)
            g.wait_ge(d_fo[1].h, f1)
            g.collective_compute("AllGather", ALU.bypass,
                                 replica_groups=[list(range(n_cores))],
                                 ins=[ag1u_i.opt()],
                                 outs=[ag1u_o.opt()]).then_inc(s_cc.h, 1)
            g.collective_compute("AllGather", ALU.bypass,
                                 replica_groups=[list(range(n_cores))],
                                 ins=[ag1d_i.opt()],
                                 outs=[ag1d_o.opt()]).then_inc(s_cc.h, 1)
        s_cc.n += 2
        CC1 = s_cc.n
        XBF_LD2 = d_xb2.n

        phase_barrier()

        with ExitStack() as es2:
            wqkv_sb = es2.enter_context(nc.sbuf_tensor("wqkv_sb", [128, MC * 3 * VC], BF16))
            xfm = [es2.enter_context(nc.sbuf_tensor(f"xfm{i}", [128, MC * TG], BF16))
                   for i in range(2)]
            ev = es2.enter_context(nc.sbuf_tensor("ev", [128, 4 * TG], BF16))
            p_qk = [es2.enter_context(nc.psum_tensor(f"p_qk{i}", [128, TG], F32))
                    for i in range(2)]
            p_v = [es2.enter_context(nc.psum_tensor(f"p_v{i}", [128, 512], F32))
                   for i in range(2)]

            nstrip = es2.enter_context(nc.sbuf_tensor("nstrip", [1, 2 * NH * NTG], F32))
            nsq2 = [es2.enter_context(nc.sbuf_tensor(f"nsq2{i}", [128, TG], BF16))
                    for i in range(3)]
            p_nn2 = [es2.enter_context(nc.psum_tensor(f"p_nn2{i}", [128, 512], F32))
                     for i in range(3)]
            nrm_ctr = [0]
            nsq_dve = {}
            nred_dve = {}

            @block.sync
            def _(sy, wcn=d_wc.n):
                sy.wait_ge(d_wc.h, wcn)
                src = wqkvbf.rearrange("(m p) f -> p m f", p=128)
                dst = wqkv_sb[:].rearrange("p (m f) -> p m f", m=MC)
                sy.dma_start(out=dst[:, 0:MC // 2, :],
                             in_=src[:, 0:MC // 2, :]).then_inc(d_wl[0].h, 16)
                sy.dma_start(out=dst[:, MC // 2:MC, :],
                             in_=src[:, MC // 2:MC, :]).then_inc(d_wl[1].h, 16)
            d_wl[0].n += 16
            d_wl[1].n += 16
            WQKV_LD = 16

            ev_cnt = [0]
            ev_slot_uses = [0, 0, 0, 0]
            ev_slot_sq = {}
            last_act_for_pqk = {}
            last_act_for_pv = {}

            def load_xfm(tg):
                buf = xfm[tg % 2]
                sem = d_x[tg % 2]

                @block.sync
                def _(sy, tg=tg, buf=buf, sem=sem, pen=s_pe.n):
                    sy.wait_ge(s_pe.h, pen)
                    if (tg + 1) * TG <= S // 2:
                        sy.wait_ge(d_xb.h, XBF_LD1)
                    else:
                        sy.wait_ge(d_xb2.h, XBF_LD2)
                    for mc in range(MC):
                        sy.dma_start_transpose(
                            out=buf[:, mc * TG:(mc + 1) * TG],
                            in_=xbf[tg * TG:(tg + 1) * TG, mc * 128:(mc + 1) * 128]
                        ).then_inc(sem.h, 16)
                sem.n += 16 * MC

            def qkv_tile(kind, idx, tg, lhs_fn, rhs_fn, psl, last_act_map, scale, dst_fn):
                p = psl[idx % 2]

                @block.tensor
                def _(t, p=p, kind=kind, idx=idx, tg=tg,
                      xn=d_x[tg % 2].n, wq=WQKV_LD,
                      pfree=last_act_map.get(idx % 2, 0)):
                    t.wait_ge(d_x[tg % 2].h, xn)
                    t.wait_ge(d_wl[0].h, 16)
                    t.wait_ge(s_act.h, pfree)
                    for mc in range(MC):
                        if mc == MC // 2:
                            t.wait_ge(d_wl[1].h, wq)
                        last = t.matmul(p[:, 0:rhs_w], lhs_fn(t, mc), rhs_fn(t, mc),
                                        start=(mc == 0), stop=(mc == MC - 1))
                    last.then_inc(s_pe.h, 1)
                s_pe.n += 1

                slot = ev_cnt[0] % 4

                @block.scalar
                def _(s, p=p, slot=slot, pen=s_pe.n, su=ev_slot_uses[slot],
                      sqfree=ev_slot_sq.get(slot, None)):
                    s.wait_ge(s_pe.h, pen)
                    s.wait_ge(d_e[slot].h, su * 16)
                    if sqfree is not None:
                        s.wait_ge(s_dve.h, sqfree)
                    s.activation(ev[:, slot * TG: slot * TG + rhs_w],
                                 p[:, 0:rhs_w], AF.Copy, scale=scale
                                 ).then_inc(s_act.h, 1)
                s_act.n += 1
                last_act_map[idx % 2] = s_act.n

                @block.sync
                def _(sy, slot=slot, acn=s_act.n):
                    sy.wait_ge(s_act.h, acn)
                    dst_fn(sy, ev[:, slot * TG: slot * TG + rhs_w], d_e[slot])
                d_e[slot].n += 16
                if kind == "qk":
                    u = nrm_ctr[0]

                    @block.vector
                    def _(v, slot=slot, u=u, acn=s_act.n,
                          sqf=nsq_dve.get(u - 3, 0), pef=s_pe.n):
                        v.wait_ge(s_act.h, acn)
                        v.wait_ge(s_pe.h, pef)
                        v.tensor_tensor(out=nsq2[u % 3][:],
                                        in0=ev[:, slot * TG: (slot + 1) * TG],
                                        in1=ev[:, slot * TG: (slot + 1) * TG],
                                        op=ALU.mult).then_inc(s_dve.h, 1)
                    s_dve.n += 1
                    nsq_dve[u] = s_dve.n
                    ev_slot_sq[slot] = s_dve.n

                    @block.tensor
                    def _(t, u=u, dvn=s_dve.n, rfree=nred_dve.get(u - 3, 0)):
                        t.wait_ge(s_dve.h, dvn)
                        t.wait_ge(s_dve.h, rfree)
                        t.matmul(p_nn2[u % 3][:], ones128[:], nsq2[u % 3][:],
                                 start=True, stop=True).then_inc(s_pe.h, 1)
                    s_pe.n += 1

                    @block.vector
                    def _(v, u=u, pen=s_pe.n):
                        v.wait_ge(s_pe.h, pen)
                        v.tensor_reduce(out=nstrip[0:1, u:u + 1],
                                        in_=p_nn2[u % 3][0:1, :], axis=AXL.X,
                                        op=ALU.max).then_inc(s_dve.h, 1)
                    s_dve.n += 1
                    nred_dve[u] = s_dve.n
                    nrm_ctr[0] += 1
                ev_slot_uses[slot] += 1
                ev_cnt[0] += 1

            load_xfm(0)
            for tg in range(NTG):
                if tg + 1 < NTG:
                    load_xfm(tg + 1)
                buf = xfm[tg % 2]
                rhs_w = TG
                for j in range(2 * NH):
                    def dst_qk(g, src_ap, dsem, j=j, tg=tg):
                        scr = qscr if j < NH else kscr
                        hh = j % NH
                        g.dma_start(out=scr[hh * 128:(hh + 1) * 128, tg * TG:(tg + 1) * TG],
                                    in_=src_ap).then_inc(dsem.h, 16)
                    qkv_tile("qk", j, tg,
                             (lambda t, mc, j=j: wqkv_sb[:, mc * 3 * VC + j * 128:
                                                         mc * 3 * VC + (j + 1) * 128]),
                             (lambda t, mc, buf=buf: buf[:, mc * TG:(mc + 1) * TG]),
                             p_qk, last_act_for_pqk,
                             Q_SCALE if j < NH else 1.0, dst_qk)
                rhs_w = 512
                vi = 0
                for tt in range(TG // 128):
                    for fh in range(VC // 512):
                        def dst_v(g, src_ap, dsem, tt=tt, fh=fh, tg=tg):
                            cc0 = (tg * TG + tt * 128) // 128
                            dst = vscr.rearrange("(h p cc) d -> p h cc d",
                                                 h=NH, p=128)
                            g.dma_start(out=dst[:, fh * 4:(fh + 1) * 4, cc0, :],
                                        in_=src_ap.rearrange("p (h d) -> p h d", d=128)
                                        ).then_inc(dsem.h, 16)
                        qkv_tile("v", vi, tg,
                                 (lambda t, mc, buf=buf, tt=tt:
                                  buf[:, mc * TG + tt * 128: mc * TG + (tt + 1) * 128]),
                                 (lambda t, mc, fh=fh:
                                  wqkv_sb[:, mc * 3 * VC + 2 * VC + fh * 512:
                                          mc * 3 * VC + 2 * VC + (fh + 1) * 512]),
                                 p_v, last_act_for_pv, 1.0, dst_v)
                        vi += 1

            with ExitStack() as esn:
                nrm = esn.enter_context(nc.sbuf_tensor("nrm", [1, 6 * NH], F32))
                p_bb = esn.enter_context(nc.psum_tensor("p_bb", [128, NH], F32))

                @block.vector
                def _(v, dvn=s_dve.n):
                    v.wait_ge(s_dve.h, dvn)
                    v.drain()
                    for h in range(NH):
                        v.tensor_reduce(out=nrm[0:1, h:h + 1],
                                        in_=nstrip[0:1, h:2 * NH * NTG:2 * NH],
                                        axis=AXL.X, op=ALU.max)
                        v.tensor_reduce(out=nrm[0:1, NH + h:NH + h + 1],
                                        in_=nstrip[0:1, NH + h:2 * NH * NTG:2 * NH],
                                        axis=AXL.X, op=ALU.max)
                    v.drain()
                    v.tensor_tensor(out=nrm[0:1, 2 * NH:3 * NH],
                                    in0=nrm[0:1, 0:NH],
                                    in1=nrm[0:1, NH:2 * NH],
                                    op=ALU.mult).then_inc(s_dve.h, 1)
                s_dve.n += 1

                @block.scalar
                def _(s, dvn=s_dve.n):
                    s.wait_ge(s_dve.h, dvn)
                    s.activation(nrm[0:1, 3 * NH:4 * NH], nrm[0:1, 2 * NH:3 * NH],
                                 AF.Ln)
                    s.drain()
                    s.activation(nrm[0:1, 4 * NH:5 * NH], nrm[0:1, 3 * NH:4 * NH],
                                 AF.Exp, scale=0.5).then_inc(s_act.h, 1)
                s_act.n += 1

                @block.vector
                def _(v, acn=s_act.n):
                    v.wait_ge(s_act.h, acn)
                    v.tensor_scalar(out=nrm[0:1, 5 * NH:6 * NH],
                                    in0=nrm[0:1, 4 * NH:5 * NH],
                                    scalar1=-1.0, scalar2=None,
                                    op0=ALU.mult).then_inc(s_dve.h, 1)
                s_dve.n += 1

                @block.tensor
                def _(t, dvn=s_dve.n):
                    t.wait_ge(s_dve.h, dvn)
                    t.matmul(p_bb[:], onesr[:], nrm[0:1, 5 * NH:6 * NH],
                             start=True, stop=True).then_inc(s_pe.h, 1)
                s_pe.n += 1

                @block.vector
                def _(v, pen=s_pe.n):
                    v.wait_ge(s_pe.h, pen)
                    v.tensor_copy(out=ubias[:, 0:NH],
                                  in_=p_bb[:]).then_inc(s_dve.h, 1)
                s_dve.n += 1
                block.flush()

        phase_barrier()

        with ExitStack() as es3:
            K_h = [es3.enter_context(nc.sbuf_tensor(f"K_h{i}", [128, S], BF16)) for i in range(2)]
            Q_h = [es3.enter_context(nc.sbuf_tensor(f"Q_h{i}", [128, S], BF16)) for i in range(2)]
            V_h = [es3.enter_context(nc.sbuf_tensor(f"V_h{i}", [128, S], BF16)) for i in range(2)]
            A_t = es3.enter_context(nc.sbuf_tensor("A_t", [128, 2 * 1024], BF16))
            r_ln = es3.enter_context(nc.sbuf_tensor("r_ln", [128, 512], F32))
            r_rec = es3.enter_context(nc.sbuf_tensor("r_rec", [128, 2 * 512], F32))
            O_t = es3.enter_context(nc.sbuf_tensor("O_t", [128, 2 * 512], BF16))
            p_S = [es3.enter_context(nc.psum_tensor(f"p_S{i}", [128, 1024], F32))
                   for i in range(3)]
            p_O = es3.enter_context(nc.psum_tensor("p_O", [128, 512], F32))
            p_rb = es3.enter_context(nc.psum_tensor("p_rb", [128, 512], F32))

            head_done_pe = {}
            exp_act = {}
            cons_pe = {}
            ep = {}
            gp_ctr = [0]
            tl_ctr = [0]

            def load_head(h):
                par = h % 2
                sem = d_k[par]

                @block.gpsimd
                def _(g, h=h, par=par, sem=sem,
                      free=head_done_pe.get(h - 2, 0),
                      devals=[(dd.h, dd.n) for dd in d_e] if h < 2 else []):
                    g.wait_ge(s_pe.h, free)
                    for hh2, vv2 in devals:
                        g.wait_ge(hh2, vv2)
                    g.dma_start(out=K_h[par][:], in_=kscr[h * 128:(h + 1) * 128, :]
                                ).then_inc(sem.h, 16)
                    g.dma_start(out=Q_h[par][:], in_=qscr[h * 128:(h + 1) * 128, :]
                                ).then_inc(sem.h, 16)
                    vsrc = vscr.rearrange("(h p cc) d -> h p cc d", h=NH, p=128)
                    vdst = V_h[par][:].rearrange("p (cc d) -> p cc d", d=128)
                    g.dma_start(out=vdst, in_=vsrc[h]).then_inc(sem.h, 16)
                sem.n += 48

            def emit_epilogue(tile):
                if tile not in ep or ep[tile].get("done"):
                    return
                e = ep[tile]
                e["done"] = True

                @block.vector
                def _(v, e=e, don=d_o[e["tp"]].n):
                    v.wait_ge(s_act.h, e["recip"])
                    v.wait_ge(d_o[e["tp"]].h, don)
                    v.tensor_tensor(out=O_t[:, e["tp"] * 512:(e["tp"] + 1) * 512],
                                    in0=p_O[:],
                                    in1=r_rec[:, e["tp"] * 512:(e["tp"] + 1) * 512],
                                    op=ALU.mult).then_inc(s_dve.h, 1)
                s_dve.n += 1
                e["mult"] = s_dve.n

                @block.gpsimd
                def _(g, e=e):
                    g.wait_ge(s_dve.h, e["mult"])
                    g.dma_start(out=ag2_i[e["h"] * 128:(e["h"] + 1) * 128,
                                          e["q0"]:e["q0"] + 512],
                                in_=O_t[:, e["tp"] * 512:(e["tp"] + 1) * 512]
                                ).then_inc(d_o[e["tp"]].h, 16)
                d_o[e["tp"]].n += 16

            load_head(0)
            if NH > 1:
                load_head(1)
            for h in range(NH):
                par = h % 2
                ksem_val = d_k[par].n
                for qt in range(NQT):
                    tile = tl_ctr[0]
                    tp = tile % 2
                    q0 = qt * QT
                    cmax = (qt + 1) * CPT
                    G = cmax // 2
                    gp0 = gp_ctr[0]

                    emit_epilogue(tile - 1)

                    def s_pair(g, h=h, par=par, q0=q0, gp0=gp0, G=G, ksem_val=ksem_val):
                        if g >= G:
                            return
                        gp = gp0 + g

                        @block.tensor
                        def _(t, g=g, gp=gp,
                              psfree=exp_act.get(gp - 3, 0)):
                            t.wait_ge(s_act.h, psfree)
                            if g == 0:
                                t.wait_ge(d_k[par].h, ksem_val)
                            for ci in range(2):
                                c = 2 * g + ci
                                last = t.matmul(p_S[gp % 3][:, ci * 512:(ci + 1) * 512],
                                                K_h[par][:, c * 128:(c + 1) * 128],
                                                Q_h[par][:, q0:q0 + 512],
                                                start=True, stop=True)
                            last.then_inc(s_pe.h, 1)
                        s_pe.n += 1
                        return s_pe.n

                    smm = {}
                    smm[0] = s_pair(0)
                    if G > 1:
                        smm[1] = s_pair(1)
                    if G > 2:
                        smm[2] = s_pair(2)
                    for g in range(G):
                        gp = gp0 + g
                        @block.scalar
                        def _(s, g=g, gp=gp, h=h, sv=smm[g],
                              afree=cons_pe.get(gp - 2, 0)):
                            s.wait_ge(s_pe.h, sv)
                            s.wait_ge(s_pe.h, afree)
                            s.activation(
                                A_t[:, (gp % 2) * 1024:(gp % 2) * 1024 + 1024],
                                p_S[gp % 3][:], AF.Exp,
                                bias=ubias[:, h:h + 1]).then_inc(s_act.h, 1)
                        s_act.n += 1
                        exp_act[gp] = s_act.n
                        gate_sem, gate_val = s_act, s_act.n
                        if 2 * g >= cmax - CPT:
                            @block.vector
                            def _(v, g=g, gp=gp, qt=qt, cmax=cmax, acn=s_act.n):
                                v.wait_ge(s_act.h, acn)
                                for ci in range(2):
                                    c = 2 * g + ci
                                    jj = c - (cmax - CPT)
                                    sl = A_t[:, (gp % 2) * 1024 + ci * 512:
                                             (gp % 2) * 1024 + (ci + 1) * 512]
                                    last = v.tensor_tensor(out=sl, in0=sl,
                                                           in1=maskt[:, MOFF - 128 * jj:
                                                                     MOFF - 128 * jj + QT],
                                                           op=ALU.mult)
                                last.then_inc(s_dve.h, 1)
                            s_dve.n += 1
                            gate_sem, gate_val = s_dve, s_dve.n
                        @block.tensor
                        def _(t, g=g, gp=gp, tile=tile, tp=tp, par=par, cmax=cmax,
                              gs=gate_sem, gv=gate_val,
                              pofree=(ep.get(tile - 1, {}).get("mult", 0)),
                              prfree=(ep.get(tile - 1, {}).get("recip", 0))):
                            t.wait_ge(gs.h, gv)
                            if g == 0:
                                t.wait_ge(s_dve.h, pofree)
                                t.wait_ge(s_act.h, prfree)
                            for ci in range(2):
                                c = 2 * g + ci
                                sl = A_t[:, (gp % 2) * 1024 + ci * 512:
                                         (gp % 2) * 1024 + (ci + 1) * 512]
                                t.matmul(p_rb[:], ones128[:], sl,
                                         start=(c == 0), stop=(c == cmax - 1))
                                last = t.matmul(p_O[:],
                                                V_h[par][:, c * 128:(c + 1) * 128],
                                                sl, start=(c == 0), stop=(c == cmax - 1))
                            last.then_inc(s_pe.h, 1)
                        s_pe.n += 1
                        cons_pe[gp] = s_pe.n
                        if g + 3 < G:
                            smm[g + 3] = s_pair(g + 3)
                    @block.scalar
                    def _(s, tp=tp, pen=s_pe.n,
                          rfree=(ep.get(tile - 2, {}).get("mult", 0))):
                        s.wait_ge(s_pe.h, pen)
                        s.wait_ge(s_dve.h, rfree)
                        s.activation(r_ln[:], p_rb[:], AF.Ln)
                        s.drain()
                        s.activation(r_rec[:, tp * 512:(tp + 1) * 512], r_ln[:],
                                     AF.Exp, scale=-1.0).then_inc(s_act.h, 1)
                    s_act.n += 1
                    ep[tile] = {"recip": s_act.n, "tp": tp, "h": h, "q0": q0}
                    gp_ctr[0] += G
                    tl_ctr[0] += 1
                head_done_pe[h] = s_pe.n
                if h + 2 < NH:
                    load_head(h + 2)
                if h == NH // 2 - 1:
                    emit_epilogue(tl_ctr[0] - 1)

                    @block.gpsimd
                    def _(g, don0=d_o[0].n, don1=d_o[1].n):
                        g.wait_ge(d_o[0].h, don0)
                        g.wait_ge(d_o[1].h, don1)
                        g.collective_compute(
                            "AllGather", ALU.bypass,
                            replica_groups=[list(range(n_cores))],
                            ins=[ag2_i[0:HH * 128, :].opt()],
                            outs=[ag2_oa.opt()]).then_inc(s_cc.h, 1)
                    s_cc.n += 1
            emit_epilogue(tl_ctr[0] - 1)
            block.flush()

        CC2A = s_cc.n

        @block.gpsimd
        def _(g, don0=d_o[0].n, don1=d_o[1].n):
            g.wait_ge(d_o[0].h, don0)
            g.wait_ge(d_o[1].h, don1)
            g.collective_compute("AllGather", ALU.bypass,
                                 replica_groups=[list(range(n_cores))],
                                 ins=[ag2_i[HH * 128:NH * 128, :].opt()],
                                 outs=[ag2_ob.opt()]).then_inc(s_cc.h, 1)
        s_cc.n += 1
        CC2 = s_cc.n

        phase_barrier()

        with ExitStack() as es4:
            o_sb = es4.enter_context(nc.sbuf_tensor("o_sb", [128, 2 * NH * SH], BF16))
            wout_sb = es4.enter_context(nc.sbuf_tensor("wout_sb", [128, MC * DM], BF16))
            xh_sb = [es4.enter_context(nc.sbuf_tensor(f"xh_sb{i}", [128, DM], F32))
                     for i in range(2)]
            upre = [es4.enter_context(nc.sbuf_tensor(f"upre{i}", [128, DM], F32))
                    for i in range(2)]
            sqscr = es4.enter_context(nc.sbuf_tensor("sqscr", [128, DM], F32))
            statp = es4.enter_context(nc.sbuf_tensor("statp", [128, 8], F32))
            u_t = [es4.enter_context(nc.sbuf_tensor(f"u_t{i}", [128, DM], BF16))
                   for i in range(2)]
            p_pr = [es4.enter_context(nc.psum_tensor(f"p_pr{i}", [128, DM], F32))
                    for i in range(2)]

            @block.gpsimd
            def _(g, cca=CC2A, ccb=CC2):
                g.dma_start(out=wout_sb[:].rearrange("p (m f) -> p m f", m=MC),
                            in_=wout_in.rearrange("(m p) f -> p m f", p=128)
                            ).then_inc(d_ob.h, 16)
                pid = g.compute_val(nc.partition_id())
                with g.register("sibr") as rs, g.register("ownb") as ro, \
                        g.register("sibb") as rb, g.register("colr") as rc:
                    g.reg_alu(rs, pid, 1, ALU.bitwise_xor)
                    g.reg_alu(ro, pid, HH * 128, ALU.mult)
                    g.reg_alu(rb, rs, HH * 128, ALU.mult)
                    g.reg_alu(rc, pid, 1, ALU.bitwise_and)
                    g.reg_alu(rc, rc, SH, ALU.mult)
                    col_rv = RuntimeValue(rc)
                    for hb, (agt, ccv) in enumerate(((ag2_oa, cca), (ag2_ob, ccb))):
                        g.wait_ge(s_cc.h, ccv)
                        sem = d_ob if hb == 0 else d_ob2
                        for si, rreg in ((0, ro), (1, rb)):
                            src = agt[ds(RuntimeValue(rreg), HH * 128), ds(col_rv, SH)]
                            src = src.rearrange("(hh p) q -> p hh q", p=128)
                            dst = o_sb[:, (si * NH + hb * HH) * SH:
                                       (si * NH + (hb + 1) * HH) * SH]
                            dst = dst.rearrange("p (hh q) -> p hh q", hh=HH)
                            g.dma_start(out=dst, in_=src).then_inc(sem.h, 16)
            d_ob.n += 16 + 32
            d_ob2.n += 32
            OSB_A = d_ob.n
            OSB_LD = d_ob2.n

            ln_evac = {}
            norm_act = {}
            for ti in range(NT16):
                tpp = ti % 2

                @block.sync
                def _(sy, ti=ti, tpp=tpp, free=ln_evac.get(ti - 2, 0)):
                    sy.wait_ge(s_dve.h, free)
                    sy.dma_start(out=xh_sb[tpp][:], in_=xh_in[ti * 128:(ti + 1) * 128, :]
                                 ).then_inc(d_xh[tpp].h, 16)
                d_xh[tpp].n += 16

                @block.tensor
                def _(t, ti=ti, tpp=tpp, oba=OSB_A, obn=OSB_LD,
                      free=ln_evac.get(ti - 2, 0)):
                    t.wait_ge(d_ob.h, oba)
                    t.wait_ge(s_dve.h, free)
                    iorder = ([si * NH + hh for si in range(2) for hh in range(HH)]
                              + [si * NH + HH + hh for si in range(2)
                                 for hh in range(HH)])
                    for k, i in enumerate(iorder):
                        if k == NH:
                            t.wait_ge(d_ob2.h, obn)
                        for ng in range(NGO):
                            last = t.matmul(
                                p_pr[tpp][:, ng * 512:(ng + 1) * 512],
                                o_sb[:, i * SH + ti * 128: i * SH + (ti + 1) * 128],
                                wout_sb[:, i * DM + ng * 512: i * DM + (ng + 1) * 512],
                                start=(k == 0), stop=(k == 2 * NH - 1),
                                skip_group_check=True)
                    last.then_inc(s_pe.h, 1)
                s_pe.n += 1

                @block.vector
                def _(v, ti=ti, tpp=tpp, pen=s_pe.n, xn=d_xh[tpp].n,
                      nfree=norm_act.get(ti - 2, 0)):
                    v.wait_ge(s_pe.h, pen)
                    v.wait_ge(d_xh[tpp].h, xn)
                    v.wait_ge(s_dve.h, nfree)
                    v.drain()
                    v.tensor_tensor(out=upre[tpp][:], in0=p_pr[tpp][:], in1=xh_sb[tpp][:],
                                    op=ALU.add)
                    v.drain()
                    v.tensor_reduce(out=statp[:, 0:1], in_=upre[tpp][:],
                                    axis=AXL.X, op=ALU.add,
                                    negate=True).then_inc(s_dve.h, 1)
                s_dve.n += 1
                ln_evac[ti] = s_dve.n

                @block.vector
                def _(v, tpp=tpp, dvn=s_dve.n):
                    v.wait_ge(s_dve.h, dvn)
                    v.drain()
                    v.tensor_tensor(out=sqscr[:], in0=upre[tpp][:], in1=upre[tpp][:],
                                    op=ALU.mult)
                    v.drain()
                    v.tensor_reduce(out=statp[:, 1:2], in_=sqscr[:], axis=AXL.X,
                                    op=ALU.add)
                    v.drain()
                    v.tensor_scalar(out=statp[:, 2:3], in0=statp[:, 0:1],
                                    scalar1=1.0 / DM, scalar2=None, op0=ALU.mult)
                    v.tensor_scalar(out=statp[:, 3:4], in0=statp[:, 1:2],
                                    scalar1=1.0 / DM, scalar2=None, op0=ALU.mult)
                    v.drain()
                    v.tensor_tensor(out=statp[:, 4:5], in0=statp[:, 2:3],
                                    in1=statp[:, 2:3], op=ALU.mult)
                    v.drain()
                    v.tensor_tensor(out=statp[:, 5:6], in0=statp[:, 3:4],
                                    in1=statp[:, 4:5], op=ALU.subtract)
                    v.drain()
                    v.tensor_scalar(out=statp[:, 5:6], in0=statp[:, 5:6],
                                    scalar1=EPS, scalar2=None,
                                    op0=ALU.add).then_inc(s_dve.h, 1)
                s_dve.n += 1

                @block.scalar
                def _(s, dvn=s_dve.n):
                    s.wait_ge(s_dve.h, dvn)
                    s.drain()
                    s.activation(statp[:, 6:7], statp[:, 5:6], AF.Ln)
                    s.drain()
                    s.activation(statp[:, 7:8], statp[:, 6:7], AF.Exp,
                                 scale=-0.5).then_inc(s_act.h, 1)
                s_act.n += 1

                @block.vector
                def _(v, acn=s_act.n):
                    v.wait_ge(s_act.h, acn)
                    v.drain()
                    v.tensor_tensor(out=statp[:, 2:3], in0=statp[:, 2:3],
                                    in1=statp[:, 7:8],
                                    op=ALU.mult).then_inc(s_dve.h, 1)
                s_dve.n += 1

                @block.vector
                def _(v, ti=ti, tpp=tpp, dvn=s_dve.n, dun=d_u[tpp].n):
                    v.wait_ge(s_dve.h, dvn)
                    v.wait_ge(d_u[tpp].h, dun)
                    v.drain()
                    v.tensor_scalar(out=u_t[tpp][:], in0=upre[tpp][:],
                                    scalar1=statp[:, 7:8], scalar2=statp[:, 2:3],
                                    op0=ALU.mult, op1=ALU.add).then_inc(s_dve.h, 1)
                s_dve.n += 1

                norm_act[ti] = s_dve.n

                @block.gpsimd
                def _(g, ti=ti, tpp=tpp, acn=s_dve.n):
                    g.wait_ge(s_dve.h, acn)
                    g.dma_start(out=uscr[ti * 128:(ti + 1) * 128, :], in_=u_t[tpp][:]
                                ).then_inc(d_u[tpp].h, 16)
                d_u[tpp].n += 16
            block.flush()

        phase_barrier()

        with ExitStack() as es5:
            hbuf = es5.enter_context(nc.sbuf_tensor("hbuf", [128, KC * TB], BF16))
            u_fm = es5.enter_context(nc.sbuf_tensor("u_fm", [128, MC * TB], BF16))
            wu_sb = [es5.enter_context(nc.sbuf_tensor(f"wu_sb{i}", [128, MC * 128], BF16))
                     for i in range(2)]
            wg_sb = [es5.enter_context(nc.sbuf_tensor(f"wg_sb{i}", [128, MC * 128], BF16))
                     for i in range(2)]
            sgb = [es5.enter_context(nc.sbuf_tensor(f"sgb{i}", [128, TB], F32))
                   for i in range(2)]
            wd_sb = [es5.enter_context(nc.sbuf_tensor(f"wd_sb{i}", [128, 8 * 512], BF16))
                     for i in range(2)]
            u_ng = [es5.enter_context(nc.sbuf_tensor(f"u_ng{i}", [128, TSUB * 512], BF16))
                    for i in range(2)]
            res = es5.enter_context(nc.sbuf_tensor("res", [128, TSUB * DM], BF16))
            sums = es5.enter_context(nc.sbuf_tensor("sums", [128, TSUB * NGO], F32))
            ssq = es5.enter_context(nc.sbuf_tensor("ssq", [128, TSUB * NGO], F32))
            st2 = es5.enter_context(nc.sbuf_tensor("st2", [128, 4 * TSUB], F32))
            sq2 = es5.enter_context(nc.sbuf_tensor("sq2", [128, 512], F32))
            outt = [es5.enter_context(nc.sbuf_tensor(f"outt{i}", [128, DM], F32))
                    for i in range(2)]
            p_up = [es5.enter_context(nc.psum_tensor(f"p_up{i}", [128, TB], F32))
                    for i in range(2)]
            p_gt = [es5.enter_context(nc.psum_tensor(f"p_gt{i}", [128, TB], F32))
                    for i in range(2)]
            p_dn = [es5.enter_context(nc.psum_tensor(f"p_dn{i}", [128, 512], F32))
                    for i in range(TSUB)]

            UG_SLOT = 2 * JL * MC * 128
            ug_last_pe = {0: 0, 1: 0}
            wd_last_pe = {0: 0, 1: 0}
            ug_end_pe = {}
            for nb in range(NB):
                t0 = nb * TB
                @block.sync
                def _(sy, nb=nb, t0=t0,
                      pen=(ug_end_pe.get(nb - 1, 0) if nb > 0 else s_pe.n),
                      du0=d_u[0].n, du1=d_u[1].n):
                    sy.wait_ge(s_pe.h, pen)
                    sy.wait_ge(d_u[0].h, du0)
                    sy.wait_ge(d_u[1].h, du1)
                    for mc in range(MC):
                        sy.dma_start_transpose(
                            out=u_fm[:, mc * TB:(mc + 1) * TB],
                            in_=uscr[t0:t0 + TB, mc * 128:(mc + 1) * 128]
                        ).then_inc(d_uf.h, 16)
                d_uf.n += 16 * MC
                UFM_LD = d_uf.n

                hdve = {}
                ugact = {}
                wld = {}
                ACT0B = s_act.n
                DVE0B = s_dve.n

                def load_ug(j):
                    sl = j % 2

                    @block.sync
                    def _(sy, j=j, sl=sl, ccv=CC1, pefree=ug_last_pe[j % 2]):
                        sy.wait_ge(s_cc.h, ccv)
                        sy.wait_ge(s_pe.h, pefree)
                        slot = j // JL
                        jl = j % JL
                        for w_i, wbuf in ((0, wu_sb[sl]), (1, wg_sb[sl])):
                            rbase = slot * UG_SLOT + (w_i * JL + jl) * MC * 128
                            src = ag1u_o[rbase:rbase + MC * 128, :]
                            sy.dma_start(out=wbuf[:].rearrange("p (m c) -> p m c", m=MC),
                                         in_=src.rearrange("(m p) c -> p m c", p=128)
                                         ).then_inc(d_f[sl].h, 16)
                    d_f[j % 2].n += 32
                    wld[("ld", j)] = d_f[j % 2].n

                load_ug(0)
                if KC > 1:
                    load_ug(1)
                for j in range(KC):
                    sl = j % 2

                    @block.tensor
                    def _(t, j=j, sl=sl, ldn=wld[("ld", j)], un=UFM_LD,
                          pfree=ugact.get(j - 2, ACT0B), pfree2=hdve.get(j - 2, DVE0B)):
                        t.wait_ge(d_f[sl].h, ldn)
                        t.wait_ge(d_uf.h, un)
                        t.wait_ge(s_act.h, pfree)
                        t.wait_ge(s_dve.h, pfree2)
                        for mc in range(MC):
                            t.matmul(p_up[sl][:], wu_sb[sl][:, mc * 128:(mc + 1) * 128],
                                     u_fm[:, mc * TB:(mc + 1) * TB],
                                     start=(mc == 0), stop=(mc == MC - 1))
                        for mc in range(MC):
                            last = t.matmul(p_gt[sl][:], wg_sb[sl][:, mc * 128:(mc + 1) * 128],
                                            u_fm[:, mc * TB:(mc + 1) * TB],
                                            start=(mc == 0), stop=(mc == MC - 1))
                        last.then_inc(s_pe.h, 1)
                    s_pe.n += 1
                    wld[("pe", j)] = s_pe.n
                    ug_last_pe[j % 2] = s_pe.n

                    @block.scalar
                    def _(s, j=j, sl=sl, pen=s_pe.n, hfree=hdve.get(j - 2, DVE0B)):
                        s.wait_ge(s_pe.h, pen)
                        s.wait_ge(s_dve.h, hfree)
                        s.activation(sgb[sl][:], p_gt[sl][:],
                                     AF.Silu).then_inc(s_act.h, 1)
                    s_act.n += 1
                    ugact[j] = s_act.n

                    @block.vector
                    def _(v, j=j, sl=sl, acn=s_act.n):
                        v.wait_ge(s_act.h, acn)
                        v.scalar_tensor_tensor(
                            out=hbuf[:, j * TB:(j + 1) * TB], in0=p_up[sl][:],
                            scalar=1.0, in1=sgb[sl][:], op0=ALU.mult,
                            op1=ALU.mult).then_inc(s_dve.h, 1)
                    s_dve.n += 1
                    hdve[j] = s_dve.n
                    if j + 2 < KC:
                        load_ug(j + 2)
                H_DONE = s_dve.n
                ug_end_pe[nb] = s_pe.n

                KG = KC // 8
                dn_evac = {}
                dwld = {}
                ACT0D = s_act.n
                DVE0D = s_dve.n

                def load_wd(ng, kg):
                    sl = kg % 2

                    @block.sync
                    def _(sy, ng=ng, kg=kg, sl=sl, ccv=CC1, free=wd_last_pe[kg % 2]):
                        sy.wait_ge(s_cc.h, ccv)
                        sy.wait_ge(s_pe.h, free)
                        nsub = 512 // NSH if NSH < 512 else 1
                        csub = min(NSH, 512)
                        for si in range(nsub):
                            shard = (ng * 512) // NSH + si
                            rbase = shard * KC * 128 + kg * 8 * 128
                            src = ag1d_o[rbase:rbase + 8 * 128, :]
                            dstv = wd_sb[sl][:].rearrange("p (k n) -> p k n", k=8)
                            sy.dma_start(out=dstv[:, :, si * csub:(si + 1) * csub],
                                         in_=src.rearrange("(k p) n -> p k n", p=128)
                                         ).then_inc(d_d[sl].h, 16)
                    d_d[kg % 2].n += 16 * (512 // NSH if NSH < 512 else 1)
                    dwld[("ld", ng, kg)] = d_d[kg % 2].n

                for ng in range(NGO):
                    load_wd(ng, 0)
                    if KG > 1:
                        load_wd(ng, 1)
                    usl = ng % 2

                    @block.sync
                    def _(sy, ng=ng, t0=t0, usl=usl, free=dn_evac.get(ng - 2, DVE0D),
                          du0=d_u[0].n, du1=d_u[1].n):
                        sy.wait_ge(s_dve.h, free)
                        sy.wait_ge(d_u[0].h, du0)
                        sy.wait_ge(d_u[1].h, du1)
                        sy.dma_start(out=u_ng[usl][:].rearrange("p (t n) -> p t n", t=TSUB),
                                     in_=uscr[t0:t0 + TB, ng * 512:(ng + 1) * 512]
                                     .rearrange("(t p) n -> p t n", p=128)
                                     ).then_inc(d_un[usl].h, 16)
                    d_un[usl].n += 16

                    for kg in range(KG):
                        sl = kg % 2

                        @block.tensor
                        def _(t, ng=ng, kg=kg, sl=sl, ldn=dwld[("ld", ng, kg)],
                              hn=H_DONE, efree=dn_evac.get(ng - 1, DVE0D),
                              afree=dn_evac.get(("a", ng - 1), ACT0D)):
                            t.wait_ge(d_d[sl].h, ldn)
                            t.wait_ge(s_dve.h, hn)
                            if kg == 0:
                                t.wait_ge(s_dve.h, efree)
                                t.wait_ge(s_act.h, afree)
                            for k8 in range(8):
                                k = kg * 8 + k8
                                for ts in range(TSUB):
                                    last = t.matmul(
                                        p_dn[ts][:],
                                        hbuf[:, k * TB + ts * 128: k * TB + (ts + 1) * 128],
                                        wd_sb[sl][:, k8 * 512:(k8 + 1) * 512],
                                        start=(k == 0), stop=(k == KC - 1))
                            last.then_inc(s_pe.h, 1)
                        s_pe.n += 1
                        dwld[("pe", ng, kg)] = s_pe.n
                        wd_last_pe[kg % 2] = s_pe.n
                        if kg + 2 < KG:
                            load_wd(ng, kg + 2)
                    for ts in range(TSUB):
                        @block.vector
                        def _(v, ng=ng, ts=ts, usl=usl, pen=s_pe.n, dn=d_un[usl].n):
                            v.wait_ge(s_pe.h, pen)
                            v.wait_ge(d_un[usl].h, dn)
                            v.drain()
                            v.tensor_tensor(
                                out=res[:, ts * DM + ng * 512: ts * DM + (ng + 1) * 512],
                                in0=p_dn[ts][:], in1=u_ng[usl][:, ts * 512:(ts + 1) * 512],
                                op=ALU.add)
                            v.drain()
                            v.tensor_reduce(
                                out=sums[:, ts * NGO + ng: ts * NGO + ng + 1],
                                in_=res[:, ts * DM + ng * 512: ts * DM + (ng + 1) * 512],
                                axis=AXL.X, op=ALU.add).then_inc(s_dve.h, 1)
                        s_dve.n += 1

                        @block.vector
                        def _(v, ng=ng, ts=ts, dvn=s_dve.n):
                            v.drain()
                            rslice = res[:, ts * DM + ng * 512: ts * DM + (ng + 1) * 512]
                            v.tensor_tensor(out=sq2[:], in0=rslice, in1=rslice,
                                            op=ALU.mult)
                            v.drain()
                            v.tensor_reduce(
                                out=ssq[:, ts * NGO + ng: ts * NGO + ng + 1],
                                in_=sq2[:], axis=AXL.X,
                                op=ALU.add).then_inc(s_dve.h, 1)
                        s_dve.n += 1
                    dn_evac[ng] = s_dve.n
                    dn_evac[("a", ng)] = s_act.n

                @block.vector
                def _(v, dvn=s_dve.n):
                    v.wait_ge(s_dve.h, dvn)
                    v.drain()
                    for ts in range(TSUB):
                        v.tensor_reduce(out=st2[:, ts:ts + 1],
                                        in_=sums[:, ts * NGO:(ts + 1) * NGO],
                                        axis=AXL.X, op=ALU.add, negate=True)
                    for ts in range(TSUB):
                        v.tensor_reduce(out=st2[:, TSUB + ts:TSUB + ts + 1],
                                        in_=ssq[:, ts * NGO:(ts + 1) * NGO],
                                        axis=AXL.X, op=ALU.add)
                    v.drain()
                    v.tensor_scalar(out=st2[:, 0:TSUB], in0=st2[:, 0:TSUB],
                                    scalar1=1.0 / DM, scalar2=None, op0=ALU.mult)
                    v.tensor_scalar(out=st2[:, TSUB:2 * TSUB], in0=st2[:, TSUB:2 * TSUB],
                                    scalar1=1.0 / DM, scalar2=None, op0=ALU.mult)
                    v.drain()
                    v.tensor_tensor(out=st2[:, 2 * TSUB:3 * TSUB], in0=st2[:, 0:TSUB],
                                    in1=st2[:, 0:TSUB], op=ALU.mult)
                    v.drain()
                    v.tensor_tensor(out=st2[:, 2 * TSUB:3 * TSUB],
                                    in0=st2[:, TSUB:2 * TSUB],
                                    in1=st2[:, 2 * TSUB:3 * TSUB], op=ALU.subtract)
                    v.drain()
                    v.tensor_scalar(out=st2[:, 2 * TSUB:3 * TSUB],
                                    in0=st2[:, 2 * TSUB:3 * TSUB],
                                    scalar1=EPS, scalar2=None,
                                    op0=ALU.add).then_inc(s_dve.h, 1)
                s_dve.n += 1

                @block.scalar
                def _(s, dvn=s_dve.n):
                    s.wait_ge(s_dve.h, dvn)
                    s.drain()
                    s.activation(st2[:, 3 * TSUB:4 * TSUB], st2[:, 2 * TSUB:3 * TSUB],
                                 AF.Ln)
                    s.drain()
                    s.activation(st2[:, 2 * TSUB:3 * TSUB], st2[:, 3 * TSUB:4 * TSUB],
                                 AF.Exp, scale=-0.5).then_inc(s_act.h, 1)
                s_act.n += 1

                @block.vector
                def _(v, acn=s_act.n):
                    v.wait_ge(s_act.h, acn)
                    v.drain()
                    v.tensor_tensor(out=st2[:, 0:TSUB], in0=st2[:, 0:TSUB],
                                    in1=st2[:, 2 * TSUB:3 * TSUB],
                                    op=ALU.mult).then_inc(s_dve.h, 1)
                s_dve.n += 1

                for ts in range(TSUB):
                    @block.vector
                    def _(v, ts=ts, dvn=s_dve.n, don=d_out[ts % 2].n):
                        v.wait_ge(s_dve.h, dvn)
                        v.wait_ge(d_out[ts % 2].h, don)
                        v.drain()
                        v.tensor_scalar(out=outt[ts % 2][:],
                                        in0=res[:, ts * DM:(ts + 1) * DM],
                                        scalar1=st2[:, 2 * TSUB + ts:2 * TSUB + ts + 1],
                                        scalar2=st2[:, ts:ts + 1],
                                        op0=ALU.mult, op1=ALU.add).then_inc(s_dve.h, 1)
                    s_dve.n += 1

                    @block.gpsimd
                    def _(g, nb=nb, ts=ts, t0=t0, acn=s_dve.n):
                        g.wait_ge(s_dve.h, acn)
                        g.dma_start(out=out_ext[t0 + ts * 128: t0 + (ts + 1) * 128, :],
                                    in_=outt[ts % 2][:]).then_inc(d_out[ts % 2].h, 16)
                    d_out[ts % 2].n += 16
            block.flush()

        @block.gpsimd
        def _(g, don0=d_out[0].n, don1=d_out[1].n):
            g.wait_ge(d_out[0].h, don0)
            g.wait_ge(d_out[1].h, don1)
        block.flush()

    return nc


_NC_CACHE = {}


def shard_inputs(x, w_qkv, w_out, su_ug, su_dp, ws_ug, ws_dp, w_up, w_gate, w_down,
                 n_cores=8):
    B, S, DM = x.shape
    HEADS = DM // 128
    INTER = w_up.shape[0]
    NH = HEADS // 2
    VC = NH * 128
    SH = S // 2
    FSH = INTER // n_cores
    NSH = DM // n_cores
    MC, KC, NST = DM // 128, INTER // 128, NSH // 128
    wq = w_qkv[:, 0:DM]
    wk = w_qkv[:, DM:2 * DM]
    wv = w_qkv[:, 2 * DM:3 * DM]
    c = np.ascontiguousarray
    in_maps = []
    for core in range(n_cores):
        b, hs = core // 2, core % 2
        hsl = slice(hs * VC, (hs + 1) * VC)
        in_maps.append({
            "x_in": c(x[b]),
            "xh_in": c(x[b, hs * SH:(hs + 1) * SH]),
            "wqkv_in": c(np.concatenate([wq[:, hsl], wk[:, hsl], wv[:, hsl]], axis=1)),
            "wout_in": c(np.concatenate([w_out[hs * VC:(hs + 1) * VC],
                                         w_out[(1 - hs) * VC:(2 - hs) * VC]], axis=0)),
            "wup_in": c(w_up[core * FSH:(core + 1) * FSH]),
            "wgate_in": c(w_gate[core * FSH:(core + 1) * FSH]),
            "wdown_in": c(w_down[core * NSH:(core + 1) * NSH]),
            "suug_in": c(su_ug.reshape(MC, 128)),
            "sudp_in": c(su_dp.reshape(KC, 128)),
            "wsu_in": c(ws_ug[core * FSH:(core + 1) * FSH].reshape(FSH // 128, 128)),
            "wsg_in": c(ws_ug[INTER + core * FSH:INTER + (core + 1) * FSH]
                        .reshape(FSH // 128, 128)),
            "wsdp_in": c(ws_dp[core * NSH:(core + 1) * NSH].reshape(NST, 128)),
        })
    return in_maps


def kernel(L0_attn_x, L0_attn_w_qkv, L0_attn_w_out, L0_ffn_SU_ug, L0_ffn_SU_dp,
           L0_ffn_Wscale_ug, L0_ffn_Wscale_dp, L0_ffn_W_up, L0_ffn_W_gate, L0_ffn_W_down):
    x = np.asarray(L0_attn_x)
    B, S, DM = x.shape
    HEADS = DM // 128
    INTER = np.asarray(L0_ffn_W_up).shape[0]
    n_cores = 8
    NH = HEADS // 2
    VC = NH * 128
    SH = S // 2
    FSH = INTER // n_cores
    NSH = DM // n_cores
    MC, KC, NST = DM // 128, INTER // 128, NSH // 128

    key = (S, DM, HEADS, INTER)
    if key not in _NC_CACHE:
        _NC_CACHE[key] = build_nc(S, DM, HEADS, INTER, n_cores)
    nc = _NC_CACHE[key]

    in_maps = shard_inputs(
        x, np.asarray(L0_attn_w_qkv), np.asarray(L0_attn_w_out),
        np.asarray(L0_ffn_SU_ug), np.asarray(L0_ffn_SU_dp),
        np.asarray(L0_ffn_Wscale_ug), np.asarray(L0_ffn_Wscale_dp),
        np.asarray(L0_ffn_W_up), np.asarray(L0_ffn_W_gate), np.asarray(L0_ffn_W_down),
        n_cores)

    res = run_bass_kernel_spmd(nc, in_maps, core_ids=list(range(n_cores)))
    out = np.empty((B, S, DM), np.float32)
    for core in range(n_cores):
        b, hs = core // 2, core % 2
        out[b, hs * SH:(hs + 1) * SH] = res.results[core]["out_ext"]
    return out


# version 16
# speedup vs baseline: 1.0624x; 1.0132x over previous
import math
import numpy as np
import ml_dtypes
from contextlib import ExitStack

import concourse.bass as bass
import concourse.mybir as mybir
from concourse.bass_utils import run_bass_kernel_spmd
from concourse.expressions import RuntimeValue

F32 = mybir.dt.float32
BF16 = mybir.dt.bfloat16
AF = mybir.ActivationFunctionType
ALU = mybir.AluOpType
AXL = mybir.AxisListType
ds = bass.ds

EPS = 1e-5


def had(n):
    a = np.arange(n)
    bits = a[:, None] & a[None, :]
    pc = np.zeros((n, n), np.int64)
    x = bits
    while x.max() > 0:
        pc += x & 1
        x >>= 1
    return np.where(pc % 2 == 0, 1.0, -1.0).astype(np.float32)


def cgroups(total, w):
    off = 0
    while off < total:
        yield off, min(w, total - off)
        off += w


class Sem:
    def __init__(self, es, nc, name):
        self.h = es.enter_context(nc.semaphore(name))
        self.n = 0

    def inc(self, instr, k=1):
        instr.then_inc(self.h, k)
        self.n += k
        return self.n


class BufferedBlock:

    def __init__(self, real):
        self._real = real
        self._q = []

    def _add(self, eng, fn):
        if self._q and self._q[-1][0] == eng:
            self._q[-1][1].append(fn)
        else:
            self._q.append((eng, [fn]))
        return fn

    def tensor(self, fn):
        return self._add("tensor", fn)

    def scalar(self, fn):
        return self._add("scalar", fn)

    def vector(self, fn):
        return self._add("vector", fn)

    def gpsimd(self, fn):
        return self._add("gpsimd", fn)

    def sync(self, fn):
        return self._add("sync", fn)

    def flush(self):
        merged = {}
        order = []
        for eng, fns in self._q:
            if eng not in merged:
                merged[eng] = []
                order.append(eng)
            merged[eng].extend(fns)
        self._q = []
        for eng in order:
            fns = merged[eng]

            def body(e, fns=fns):
                for f in fns:
                    f(e)
            getattr(self._real, eng)(body)


def build_nc(S, DM, HEADS, INTER, n_cores=8):
    HD = 128
    assert DM == HEADS * HD and DM % 512 == 0 and INTER % 512 == 0 and S % 1024 == 0
    NH = HEADS // 2
    MC = DM // 128
    KC = INTER // 128
    FSH = INTER // n_cores
    JL = FSH // 128
    NSH = DM // n_cores
    NST = NSH // 128
    assert NST >= 1 and NSH % 128 == 0 and 512 % NSH == 0 or NSH % 512 == 0
    QT = 512
    NQT = S // QT
    CPT = QT // 128
    SH = S // 2
    TB = 512
    NB = SH // TB
    TSUB = TB // 128
    NGO = DM // 512
    VC = NH * HD
    TG = 512
    NTG = S // TG
    NSG = S // 512
    NT16 = SH // 128
    Q_SCALE = 1.0 / math.sqrt(float(HD))
    ALPHA_UG = 1.0 / math.sqrt(float(DM))
    ALPHA_DP = 1.0 / math.sqrt(float(INTER))
    MOFF = (CPT - 1) * 128
    MW = MOFF + QT

    nc = bass.Bass("TRN2", target_bir_lowering=False, debug=False)

    x_in = nc.dram_tensor("x_in", [S, DM], F32, kind="ExternalInput").ap()
    xh_in = nc.dram_tensor("xh_in", [SH, DM], F32, kind="ExternalInput").ap()
    wqkv_in = nc.dram_tensor("wqkv_in", [DM, 3 * VC], F32, kind="ExternalInput").ap()
    wout_in = nc.dram_tensor("wout_in", [DM, DM], F32, kind="ExternalInput").ap()
    wup_in = nc.dram_tensor("wup_in", [FSH, DM], F32, kind="ExternalInput").ap()
    wgate_in = nc.dram_tensor("wgate_in", [FSH, DM], F32, kind="ExternalInput").ap()
    wdown_in = nc.dram_tensor("wdown_in", [NSH, INTER], F32, kind="ExternalInput").ap()
    suug_in = nc.dram_tensor("suug_in", [MC, 128], F32, kind="ExternalInput").ap()
    sudp_in = nc.dram_tensor("sudp_in", [KC, 128], F32, kind="ExternalInput").ap()
    wsu_in = nc.dram_tensor("wsu_in", [JL, 128], F32, kind="ExternalInput").ap()
    wsg_in = nc.dram_tensor("wsg_in", [JL, 128], F32, kind="ExternalInput").ap()
    wsdp_in = nc.dram_tensor("wsdp_in", [NST, 128], F32, kind="ExternalInput").ap()
    out_ext = nc.dram_tensor("out_ext", [SH, DM], F32, kind="ExternalOutput").ap()

    h128_d = nc.inline_tensor(had(128).astype(ml_dtypes.bfloat16), name="h128c")
    identb_d = nc.inline_tensor(np.eye(128, dtype=ml_dtypes.bfloat16), name="identbc")
    identf_d = nc.inline_tensor(np.eye(128, dtype=np.float32), name="identfc")
    onesc_d = nc.inline_tensor(np.ones((128, 1), dtype=ml_dtypes.bfloat16), name="onescc")
    onesr_d = nc.inline_tensor(np.ones((1, 128), dtype=np.float32), name="onesrc")
    ones128_d = nc.inline_tensor(np.ones((128, 128), dtype=ml_dtypes.bfloat16),
                                 name="ones128c")
    mk = np.arange(128)[:, None] <= (np.arange(MW)[None, :] - MOFF)
    mask_d = nc.inline_tensor(mk.astype(ml_dtypes.bfloat16), name="maskc")

    xbf = nc.dram_tensor("xbf", [S, DM], BF16).ap()
    wqkvbf = nc.dram_tensor("wqkvbf", [DM, 3 * VC], BF16).ap()
    qscr = nc.dram_tensor("qscr", [NH * 128, S], BF16).ap()
    kscr = nc.dram_tensor("kscr", [NH * 128, S], BF16).ap()
    vscr = nc.dram_tensor("vscr", [NH * 128 * (S // 128), 128], BF16).ap()
    uscr = nc.dram_tensor("uscr", [SH, DM], BF16).ap()
    ag1u_i = nc.dram_tensor("ag1u_i", [2 * JL * MC * 128, 128], BF16).ap()
    ag1u_o = nc.dram_tensor("ag1u_o", [n_cores * 2 * JL * MC * 128, 128], BF16,
                            addr_space="Shared").ap()
    ag1d_i = nc.dram_tensor("ag1d_i", [KC * 128, NSH], BF16).ap()
    ag1d_o = nc.dram_tensor("ag1d_o", [n_cores * KC * 128, NSH], BF16,
                            addr_space="Shared").ap()
    ag2_i = nc.dram_tensor("ag2_i", [NH * 128, S], BF16).ap()
    HH = NH // 2
    ag2_oa = nc.dram_tensor("ag2_oa", [n_cores * HH * 128, S], BF16,
                            addr_space="Shared").ap()
    ag2_ob = nc.dram_tensor("ag2_ob", [n_cores * HH * 128, S], BF16,
                            addr_space="Shared").ap()

    ES = ExitStack()
    with ES:
        h128 = ES.enter_context(nc.sbuf_tensor("h128", [128, 128], BF16))
        identb = ES.enter_context(nc.sbuf_tensor("identb", [128, 128], BF16))
        identf = ES.enter_context(nc.sbuf_tensor("identf", [128, 128], F32))
        onesc = ES.enter_context(nc.sbuf_tensor("onesc", [128, 1], BF16))
        onesr = ES.enter_context(nc.sbuf_tensor("onesr", [1, 128], F32))
        ones128 = ES.enter_context(nc.sbuf_tensor("ones128", [128, 128], BF16))
        maskt = ES.enter_context(nc.sbuf_tensor("maskt", [128, MW], BF16))
        su_ug = ES.enter_context(nc.sbuf_tensor("su_ug", [128, MC], F32))
        su_dp = ES.enter_context(nc.sbuf_tensor("su_dp", [128, KC], F32))
        ws_up = ES.enter_context(nc.sbuf_tensor("ws_up", [128, JL], F32))
        ws_gt = ES.enter_context(nc.sbuf_tensor("ws_gt", [128, JL], F32))
        ws_dp = ES.enter_context(nc.sbuf_tensor("ws_dp", [128, NST], F32))
        ubias = ES.enter_context(nc.sbuf_tensor("ubias", [128, NH], F32))

        s_pe = Sem(ES, nc, "s_pe")
        s_act = Sem(ES, nc, "s_act")
        s_dve = Sem(ES, nc, "s_dve")
        s_pool = Sem(ES, nc, "s_pool")
        s_sp = Sem(ES, nc, "s_sp")
        s_cc = Sem(ES, nc, "s_cc")
        d_gl = Sem(ES, nc, "d_gl")
        d_gl2 = [Sem(ES, nc, f"d_gl2{i}") for i in range(2)]
        d_wc = Sem(ES, nc, "d_wc")
        d_wl = [Sem(ES, nc, f"d_wl{i}") for i in range(2)]
        d_xb = Sem(ES, nc, "d_xb")
        d_xb2 = Sem(ES, nc, "d_xb2")
        d_c = Sem(ES, nc, "d_c")
        d_e = [Sem(ES, nc, f"d_e{i}") for i in range(4)]
        d_x = [Sem(ES, nc, f"d_x{i}") for i in range(2)]
        d_k = [Sem(ES, nc, f"d_k{i}") for i in range(2)]
        d_o = [Sem(ES, nc, f"d_o{i}") for i in range(2)]
        d_ag = Sem(ES, nc, "d_ag")
        d_fo = [Sem(ES, nc, f"d_fo{i}") for i in range(2)]
        d_ob = Sem(ES, nc, "d_ob")
        d_ob2 = Sem(ES, nc, "d_ob2")
        d_xh = [Sem(ES, nc, f"d_xh{i}") for i in range(2)]
        d_u = [Sem(ES, nc, f"d_u{i}") for i in range(2)]
        d_uf = Sem(ES, nc, "d_uf")
        d_un = [Sem(ES, nc, f"d_un{i}") for i in range(2)]
        d_f = [Sem(ES, nc, f"d_f{i}") for i in range(2)]
        d_d = [Sem(ES, nc, f"d_d{i}") for i in range(2)]
        d_out = [Sem(ES, nc, f"d_out{i}") for i in range(2)]

        block = BufferedBlock(ES.enter_context(nc.Block()))
        sb_pe = Sem(ES, nc, "sb_pe")
        sb_act = Sem(ES, nc, "sb_act")
        sb_dve = Sem(ES, nc, "sb_dve")
        sb_pool = Sem(ES, nc, "sb_pool")
        sb_sp = Sem(ES, nc, "sb_sp")

        def phase_barrier():
            block.flush()
            for s in (sb_pe, sb_act, sb_dve, sb_pool, sb_sp):
                s.n += 1

            @block.tensor
            def _(t):
                t.drain().then_inc(sb_pe.h, 1)

            @block.scalar
            def _(s):
                s.drain().then_inc(sb_act.h, 1)

            @block.vector
            def _(v):
                v.drain().then_inc(sb_dve.h, 1)

            @block.gpsimd
            def _(g):
                g.drain().then_inc(sb_pool.h, 1)

            @block.sync
            def _(sy):
                sy.drain().then_inc(sb_sp.h, 1)
            vals = [(s, s.n) for s in (sb_pe, sb_act, sb_dve, sb_pool, sb_sp)]
            for eng_name, own in (("tensor", sb_pe), ("scalar", sb_act), ("vector", sb_dve),
                                  ("gpsimd", sb_pool), ("sync", sb_sp)):
                def mk(own=own):
                    def f(eng):
                        for s, v in vals:
                            if s is not own:
                                eng.wait_ge(s.h, v)
                    return f
                getattr(block, eng_name)(mk())
            block.flush()

        @block.gpsimd
        def _(g):
            g.dma_start(out=xbf[0:S // 2, :], in_=x_in[0:S // 2, :]).then_inc(d_xb.h, 16)
        d_xb.n += 16
        XBF_LD1 = d_xb.n

        bg_casts = []
        for m4 in range(0, MC, 2):
            def mk_cast(m4=m4):
                def f(g):
                    g.dma_start(out=wqkvbf[m4 * 128:(m4 + 2) * 128, :],
                                in_=wqkv_in[m4 * 128:(m4 + 2) * 128, :]
                                ).then_inc(d_wc.h, 16)
                d_wc.n += 16
                return f
            bg_casts.append(mk_cast())
        for q4 in range(2):
            def mk_cast2(q4=q4):
                def f(g):
                    lo = S // 2 + q4 * (S // 4)
                    g.dma_start(out=xbf[lo:lo + S // 4, :],
                                in_=x_in[lo:lo + S // 4, :]).then_inc(d_xb2.h, 16)
                d_xb2.n += 16
                return f
            bg_casts.append(mk_cast2())

        def drip_cast(g):
            if bg_casts:
                bg_casts.pop(0)(g)

        @block.sync
        def _(sy):
            for dst, src in ((h128, h128_d), (identb, identb_d), (identf, identf_d),
                             (onesc, onesc_d), (onesr, onesr_d),
                             (ones128, ones128_d), (maskt, mask_d)):
                sy.dma_start(out=dst[:], in_=src.ap()).then_inc(d_c.h, 16)
        d_c.n += 7 * 16

        with ExitStack() as es0:
            tvin = es0.enter_context(nc.sbuf_tensor("tvin", [128, 128], F32))
            ptv = es0.enter_context(nc.psum_tensor("ptv", [128, 128], F32))
            vecs = [(suug_in, MC, su_ug, ALPHA_UG), (sudp_in, KC, su_dp, ALPHA_DP),
                    (wsu_in, JL, ws_up, 1.0), (wsg_in, JL, ws_gt, 1.0),
                    (wsdp_in, NST, ws_dp, 1.0)]
            for vi, (src, rows, dst, scale) in enumerate(vecs):
                @block.sync
                def _(sy, src=src, rows=rows, vi=vi, dvn=s_dve.n):
                    if vi > 0:
                        sy.wait_ge(s_dve.h, dvn)
                    sy.dma_start(out=tvin[0:rows, :], in_=src).then_inc(d_c.h, 16)
                d_c.n += 16

                @block.tensor
                def _(t, rows=rows, vi=vi, dcn=d_c.n, dvn=s_dve.n):
                    t.wait_ge(d_c.h, dcn)
                    if vi > 0:
                        t.wait_ge(s_dve.h, dvn)
                    t.transpose(ptv[:, 0:rows], tvin[0:rows, :],
                                identf[0:rows, 0:rows]).then_inc(s_pe.h, 1)
                s_pe.n += 1

                @block.vector
                def _(v, rows=rows, dst=dst, scale=scale, pen=s_pe.n):
                    v.wait_ge(s_pe.h, pen)
                    v.tensor_scalar(out=dst[:, 0:rows], in0=ptv[:, 0:rows],
                                    scalar1=float(scale), scalar2=None,
                                    op0=ALU.mult).then_inc(s_dve.h, 1)
                s_dve.n += 1
            block.flush()

        phase_barrier()

        with ExitStack() as es1:
            BW = max(MC * FSH, KC * NSH)
            b1 = es1.enter_context(nc.sbuf_tensor("b1", [128, BW], F32))
            b2 = es1.enter_context(nc.sbuf_tensor("b2", [128, BW], F32))
            wtb = es1.enter_context(nc.sbuf_tensor("wtb", [128, BW], BF16))
            wrow = [es1.enter_context(nc.sbuf_tensor(f"wrow{i}", [128, max(DM, INTER)],
                                                     BF16)) for i in range(2)]
            fout = [es1.enter_context(nc.sbuf_tensor(f"fout{i}", [128, max(FSH, NSH)],
                                                     BF16)) for i in range(2)]
            ptr4 = [es1.enter_context(nc.psum_tensor(f"ptr4{i}", [128, 512], BF16))
                    for i in range(2)]
            ph1 = [es1.enter_context(nc.psum_tensor(f"ph1{i}", [128, 1024], F32))
                   for i in range(2)]

            g_rt = [0]
            tr_pe_by_rt = {}
            g_cc = [0]

            def transform(w_src, n_rows_tiles, n_chunks, chunk_w, row_w, sign_tile,
                          scale_rows_ap, ag_write):
                tr_evac = {}
                tgi = [0]
                for r in range(n_rows_tiles):
                    rsl = g_rt[0] % 2
                    wsl = wrow[rsl]
                    dg = d_gl2[rsl]

                    @block.gpsimd
                    def _(g, r=r, wsl=wsl, dg=dg, pen=tr_pe_by_rt.get(g_rt[0] - 2, 0)):
                        g.wait_ge(s_pe.h, pen)
                        g.dma_start(out=wsl[:, 0:row_w],
                                    in_=w_src[r * 128:(r + 1) * 128, :]).then_inc(dg.h, 16)
                        drip_cast(g)
                    dg.n += 16

                    if scale_rows_ap is not None:
                        @block.vector
                        def _(v, r=r, wsl=wsl, dgn=dg.n, dg=dg):
                            v.wait_ge(dg.h, dgn)
                            v.tensor_scalar(out=wsl[:, 0:row_w],
                                            in0=wsl[:, 0:row_w],
                                            scalar1=scale_rows_ap(r),
                                            scalar2=None, op0=ALU.mult
                                            ).then_inc(s_dve.h, 1)
                        s_dve.n += 1

                    nch = row_w // 128
                    for cc4 in range(0, nch, 4):
                        ng4 = min(4, nch - cc4)
                        gslot = tgi[0] % 2

                        @block.tensor
                        def _(t, r=r, wsl=wsl, dg=dg, cc4=cc4, ng4=ng4, gslot=gslot,
                              dgn=dg.n, dvn=s_dve.n, dcn=d_c.n,
                              pfree=tr_evac.get(tgi[0] - 2, 0)):
                            t.wait_ge(d_c.h, dcn)
                            if scale_rows_ap is not None:
                                t.wait_ge(s_dve.h, dvn)
                            else:
                                t.wait_ge(dg.h, dgn)
                            t.wait_ge(s_act.h, pfree)
                            for i in range(ng4):
                                cc = cc4 + i
                                last = t.transpose(ptr4[gslot][:, i * 128:(i + 1) * 128],
                                                   wsl[:, cc * 128:(cc + 1) * 128],
                                                   identb[:])
                            last.then_inc(s_pe.h, 1)
                        s_pe.n += 1

                        @block.scalar
                        def _(s, r=r, cc4=cc4, ng4=ng4, gslot=gslot, pen=s_pe.n):
                            s.wait_ge(s_pe.h, pen)
                            for i in range(ng4):
                                cc = cc4 + i
                                last = s.activation(
                                    wtb[:, cc * chunk_w + r * 128:
                                        cc * chunk_w + (r + 1) * 128],
                                    ptr4[gslot][:, i * 128:(i + 1) * 128], AF.Copy)
                            last.then_inc(s_act.h, 1)
                        s_act.n += 1
                        tr_evac[tgi[0]] = s_act.n
                        tgi[0] += 1
                    tr_pe_by_rt[g_rt[0]] = s_pe.n
                    g_rt[0] += 1
                h_evac = {}
                hgi = [0]
                for cc in range(n_chunks):
                    for off, w in cgroups(chunk_w, 1024):
                        gslot = hgi[0] % 2

                        @block.tensor
                        def _(t, cc=cc, off=off, w=w, gslot=gslot, acn=s_act.n,
                              dvn=h_evac.get(hgi[0] - 2, 0)):
                            t.wait_ge(s_act.h, acn)
                            t.wait_ge(s_dve.h, dvn)
                            for o2, w2 in cgroups(w, 512):
                                last = t.matmul(
                                    ph1[gslot][:, o2:o2 + w2], h128[:],
                                    wtb[:, cc * chunk_w + off + o2:
                                        cc * chunk_w + off + o2 + w2],
                                    start=True, stop=True)
                            last.then_inc(s_pe.h, 1)
                        s_pe.n += 1

                        @block.vector
                        def _(v, cc=cc, off=off, w=w, gslot=gslot, pen=s_pe.n):
                            v.wait_ge(s_pe.h, pen)
                            v.tensor_copy(
                                out=b1[:, cc * chunk_w + off: cc * chunk_w + off + w],
                                in_=ph1[gslot][:, 0:w]).then_inc(s_dve.h, 1)
                        s_dve.n += 1
                        h_evac[hgi[0]] = s_dve.n
                        hgi[0] += 1
                nst = int(round(math.log2(n_chunks)))
                hw = chunk_w // 2

                def bfly(eng, sem, c0, c1):
                    cur, nxt = b1, b2
                    for st in range(nst):
                        stride = 1 << st
                        cv = cur[:, 0:n_chunks * chunk_w].rearrange(
                            "p (g two s w) -> p g two s w",
                            two=2, s=stride, w=chunk_w)
                        nv = nxt[:, 0:n_chunks * chunk_w].rearrange(
                            "p (g two s w) -> p g two s w",
                            two=2, s=stride, w=chunk_w)
                        eng.tensor_tensor(out=nv[:, :, 0, :, c0:c1],
                                          in0=cv[:, :, 0, :, c0:c1],
                                          in1=cv[:, :, 1, :, c0:c1],
                                          op=ALU.add)
                        eng.tensor_tensor(out=nv[:, :, 1, :, c0:c1],
                                          in0=cv[:, :, 0, :, c0:c1],
                                          in1=cv[:, :, 1, :, c0:c1],
                                          op=ALU.subtract)
                        eng.drain()
                        cur, nxt = nxt, cur
                    eng.drain().then_inc(sem, 1)

                @block.vector
                def _(v, dvn=s_dve.n, pon=s_pool.n):
                    v.wait_ge(s_dve.h, dvn)
                    v.wait_ge(s_pool.h, pon)
                    bfly(v, s_dve.h, 0, hw)
                s_dve.n += 1

                @block.gpsimd
                def _(g, dvn=s_dve.n - 1, pon=s_pool.n):
                    g.wait_ge(s_dve.h, dvn)
                    g.wait_ge(s_pool.h, pon)
                    bfly(g, s_pool.h, hw, chunk_w)
                s_pool.n += 1
                cur = b1 if nst % 2 == 0 else b2
                nxt = b2 if nst % 2 == 0 else b1
                for cc in range(n_chunks):
                    fsl = g_cc[0] % 2
                    fo = fout[fsl]

                    @block.vector
                    def _(v, cc=cc, cur=cur, fo=fo, fsl=fsl, dvn=s_dve.n,
                          pon=s_pool.n, dan=d_fo[fsl].n):
                        v.wait_ge(s_dve.h, dvn)
                        v.wait_ge(s_pool.h, pon)
                        v.wait_ge(d_fo[fsl].h, dan)
                        v.tensor_scalar(
                            out=fo[:, 0:chunk_w], in0=cur[:, cc * chunk_w:(cc + 1) * chunk_w],
                            scalar1=sign_tile[:, cc:cc + 1], scalar2=None,
                            op0=ALU.mult).then_inc(s_dve.h, 1)
                    s_dve.n += 1

                    @block.gpsimd
                    def _(g, cc=cc, fo=fo, fsl=fsl, dvn=s_dve.n):
                        g.wait_ge(s_dve.h, dvn)
                        ag_write(g, cc, fo, d_fo[fsl].h)
                    d_fo[fsl].n += 16
                    g_cc[0] += 1

            for w_i, (w_src, wsrows, base) in enumerate(
                    ((wup_in, ws_up, 0), (wgate_in, ws_gt, JL * MC * 128))):
                def agw_ug(g, mc, fo, semh, base=base):
                    dst = ag1u_i[base:base + JL * MC * 128, :]
                    dst = dst.rearrange("(jl m p) c -> p jl m c", jl=JL, m=MC)
                    g.dma_start(out=dst[:, :, mc, :],
                                in_=fo[:, 0:FSH].rearrange("p (jl c) -> p jl c", jl=JL)
                                ).then_inc(semh, 16)
                transform(w_src, JL, MC, FSH, DM, su_ug,
                          (lambda r, wsrows=wsrows: wsrows[:, r:r + 1]), agw_ug)
            def agw_dn(g, cc, fo, semh):
                g.dma_start(out=ag1d_i[cc * 128:(cc + 1) * 128, :],
                            in_=fo[:, 0:NSH]).then_inc(semh, 16)
            transform(wdown_in, NST, KC, NSH, INTER, su_dp,
                      (lambda r: ws_dp[:, r:r + 1]), agw_dn)

            if bg_casts:
                @block.gpsimd
                def _(g):
                    while bg_casts:
                        bg_casts.pop(0)(g)

            block.flush()

        @block.gpsimd
        def _(g, f0=d_fo[0].n, f1=d_fo[1].n):
            g.wait_ge(d_fo[0].h, f0)
            g.wait_ge(d_fo[1].h, f1)
            g.collective_compute("AllGather", ALU.bypass,
                                 replica_groups=[list(range(n_cores))],
                                 ins=[ag1u_i.opt()],
                                 outs=[ag1u_o.opt()]).then_inc(s_cc.h, 1)
            g.collective_compute("AllGather", ALU.bypass,
                                 replica_groups=[list(range(n_cores))],
                                 ins=[ag1d_i.opt()],
                                 outs=[ag1d_o.opt()]).then_inc(s_cc.h, 1)
        s_cc.n += 2
        CC1 = s_cc.n
        XBF_LD2 = d_xb2.n

        phase_barrier()

        with ExitStack() as es2:
            wqkv_sb = es2.enter_context(nc.sbuf_tensor("wqkv_sb", [128, MC * 3 * VC], BF16))
            xfm = [es2.enter_context(nc.sbuf_tensor(f"xfm{i}", [128, MC * TG], BF16))
                   for i in range(2)]
            ev = es2.enter_context(nc.sbuf_tensor("ev", [128, 4 * TG], BF16))
            p_qk = [es2.enter_context(nc.psum_tensor(f"p_qk{i}", [128, TG], F32))
                    for i in range(2)]
            p_v = [es2.enter_context(nc.psum_tensor(f"p_v{i}", [128, 512], F32))
                   for i in range(2)]

            nstrip = es2.enter_context(nc.sbuf_tensor("nstrip", [1, 2 * NH * NTG], F32))
            nsq2 = [es2.enter_context(nc.sbuf_tensor(f"nsq2{i}", [128, TG], BF16))
                    for i in range(3)]
            p_nn2 = [es2.enter_context(nc.psum_tensor(f"p_nn2{i}", [128, 512], F32))
                     for i in range(3)]
            nrm_ctr = [0]
            nsq_dve = {}
            nred_dve = {}

            @block.sync
            def _(sy, wcn=d_wc.n):
                sy.wait_ge(d_wc.h, wcn)
                src = wqkvbf.rearrange("(m p) f -> p m f", p=128)
                dst = wqkv_sb[:].rearrange("p (m f) -> p m f", m=MC)
                sy.dma_start(out=dst[:, 0:MC // 2, :],
                             in_=src[:, 0:MC // 2, :]).then_inc(d_wl[0].h, 16)
                sy.dma_start(out=dst[:, MC // 2:MC, :],
                             in_=src[:, MC // 2:MC, :]).then_inc(d_wl[1].h, 16)
            d_wl[0].n += 16
            d_wl[1].n += 16
            WQKV_LD = 16

            ev_cnt = [0]
            ev_slot_uses = [0, 0, 0, 0]
            ev_slot_sq = {}
            last_act_for_pqk = {}
            last_act_for_pv = {}

            def load_xfm(tg):
                buf = xfm[tg % 2]
                sem = d_x[tg % 2]

                @block.sync
                def _(sy, tg=tg, buf=buf, sem=sem, pen=s_pe.n):
                    sy.wait_ge(s_pe.h, pen)
                    if (tg + 1) * TG <= S // 2:
                        sy.wait_ge(d_xb.h, XBF_LD1)
                    else:
                        sy.wait_ge(d_xb2.h, XBF_LD2)
                    for mc in range(MC):
                        sy.dma_start_transpose(
                            out=buf[:, mc * TG:(mc + 1) * TG],
                            in_=xbf[tg * TG:(tg + 1) * TG, mc * 128:(mc + 1) * 128]
                        ).then_inc(sem.h, 16)
                sem.n += 16 * MC

            def qkv_tile(kind, idx, tg, lhs_fn, rhs_fn, psl, last_act_map, scale, dst_fn):
                p = psl[idx % 2]

                @block.tensor
                def _(t, p=p, kind=kind, idx=idx, tg=tg,
                      xn=d_x[tg % 2].n, wq=WQKV_LD,
                      pfree=last_act_map.get(idx % 2, 0)):
                    t.wait_ge(d_x[tg % 2].h, xn)
                    t.wait_ge(d_wl[0].h, 16)
                    t.wait_ge(s_act.h, pfree)
                    for mc in range(MC):
                        if mc == MC // 2:
                            t.wait_ge(d_wl[1].h, wq)
                        last = t.matmul(p[:, 0:rhs_w], lhs_fn(t, mc), rhs_fn(t, mc),
                                        start=(mc == 0), stop=(mc == MC - 1))
                    last.then_inc(s_pe.h, 1)
                s_pe.n += 1

                slot = ev_cnt[0] % 4

                @block.scalar
                def _(s, p=p, slot=slot, pen=s_pe.n, su=ev_slot_uses[slot],
                      sqfree=ev_slot_sq.get(slot, None)):
                    s.wait_ge(s_pe.h, pen)
                    s.wait_ge(d_e[slot].h, su * 16)
                    if sqfree is not None:
                        s.wait_ge(s_dve.h, sqfree)
                    s.activation(ev[:, slot * TG: slot * TG + rhs_w],
                                 p[:, 0:rhs_w], AF.Copy, scale=scale
                                 ).then_inc(s_act.h, 1)
                s_act.n += 1
                last_act_map[idx % 2] = s_act.n

                @block.sync
                def _(sy, slot=slot, acn=s_act.n):
                    sy.wait_ge(s_act.h, acn)
                    dst_fn(sy, ev[:, slot * TG: slot * TG + rhs_w], d_e[slot])
                d_e[slot].n += 16
                if kind == "qk":
                    u = nrm_ctr[0]

                    @block.vector
                    def _(v, slot=slot, u=u, acn=s_act.n,
                          sqf=nsq_dve.get(u - 3, 0), pef=s_pe.n):
                        v.wait_ge(s_act.h, acn)
                        v.wait_ge(s_pe.h, pef)
                        v.tensor_tensor(out=nsq2[u % 3][:],
                                        in0=ev[:, slot * TG: (slot + 1) * TG],
                                        in1=ev[:, slot * TG: (slot + 1) * TG],
                                        op=ALU.mult).then_inc(s_dve.h, 1)
                    s_dve.n += 1
                    nsq_dve[u] = s_dve.n
                    ev_slot_sq[slot] = s_dve.n

                    @block.tensor
                    def _(t, u=u, dvn=s_dve.n, rfree=nred_dve.get(u - 3, 0)):
                        t.wait_ge(s_dve.h, dvn)
                        t.wait_ge(s_dve.h, rfree)
                        t.matmul(p_nn2[u % 3][:], ones128[:], nsq2[u % 3][:],
                                 start=True, stop=True).then_inc(s_pe.h, 1)
                    s_pe.n += 1

                    @block.vector
                    def _(v, u=u, pen=s_pe.n):
                        v.wait_ge(s_pe.h, pen)
                        v.tensor_reduce(out=nstrip[0:1, u:u + 1],
                                        in_=p_nn2[u % 3][0:1, :], axis=AXL.X,
                                        op=ALU.max).then_inc(s_dve.h, 1)
                    s_dve.n += 1
                    nred_dve[u] = s_dve.n
                    nrm_ctr[0] += 1
                ev_slot_uses[slot] += 1
                ev_cnt[0] += 1

            load_xfm(0)
            for tg in range(NTG):
                if tg + 1 < NTG:
                    load_xfm(tg + 1)
                buf = xfm[tg % 2]
                rhs_w = TG
                for j in range(2 * NH):
                    def dst_qk(g, src_ap, dsem, j=j, tg=tg):
                        scr = qscr if j < NH else kscr
                        hh = j % NH
                        g.dma_start(out=scr[hh * 128:(hh + 1) * 128, tg * TG:(tg + 1) * TG],
                                    in_=src_ap).then_inc(dsem.h, 16)
                    qkv_tile("qk", j, tg,
                             (lambda t, mc, j=j: wqkv_sb[:, mc * 3 * VC + j * 128:
                                                         mc * 3 * VC + (j + 1) * 128]),
                             (lambda t, mc, buf=buf: buf[:, mc * TG:(mc + 1) * TG]),
                             p_qk, last_act_for_pqk,
                             Q_SCALE if j < NH else 1.0, dst_qk)
                rhs_w = 512
                vi = 0
                for tt in range(TG // 128):
                    for fh in range(VC // 512):
                        def dst_v(g, src_ap, dsem, tt=tt, fh=fh, tg=tg):
                            cc0 = (tg * TG + tt * 128) // 128
                            dst = vscr.rearrange("(h p cc) d -> p h cc d",
                                                 h=NH, p=128)
                            g.dma_start(out=dst[:, fh * 4:(fh + 1) * 4, cc0, :],
                                        in_=src_ap.rearrange("p (h d) -> p h d", d=128)
                                        ).then_inc(dsem.h, 16)
                        qkv_tile("v", vi, tg,
                                 (lambda t, mc, buf=buf, tt=tt:
                                  buf[:, mc * TG + tt * 128: mc * TG + (tt + 1) * 128]),
                                 (lambda t, mc, fh=fh:
                                  wqkv_sb[:, mc * 3 * VC + 2 * VC + fh * 512:
                                          mc * 3 * VC + 2 * VC + (fh + 1) * 512]),
                                 p_v, last_act_for_pv, 1.0, dst_v)
                        vi += 1

            with ExitStack() as esn:
                nrm = esn.enter_context(nc.sbuf_tensor("nrm", [1, 6 * NH], F32))
                p_bb = esn.enter_context(nc.psum_tensor("p_bb", [128, NH], F32))

                @block.vector
                def _(v, dvn=s_dve.n):
                    v.wait_ge(s_dve.h, dvn)
                    v.drain()
                    for h in range(NH):
                        v.tensor_reduce(out=nrm[0:1, h:h + 1],
                                        in_=nstrip[0:1, h:2 * NH * NTG:2 * NH],
                                        axis=AXL.X, op=ALU.max)
                        v.tensor_reduce(out=nrm[0:1, NH + h:NH + h + 1],
                                        in_=nstrip[0:1, NH + h:2 * NH * NTG:2 * NH],
                                        axis=AXL.X, op=ALU.max)
                    v.drain()
                    v.tensor_tensor(out=nrm[0:1, 2 * NH:3 * NH],
                                    in0=nrm[0:1, 0:NH],
                                    in1=nrm[0:1, NH:2 * NH],
                                    op=ALU.mult).then_inc(s_dve.h, 1)
                s_dve.n += 1

                @block.scalar
                def _(s, dvn=s_dve.n):
                    s.wait_ge(s_dve.h, dvn)
                    s.activation(nrm[0:1, 3 * NH:4 * NH], nrm[0:1, 2 * NH:3 * NH],
                                 AF.Ln)
                    s.drain()
                    s.activation(nrm[0:1, 4 * NH:5 * NH], nrm[0:1, 3 * NH:4 * NH],
                                 AF.Exp, scale=0.5).then_inc(s_act.h, 1)
                s_act.n += 1

                @block.vector
                def _(v, acn=s_act.n):
                    v.wait_ge(s_act.h, acn)
                    v.tensor_scalar(out=nrm[0:1, 5 * NH:6 * NH],
                                    in0=nrm[0:1, 4 * NH:5 * NH],
                                    scalar1=-1.0, scalar2=None,
                                    op0=ALU.mult).then_inc(s_dve.h, 1)
                s_dve.n += 1

                @block.tensor
                def _(t, dvn=s_dve.n):
                    t.wait_ge(s_dve.h, dvn)
                    t.matmul(p_bb[:], onesr[:], nrm[0:1, 5 * NH:6 * NH],
                             start=True, stop=True).then_inc(s_pe.h, 1)
                s_pe.n += 1

                @block.vector
                def _(v, pen=s_pe.n):
                    v.wait_ge(s_pe.h, pen)
                    v.tensor_copy(out=ubias[:, 0:NH],
                                  in_=p_bb[:]).then_inc(s_dve.h, 1)
                s_dve.n += 1
                block.flush()

        phase_barrier()

        with ExitStack() as es3:
            K_h = [es3.enter_context(nc.sbuf_tensor(f"K_h{i}", [128, S], BF16)) for i in range(2)]
            Q_h = [es3.enter_context(nc.sbuf_tensor(f"Q_h{i}", [128, S], BF16)) for i in range(2)]
            V_h = [es3.enter_context(nc.sbuf_tensor(f"V_h{i}", [128, S], BF16)) for i in range(2)]
            A_t = es3.enter_context(nc.sbuf_tensor("A_t", [128, 4 * 1024], BF16))
            r_ln = es3.enter_context(nc.sbuf_tensor("r_ln", [128, 512], F32))
            r_rec = es3.enter_context(nc.sbuf_tensor("r_rec", [128, 2 * 512], F32))
            O_t = es3.enter_context(nc.sbuf_tensor("O_t", [128, 2 * 512], BF16))
            p_S = [es3.enter_context(nc.psum_tensor(f"p_S{i}", [128, 1024], F32))
                   for i in range(3)]
            p_O = es3.enter_context(nc.psum_tensor("p_O", [128, 512], F32))
            p_rb = es3.enter_context(nc.psum_tensor("p_rb", [128, 512], F32))

            head_done_pe = {}
            exp_act = {}
            cons_pe = {}
            ep = {}
            gp_ctr = [0]
            tl_ctr = [0]

            def load_head(h):
                par = h % 2
                sem = d_k[par]

                @block.gpsimd
                def _(g, h=h, par=par, sem=sem,
                      free=head_done_pe.get(h - 2, 0),
                      devals=[(dd.h, dd.n) for dd in d_e] if h < 2 else []):
                    g.wait_ge(s_pe.h, free)
                    for hh2, vv2 in devals:
                        g.wait_ge(hh2, vv2)
                    g.dma_start(out=K_h[par][:], in_=kscr[h * 128:(h + 1) * 128, :]
                                ).then_inc(sem.h, 16)
                    g.dma_start(out=Q_h[par][:], in_=qscr[h * 128:(h + 1) * 128, :]
                                ).then_inc(sem.h, 16)
                    vsrc = vscr.rearrange("(h p cc) d -> h p cc d", h=NH, p=128)
                    vdst = V_h[par][:].rearrange("p (cc d) -> p cc d", d=128)
                    g.dma_start(out=vdst, in_=vsrc[h]).then_inc(sem.h, 16)
                sem.n += 48

            def emit_epilogue(tile):
                if tile not in ep or ep[tile].get("done"):
                    return
                e = ep[tile]
                e["done"] = True

                @block.vector
                def _(v, e=e, don=d_o[e["tp"]].n):
                    v.wait_ge(s_act.h, e["recip"])
                    v.wait_ge(d_o[e["tp"]].h, don)
                    v.tensor_tensor(out=O_t[:, e["tp"] * 512:(e["tp"] + 1) * 512],
                                    in0=p_O[:],
                                    in1=r_rec[:, e["tp"] * 512:(e["tp"] + 1) * 512],
                                    op=ALU.mult).then_inc(s_dve.h, 1)
                s_dve.n += 1
                e["mult"] = s_dve.n

                @block.gpsimd
                def _(g, e=e):
                    g.wait_ge(s_dve.h, e["mult"])
                    g.dma_start(out=ag2_i[e["h"] * 128:(e["h"] + 1) * 128,
                                          e["q0"]:e["q0"] + 512],
                                in_=O_t[:, e["tp"] * 512:(e["tp"] + 1) * 512]
                                ).then_inc(d_o[e["tp"]].h, 16)
                d_o[e["tp"]].n += 16

            load_head(0)
            if NH > 1:
                load_head(1)
            for h in range(NH):
                par = h % 2
                ksem_val = d_k[par].n
                for qt in range(NQT):
                    tile = tl_ctr[0]
                    tp = tile % 2
                    q0 = qt * QT
                    cmax = (qt + 1) * CPT
                    G = cmax // 2
                    gp0 = gp_ctr[0]

                    emit_epilogue(tile - 1)

                    def s_pair(g, h=h, par=par, q0=q0, gp0=gp0, G=G, ksem_val=ksem_val):
                        if g >= G:
                            return
                        gp = gp0 + g

                        @block.tensor
                        def _(t, g=g, gp=gp,
                              psfree=exp_act.get(gp - 3, 0)):
                            t.wait_ge(s_act.h, psfree)
                            if g == 0:
                                t.wait_ge(d_k[par].h, ksem_val)
                            for ci in range(2):
                                c = 2 * g + ci
                                last = t.matmul(p_S[gp % 3][:, ci * 512:(ci + 1) * 512],
                                                K_h[par][:, c * 128:(c + 1) * 128],
                                                Q_h[par][:, q0:q0 + 512],
                                                start=True, stop=True)
                            last.then_inc(s_pe.h, 1)
                        s_pe.n += 1
                        return s_pe.n

                    smm = {}
                    smm[0] = s_pair(0)
                    if G > 1:
                        smm[1] = s_pair(1)
                    if G > 2:
                        smm[2] = s_pair(2)
                    for g in range(G):
                        gp = gp0 + g
                        @block.scalar
                        def _(s, g=g, gp=gp, h=h, sv=smm[g],
                              afree=cons_pe.get(gp - 4, 0)):
                            s.wait_ge(s_pe.h, sv)
                            s.wait_ge(s_pe.h, afree)
                            s.activation(
                                A_t[:, (gp % 4) * 1024:(gp % 4) * 1024 + 1024],
                                p_S[gp % 3][:], AF.Exp,
                                bias=ubias[:, h:h + 1]).then_inc(s_act.h, 1)
                        s_act.n += 1
                        exp_act[gp] = s_act.n
                        gate_sem, gate_val = s_act, s_act.n
                        if 2 * g >= cmax - CPT:
                            @block.vector
                            def _(v, g=g, gp=gp, qt=qt, cmax=cmax, acn=s_act.n):
                                v.wait_ge(s_act.h, acn)
                                for ci in range(2):
                                    c = 2 * g + ci
                                    jj = c - (cmax - CPT)
                                    sl = A_t[:, (gp % 4) * 1024 + ci * 512:
                                             (gp % 4) * 1024 + (ci + 1) * 512]
                                    last = v.tensor_tensor(out=sl, in0=sl,
                                                           in1=maskt[:, MOFF - 128 * jj:
                                                                     MOFF - 128 * jj + QT],
                                                           op=ALU.mult)
                                last.then_inc(s_dve.h, 1)
                            s_dve.n += 1
                            gate_sem, gate_val = s_dve, s_dve.n
                        @block.tensor
                        def _(t, g=g, gp=gp, tile=tile, tp=tp, par=par, cmax=cmax,
                              gs=gate_sem, gv=gate_val,
                              pofree=(ep.get(tile - 1, {}).get("mult", 0)),
                              prfree=(ep.get(tile - 1, {}).get("recip", 0))):
                            t.wait_ge(gs.h, gv)
                            if g == 0:
                                t.wait_ge(s_dve.h, pofree)
                                t.wait_ge(s_act.h, prfree)
                            for ci in range(2):
                                c = 2 * g + ci
                                sl = A_t[:, (gp % 4) * 1024 + ci * 512:
                                         (gp % 4) * 1024 + (ci + 1) * 512]
                                t.matmul(p_rb[:], ones128[:], sl,
                                         start=(c == 0), stop=(c == cmax - 1))
                                last = t.matmul(p_O[:],
                                                V_h[par][:, c * 128:(c + 1) * 128],
                                                sl, start=(c == 0), stop=(c == cmax - 1))
                            last.then_inc(s_pe.h, 1)
                        s_pe.n += 1
                        cons_pe[gp] = s_pe.n
                        if g + 3 < G:
                            smm[g + 3] = s_pair(g + 3)
                    @block.scalar
                    def _(s, tp=tp, pen=s_pe.n,
                          rfree=(ep.get(tile - 2, {}).get("mult", 0))):
                        s.wait_ge(s_pe.h, pen)
                        s.wait_ge(s_dve.h, rfree)
                        s.activation(r_ln[:], p_rb[:], AF.Ln)
                        s.drain()
                        s.activation(r_rec[:, tp * 512:(tp + 1) * 512], r_ln[:],
                                     AF.Exp, scale=-1.0).then_inc(s_act.h, 1)
                    s_act.n += 1
                    ep[tile] = {"recip": s_act.n, "tp": tp, "h": h, "q0": q0}
                    gp_ctr[0] += G
                    tl_ctr[0] += 1
                head_done_pe[h] = s_pe.n
                if h + 2 < NH:
                    load_head(h + 2)
                if h == NH // 2 - 1:
                    emit_epilogue(tl_ctr[0] - 1)

                    @block.gpsimd
                    def _(g, don0=d_o[0].n, don1=d_o[1].n):
                        g.wait_ge(d_o[0].h, don0)
                        g.wait_ge(d_o[1].h, don1)
                        g.collective_compute(
                            "AllGather", ALU.bypass,
                            replica_groups=[list(range(n_cores))],
                            ins=[ag2_i[0:HH * 128, :].opt()],
                            outs=[ag2_oa.opt()]).then_inc(s_cc.h, 1)
                    s_cc.n += 1
            emit_epilogue(tl_ctr[0] - 1)
            block.flush()

        CC2A = s_cc.n

        @block.gpsimd
        def _(g, don0=d_o[0].n, don1=d_o[1].n):
            g.wait_ge(d_o[0].h, don0)
            g.wait_ge(d_o[1].h, don1)
            g.collective_compute("AllGather", ALU.bypass,
                                 replica_groups=[list(range(n_cores))],
                                 ins=[ag2_i[HH * 128:NH * 128, :].opt()],
                                 outs=[ag2_ob.opt()]).then_inc(s_cc.h, 1)
        s_cc.n += 1
        CC2 = s_cc.n

        phase_barrier()

        with ExitStack() as es4:
            o_sb = es4.enter_context(nc.sbuf_tensor("o_sb", [128, 2 * NH * SH], BF16))
            wout_sb = es4.enter_context(nc.sbuf_tensor("wout_sb", [128, MC * DM], BF16))
            xh_sb = [es4.enter_context(nc.sbuf_tensor(f"xh_sb{i}", [128, DM], F32))
                     for i in range(2)]
            upre = [es4.enter_context(nc.sbuf_tensor(f"upre{i}", [128, DM], F32))
                    for i in range(2)]
            sqscr = es4.enter_context(nc.sbuf_tensor("sqscr", [128, DM], F32))
            statp = es4.enter_context(nc.sbuf_tensor("statp", [128, 8], F32))
            u_t = [es4.enter_context(nc.sbuf_tensor(f"u_t{i}", [128, DM], BF16))
                   for i in range(2)]
            p_pr = [es4.enter_context(nc.psum_tensor(f"p_pr{i}", [128, DM], F32))
                    for i in range(2)]

            @block.gpsimd
            def _(g, cca=CC2A, ccb=CC2):
                g.dma_start(out=wout_sb[:].rearrange("p (m f) -> p m f", m=MC),
                            in_=wout_in.rearrange("(m p) f -> p m f", p=128)
                            ).then_inc(d_ob.h, 16)
                pid = g.compute_val(nc.partition_id())
                with g.register("sibr") as rs, g.register("ownb") as ro, \
                        g.register("sibb") as rb, g.register("colr") as rc:
                    g.reg_alu(rs, pid, 1, ALU.bitwise_xor)
                    g.reg_alu(ro, pid, HH * 128, ALU.mult)
                    g.reg_alu(rb, rs, HH * 128, ALU.mult)
                    g.reg_alu(rc, pid, 1, ALU.bitwise_and)
                    g.reg_alu(rc, rc, SH, ALU.mult)
                    col_rv = RuntimeValue(rc)
                    for hb, (agt, ccv) in enumerate(((ag2_oa, cca), (ag2_ob, ccb))):
                        g.wait_ge(s_cc.h, ccv)
                        sem = d_ob if hb == 0 else d_ob2
                        for si, rreg in ((0, ro), (1, rb)):
                            src = agt[ds(RuntimeValue(rreg), HH * 128), ds(col_rv, SH)]
                            src = src.rearrange("(hh p) q -> p hh q", p=128)
                            dst = o_sb[:, (si * NH + hb * HH) * SH:
                                       (si * NH + (hb + 1) * HH) * SH]
                            dst = dst.rearrange("p (hh q) -> p hh q", hh=HH)
                            g.dma_start(out=dst, in_=src).then_inc(sem.h, 16)
            d_ob.n += 16 + 32
            d_ob2.n += 32
            OSB_A = d_ob.n
            OSB_LD = d_ob2.n

            ln_evac = {}
            norm_act = {}
            for ti in range(NT16):
                tpp = ti % 2

                @block.sync
                def _(sy, ti=ti, tpp=tpp, free=ln_evac.get(ti - 2, 0)):
                    sy.wait_ge(s_dve.h, free)
                    sy.dma_start(out=xh_sb[tpp][:], in_=xh_in[ti * 128:(ti + 1) * 128, :]
                                 ).then_inc(d_xh[tpp].h, 16)
                d_xh[tpp].n += 16

                @block.tensor
                def _(t, ti=ti, tpp=tpp, oba=OSB_A, obn=OSB_LD,
                      free=ln_evac.get(ti - 2, 0)):
                    t.wait_ge(d_ob.h, oba)
                    t.wait_ge(s_dve.h, free)
                    iorder = ([si * NH + hh for si in range(2) for hh in range(HH)]
                              + [si * NH + HH + hh for si in range(2)
                                 for hh in range(HH)])
                    for k, i in enumerate(iorder):
                        if k == NH:
                            t.wait_ge(d_ob2.h, obn)
                        for ng in range(NGO):
                            last = t.matmul(
                                p_pr[tpp][:, ng * 512:(ng + 1) * 512],
                                o_sb[:, i * SH + ti * 128: i * SH + (ti + 1) * 128],
                                wout_sb[:, i * DM + ng * 512: i * DM + (ng + 1) * 512],
                                start=(k == 0), stop=(k == 2 * NH - 1),
                                skip_group_check=True)
                    last.then_inc(s_pe.h, 1)
                s_pe.n += 1

                @block.vector
                def _(v, ti=ti, tpp=tpp, pen=s_pe.n, xn=d_xh[tpp].n,
                      nfree=norm_act.get(ti - 2, 0)):
                    v.wait_ge(s_pe.h, pen)
                    v.wait_ge(d_xh[tpp].h, xn)
                    v.wait_ge(s_dve.h, nfree)
                    v.drain()
                    v.tensor_tensor(out=upre[tpp][:], in0=p_pr[tpp][:], in1=xh_sb[tpp][:],
                                    op=ALU.add)
                    v.drain()
                    v.tensor_reduce(out=statp[:, 0:1], in_=upre[tpp][:],
                                    axis=AXL.X, op=ALU.add,
                                    negate=True).then_inc(s_dve.h, 1)
                s_dve.n += 1
                ln_evac[ti] = s_dve.n

                @block.vector
                def _(v, tpp=tpp, dvn=s_dve.n):
                    v.wait_ge(s_dve.h, dvn)
                    v.drain()
                    v.tensor_tensor(out=sqscr[:], in0=upre[tpp][:], in1=upre[tpp][:],
                                    op=ALU.mult)
                    v.drain()
                    v.tensor_reduce(out=statp[:, 1:2], in_=sqscr[:], axis=AXL.X,
                                    op=ALU.add)
                    v.drain()
                    v.tensor_scalar(out=statp[:, 2:3], in0=statp[:, 0:1],
                                    scalar1=1.0 / DM, scalar2=None, op0=ALU.mult)
                    v.tensor_scalar(out=statp[:, 3:4], in0=statp[:, 1:2],
                                    scalar1=1.0 / DM, scalar2=None, op0=ALU.mult)
                    v.drain()
                    v.tensor_tensor(out=statp[:, 4:5], in0=statp[:, 2:3],
                                    in1=statp[:, 2:3], op=ALU.mult)
                    v.drain()
                    v.tensor_tensor(out=statp[:, 5:6], in0=statp[:, 3:4],
                                    in1=statp[:, 4:5], op=ALU.subtract)
                    v.drain()
                    v.tensor_scalar(out=statp[:, 5:6], in0=statp[:, 5:6],
                                    scalar1=EPS, scalar2=None,
                                    op0=ALU.add).then_inc(s_dve.h, 1)
                s_dve.n += 1

                @block.scalar
                def _(s, dvn=s_dve.n):
                    s.wait_ge(s_dve.h, dvn)
                    s.drain()
                    s.activation(statp[:, 6:7], statp[:, 5:6], AF.Ln)
                    s.drain()
                    s.activation(statp[:, 7:8], statp[:, 6:7], AF.Exp,
                                 scale=-0.5).then_inc(s_act.h, 1)
                s_act.n += 1

                @block.vector
                def _(v, acn=s_act.n):
                    v.wait_ge(s_act.h, acn)
                    v.drain()
                    v.tensor_tensor(out=statp[:, 2:3], in0=statp[:, 2:3],
                                    in1=statp[:, 7:8],
                                    op=ALU.mult).then_inc(s_dve.h, 1)
                s_dve.n += 1

                @block.vector
                def _(v, ti=ti, tpp=tpp, dvn=s_dve.n, dun=d_u[tpp].n):
                    v.wait_ge(s_dve.h, dvn)
                    v.wait_ge(d_u[tpp].h, dun)
                    v.drain()
                    v.tensor_scalar(out=u_t[tpp][:], in0=upre[tpp][:],
                                    scalar1=statp[:, 7:8], scalar2=statp[:, 2:3],
                                    op0=ALU.mult, op1=ALU.add).then_inc(s_dve.h, 1)
                s_dve.n += 1

                norm_act[ti] = s_dve.n

                @block.gpsimd
                def _(g, ti=ti, tpp=tpp, acn=s_dve.n):
                    g.wait_ge(s_dve.h, acn)
                    g.dma_start(out=uscr[ti * 128:(ti + 1) * 128, :], in_=u_t[tpp][:]
                                ).then_inc(d_u[tpp].h, 16)
                d_u[tpp].n += 16
            block.flush()

        phase_barrier()

        with ExitStack() as es5:
            hbuf = es5.enter_context(nc.sbuf_tensor("hbuf", [128, KC * TB], BF16))
            u_fm = es5.enter_context(nc.sbuf_tensor("u_fm", [128, MC * TB], BF16))
            wu_sb = [es5.enter_context(nc.sbuf_tensor(f"wu_sb{i}", [128, MC * 128], BF16))
                     for i in range(2)]
            wg_sb = [es5.enter_context(nc.sbuf_tensor(f"wg_sb{i}", [128, MC * 128], BF16))
                     for i in range(2)]
            sgb = [es5.enter_context(nc.sbuf_tensor(f"sgb{i}", [128, TB], F32))
                   for i in range(2)]
            wd_sb = [es5.enter_context(nc.sbuf_tensor(f"wd_sb{i}", [128, 8 * 512], BF16))
                     for i in range(2)]
            u_ng = [es5.enter_context(nc.sbuf_tensor(f"u_ng{i}", [128, TSUB * 512], BF16))
                    for i in range(2)]
            res = es5.enter_context(nc.sbuf_tensor("res", [128, TSUB * DM], BF16))
            sums = es5.enter_context(nc.sbuf_tensor("sums", [128, TSUB * NGO], F32))
            ssq = es5.enter_context(nc.sbuf_tensor("ssq", [128, TSUB * NGO], F32))
            st2 = es5.enter_context(nc.sbuf_tensor("st2", [128, 4 * TSUB], F32))
            sq2 = es5.enter_context(nc.sbuf_tensor("sq2", [128, 512], F32))
            outt = [es5.enter_context(nc.sbuf_tensor(f"outt{i}", [128, DM], F32))
                    for i in range(2)]
            p_up = [es5.enter_context(nc.psum_tensor(f"p_up{i}", [128, TB], F32))
                    for i in range(2)]
            p_gt = [es5.enter_context(nc.psum_tensor(f"p_gt{i}", [128, TB], F32))
                    for i in range(2)]
            p_dn = [es5.enter_context(nc.psum_tensor(f"p_dn{i}", [128, 512], F32))
                    for i in range(TSUB)]

            UG_SLOT = 2 * JL * MC * 128
            ug_last_pe = {0: 0, 1: 0}
            wd_last_pe = {0: 0, 1: 0}
            ug_end_pe = {}
            for nb in range(NB):
                t0 = nb * TB
                @block.sync
                def _(sy, nb=nb, t0=t0,
                      pen=(ug_end_pe.get(nb - 1, 0) if nb > 0 else s_pe.n),
                      du0=d_u[0].n, du1=d_u[1].n):
                    sy.wait_ge(s_pe.h, pen)
                    sy.wait_ge(d_u[0].h, du0)
                    sy.wait_ge(d_u[1].h, du1)
                    for mc in range(MC):
                        sy.dma_start_transpose(
                            out=u_fm[:, mc * TB:(mc + 1) * TB],
                            in_=uscr[t0:t0 + TB, mc * 128:(mc + 1) * 128]
                        ).then_inc(d_uf.h, 16)
                d_uf.n += 16 * MC
                UFM_LD = d_uf.n

                hdve = {}
                ugact = {}
                wld = {}
                ACT0B = s_act.n
                DVE0B = s_dve.n

                def load_ug(j):
                    sl = j % 2

                    @block.sync
                    def _(sy, j=j, sl=sl, ccv=CC1, pefree=ug_last_pe[j % 2]):
                        sy.wait_ge(s_cc.h, ccv)
                        sy.wait_ge(s_pe.h, pefree)
                        slot = j // JL
                        jl = j % JL
                        for w_i, wbuf in ((0, wu_sb[sl]), (1, wg_sb[sl])):
                            rbase = slot * UG_SLOT + (w_i * JL + jl) * MC * 128
                            src = ag1u_o[rbase:rbase + MC * 128, :]
                            sy.dma_start(out=wbuf[:].rearrange("p (m c) -> p m c", m=MC),
                                         in_=src.rearrange("(m p) c -> p m c", p=128)
                                         ).then_inc(d_f[sl].h, 16)
                    d_f[j % 2].n += 32
                    wld[("ld", j)] = d_f[j % 2].n

                load_ug(0)
                if KC > 1:
                    load_ug(1)
                for j in range(KC):
                    sl = j % 2

                    @block.tensor
                    def _(t, j=j, sl=sl, ldn=wld[("ld", j)], un=UFM_LD,
                          pfree=ugact.get(j - 2, ACT0B), pfree2=hdve.get(j - 2, DVE0B)):
                        t.wait_ge(d_f[sl].h, ldn)
                        t.wait_ge(d_uf.h, un)
                        t.wait_ge(s_act.h, pfree)
                        t.wait_ge(s_dve.h, pfree2)
                        for mc in range(MC):
                            t.matmul(p_up[sl][:], wu_sb[sl][:, mc * 128:(mc + 1) * 128],
                                     u_fm[:, mc * TB:(mc + 1) * TB],
                                     start=(mc == 0), stop=(mc == MC - 1))
                        for mc in range(MC):
                            last = t.matmul(p_gt[sl][:], wg_sb[sl][:, mc * 128:(mc + 1) * 128],
                                            u_fm[:, mc * TB:(mc + 1) * TB],
                                            start=(mc == 0), stop=(mc == MC - 1))
                        last.then_inc(s_pe.h, 1)
                    s_pe.n += 1
                    wld[("pe", j)] = s_pe.n
                    ug_last_pe[j % 2] = s_pe.n

                    @block.scalar
                    def _(s, j=j, sl=sl, pen=s_pe.n, hfree=hdve.get(j - 2, DVE0B)):
                        s.wait_ge(s_pe.h, pen)
                        s.wait_ge(s_dve.h, hfree)
                        s.activation(sgb[sl][:], p_gt[sl][:],
                                     AF.Silu).then_inc(s_act.h, 1)
                    s_act.n += 1
                    ugact[j] = s_act.n

                    @block.vector
                    def _(v, j=j, sl=sl, acn=s_act.n):
                        v.wait_ge(s_act.h, acn)
                        v.scalar_tensor_tensor(
                            out=hbuf[:, j * TB:(j + 1) * TB], in0=p_up[sl][:],
                            scalar=1.0, in1=sgb[sl][:], op0=ALU.mult,
                            op1=ALU.mult).then_inc(s_dve.h, 1)
                    s_dve.n += 1
                    hdve[j] = s_dve.n
                    if j + 2 < KC:
                        load_ug(j + 2)
                H_DONE = s_dve.n
                ug_end_pe[nb] = s_pe.n

                KG = KC // 8
                dn_evac = {}
                dwld = {}
                ACT0D = s_act.n
                DVE0D = s_dve.n

                def load_wd(ng, kg):
                    sl = kg % 2

                    @block.sync
                    def _(sy, ng=ng, kg=kg, sl=sl, ccv=CC1, free=wd_last_pe[kg % 2]):
                        sy.wait_ge(s_cc.h, ccv)
                        sy.wait_ge(s_pe.h, free)
                        nsub = 512 // NSH if NSH < 512 else 1
                        csub = min(NSH, 512)
                        for si in range(nsub):
                            shard = (ng * 512) // NSH + si
                            rbase = shard * KC * 128 + kg * 8 * 128
                            src = ag1d_o[rbase:rbase + 8 * 128, :]
                            dstv = wd_sb[sl][:].rearrange("p (k n) -> p k n", k=8)
                            sy.dma_start(out=dstv[:, :, si * csub:(si + 1) * csub],
                                         in_=src.rearrange("(k p) n -> p k n", p=128)
                                         ).then_inc(d_d[sl].h, 16)
                    d_d[kg % 2].n += 16 * (512 // NSH if NSH < 512 else 1)
                    dwld[("ld", ng, kg)] = d_d[kg % 2].n

                for ng in range(NGO):
                    load_wd(ng, 0)
                    if KG > 1:
                        load_wd(ng, 1)
                    usl = ng % 2

                    @block.sync
                    def _(sy, ng=ng, t0=t0, usl=usl, free=dn_evac.get(ng - 2, DVE0D),
                          du0=d_u[0].n, du1=d_u[1].n):
                        sy.wait_ge(s_dve.h, free)
                        sy.wait_ge(d_u[0].h, du0)
                        sy.wait_ge(d_u[1].h, du1)
                        sy.dma_start(out=u_ng[usl][:].rearrange("p (t n) -> p t n", t=TSUB),
                                     in_=uscr[t0:t0 + TB, ng * 512:(ng + 1) * 512]
                                     .rearrange("(t p) n -> p t n", p=128)
                                     ).then_inc(d_un[usl].h, 16)
                    d_un[usl].n += 16

                    for kg in range(KG):
                        sl = kg % 2

                        @block.tensor
                        def _(t, ng=ng, kg=kg, sl=sl, ldn=dwld[("ld", ng, kg)],
                              hn=H_DONE, efree=dn_evac.get(ng - 1, DVE0D),
                              afree=dn_evac.get(("a", ng - 1), ACT0D)):
                            t.wait_ge(d_d[sl].h, ldn)
                            t.wait_ge(s_dve.h, hn)
                            if kg == 0:
                                t.wait_ge(s_dve.h, efree)
                                t.wait_ge(s_act.h, afree)
                            for k8 in range(8):
                                k = kg * 8 + k8
                                for ts in range(TSUB):
                                    last = t.matmul(
                                        p_dn[ts][:],
                                        hbuf[:, k * TB + ts * 128: k * TB + (ts + 1) * 128],
                                        wd_sb[sl][:, k8 * 512:(k8 + 1) * 512],
                                        start=(k == 0), stop=(k == KC - 1))
                            last.then_inc(s_pe.h, 1)
                        s_pe.n += 1
                        dwld[("pe", ng, kg)] = s_pe.n
                        wd_last_pe[kg % 2] = s_pe.n
                        if kg + 2 < KG:
                            load_wd(ng, kg + 2)
                    for ts in range(TSUB):
                        @block.vector
                        def _(v, ng=ng, ts=ts, usl=usl, pen=s_pe.n, dn=d_un[usl].n):
                            v.wait_ge(s_pe.h, pen)
                            v.wait_ge(d_un[usl].h, dn)
                            v.drain()
                            v.tensor_tensor(
                                out=res[:, ts * DM + ng * 512: ts * DM + (ng + 1) * 512],
                                in0=p_dn[ts][:], in1=u_ng[usl][:, ts * 512:(ts + 1) * 512],
                                op=ALU.add)
                            v.drain()
                            v.tensor_reduce(
                                out=sums[:, ts * NGO + ng: ts * NGO + ng + 1],
                                in_=res[:, ts * DM + ng * 512: ts * DM + (ng + 1) * 512],
                                axis=AXL.X, op=ALU.add).then_inc(s_dve.h, 1)
                        s_dve.n += 1

                        @block.vector
                        def _(v, ng=ng, ts=ts, dvn=s_dve.n):
                            v.drain()
                            rslice = res[:, ts * DM + ng * 512: ts * DM + (ng + 1) * 512]
                            v.tensor_tensor(out=sq2[:], in0=rslice, in1=rslice,
                                            op=ALU.mult)
                            v.drain()
                            v.tensor_reduce(
                                out=ssq[:, ts * NGO + ng: ts * NGO + ng + 1],
                                in_=sq2[:], axis=AXL.X,
                                op=ALU.add).then_inc(s_dve.h, 1)
                        s_dve.n += 1
                    dn_evac[ng] = s_dve.n
                    dn_evac[("a", ng)] = s_act.n

                @block.vector
                def _(v, dvn=s_dve.n):
                    v.wait_ge(s_dve.h, dvn)
                    v.drain()
                    for ts in range(TSUB):
                        v.tensor_reduce(out=st2[:, ts:ts + 1],
                                        in_=sums[:, ts * NGO:(ts + 1) * NGO],
                                        axis=AXL.X, op=ALU.add, negate=True)
                    for ts in range(TSUB):
                        v.tensor_reduce(out=st2[:, TSUB + ts:TSUB + ts + 1],
                                        in_=ssq[:, ts * NGO:(ts + 1) * NGO],
                                        axis=AXL.X, op=ALU.add)
                    v.drain()
                    v.tensor_scalar(out=st2[:, 0:TSUB], in0=st2[:, 0:TSUB],
                                    scalar1=1.0 / DM, scalar2=None, op0=ALU.mult)
                    v.tensor_scalar(out=st2[:, TSUB:2 * TSUB], in0=st2[:, TSUB:2 * TSUB],
                                    scalar1=1.0 / DM, scalar2=None, op0=ALU.mult)
                    v.drain()
                    v.tensor_tensor(out=st2[:, 2 * TSUB:3 * TSUB], in0=st2[:, 0:TSUB],
                                    in1=st2[:, 0:TSUB], op=ALU.mult)
                    v.drain()
                    v.tensor_tensor(out=st2[:, 2 * TSUB:3 * TSUB],
                                    in0=st2[:, TSUB:2 * TSUB],
                                    in1=st2[:, 2 * TSUB:3 * TSUB], op=ALU.subtract)
                    v.drain()
                    v.tensor_scalar(out=st2[:, 2 * TSUB:3 * TSUB],
                                    in0=st2[:, 2 * TSUB:3 * TSUB],
                                    scalar1=EPS, scalar2=None,
                                    op0=ALU.add).then_inc(s_dve.h, 1)
                s_dve.n += 1

                @block.scalar
                def _(s, dvn=s_dve.n):
                    s.wait_ge(s_dve.h, dvn)
                    s.drain()
                    s.activation(st2[:, 3 * TSUB:4 * TSUB], st2[:, 2 * TSUB:3 * TSUB],
                                 AF.Ln)
                    s.drain()
                    s.activation(st2[:, 2 * TSUB:3 * TSUB], st2[:, 3 * TSUB:4 * TSUB],
                                 AF.Exp, scale=-0.5).then_inc(s_act.h, 1)
                s_act.n += 1

                @block.vector
                def _(v, acn=s_act.n):
                    v.wait_ge(s_act.h, acn)
                    v.drain()
                    v.tensor_tensor(out=st2[:, 0:TSUB], in0=st2[:, 0:TSUB],
                                    in1=st2[:, 2 * TSUB:3 * TSUB],
                                    op=ALU.mult).then_inc(s_dve.h, 1)
                s_dve.n += 1

                for ts in range(TSUB):
                    @block.vector
                    def _(v, ts=ts, dvn=s_dve.n, don=d_out[ts % 2].n):
                        v.wait_ge(s_dve.h, dvn)
                        v.wait_ge(d_out[ts % 2].h, don)
                        v.drain()
                        v.tensor_scalar(out=outt[ts % 2][:],
                                        in0=res[:, ts * DM:(ts + 1) * DM],
                                        scalar1=st2[:, 2 * TSUB + ts:2 * TSUB + ts + 1],
                                        scalar2=st2[:, ts:ts + 1],
                                        op0=ALU.mult, op1=ALU.add).then_inc(s_dve.h, 1)
                    s_dve.n += 1

                    @block.gpsimd
                    def _(g, nb=nb, ts=ts, t0=t0, acn=s_dve.n):
                        g.wait_ge(s_dve.h, acn)
                        g.dma_start(out=out_ext[t0 + ts * 128: t0 + (ts + 1) * 128, :],
                                    in_=outt[ts % 2][:]).then_inc(d_out[ts % 2].h, 16)
                    d_out[ts % 2].n += 16
            block.flush()

        @block.gpsimd
        def _(g, don0=d_out[0].n, don1=d_out[1].n):
            g.wait_ge(d_out[0].h, don0)
            g.wait_ge(d_out[1].h, don1)
        block.flush()

    return nc


_NC_CACHE = {}


def shard_inputs(x, w_qkv, w_out, su_ug, su_dp, ws_ug, ws_dp, w_up, w_gate, w_down,
                 n_cores=8):
    B, S, DM = x.shape
    HEADS = DM // 128
    INTER = w_up.shape[0]
    NH = HEADS // 2
    VC = NH * 128
    SH = S // 2
    FSH = INTER // n_cores
    NSH = DM // n_cores
    MC, KC, NST = DM // 128, INTER // 128, NSH // 128
    wq = w_qkv[:, 0:DM]
    wk = w_qkv[:, DM:2 * DM]
    wv = w_qkv[:, 2 * DM:3 * DM]
    c = np.ascontiguousarray
    in_maps = []
    for core in range(n_cores):
        b, hs = core // 2, core % 2
        hsl = slice(hs * VC, (hs + 1) * VC)
        in_maps.append({
            "x_in": c(x[b]),
            "xh_in": c(x[b, hs * SH:(hs + 1) * SH]),
            "wqkv_in": c(np.concatenate([wq[:, hsl], wk[:, hsl], wv[:, hsl]], axis=1)),
            "wout_in": c(np.concatenate([w_out[hs * VC:(hs + 1) * VC],
                                         w_out[(1 - hs) * VC:(2 - hs) * VC]], axis=0)),
            "wup_in": c(w_up[core * FSH:(core + 1) * FSH]),
            "wgate_in": c(w_gate[core * FSH:(core + 1) * FSH]),
            "wdown_in": c(w_down[core * NSH:(core + 1) * NSH]),
            "suug_in": c(su_ug.reshape(MC, 128)),
            "sudp_in": c(su_dp.reshape(KC, 128)),
            "wsu_in": c(ws_ug[core * FSH:(core + 1) * FSH].reshape(FSH // 128, 128)),
            "wsg_in": c(ws_ug[INTER + core * FSH:INTER + (core + 1) * FSH]
                        .reshape(FSH // 128, 128)),
            "wsdp_in": c(ws_dp[core * NSH:(core + 1) * NSH].reshape(NST, 128)),
        })
    return in_maps


def kernel(L0_attn_x, L0_attn_w_qkv, L0_attn_w_out, L0_ffn_SU_ug, L0_ffn_SU_dp,
           L0_ffn_Wscale_ug, L0_ffn_Wscale_dp, L0_ffn_W_up, L0_ffn_W_gate, L0_ffn_W_down):
    x = np.asarray(L0_attn_x)
    B, S, DM = x.shape
    HEADS = DM // 128
    INTER = np.asarray(L0_ffn_W_up).shape[0]
    n_cores = 8
    NH = HEADS // 2
    VC = NH * 128
    SH = S // 2
    FSH = INTER // n_cores
    NSH = DM // n_cores
    MC, KC, NST = DM // 128, INTER // 128, NSH // 128

    key = (S, DM, HEADS, INTER)
    if key not in _NC_CACHE:
        _NC_CACHE[key] = build_nc(S, DM, HEADS, INTER, n_cores)
    nc = _NC_CACHE[key]

    in_maps = shard_inputs(
        x, np.asarray(L0_attn_w_qkv), np.asarray(L0_attn_w_out),
        np.asarray(L0_ffn_SU_ug), np.asarray(L0_ffn_SU_dp),
        np.asarray(L0_ffn_Wscale_ug), np.asarray(L0_ffn_Wscale_dp),
        np.asarray(L0_ffn_W_up), np.asarray(L0_ffn_W_gate), np.asarray(L0_ffn_W_down),
        n_cores)

    res = run_bass_kernel_spmd(nc, in_maps, core_ids=list(range(n_cores)))
    out = np.empty((B, S, DM), np.float32)
    for core in range(n_cores):
        b, hs = core // 2, core % 2
        out[b, hs * SH:(hs + 1) * SH] = res.results[core]["out_ext"]
    return out
